# version 03
# baseline (speedup 1.0000x reference)
import numpy as np
import ml_dtypes
from contextlib import ExitStack

import concourse.bass as bass
import concourse.mybir as mybir
from concourse.bass_utils import run_bass_kernel_spmd
from concourse.dve_op import DveOp
from concourse.dve_spec import Spec, Src0, Src1, C0, C1, C2, C3, lower as dve_lower
from concourse.dve_table_gen import dve_ver_for
import concourse.dve_ops as dve_ops_mod
from concourse.dve_ops import AFFINE_THEN_ADD

F32 = mybir.dt.float32
BF16 = mybir.dt.bfloat16
I32 = mybir.dt.int32
AF = mybir.ActivationFunctionType
ALU = mybir.AluOpType
AX = mybir.AxisListType

_VER = dve_ver_for("TRN2")
DEBUG_ATTN = False


def _reg_op(name, spec, reference):
    for op in dve_ops_mod.OPS:
        if op.name == name:
            return op
    sha = dve_lower(spec, ver=_VER, perf_en=False, name=name).sha(_VER)
    op = DveOp(name, spec, False, {_VER: sha}, reference=reference)
    dve_ops_mod.OPS.append(op)
    dve_ops_mod._SUB_OPCODE_FOR_NAME[name] = (
        dve_ops_mod._CUSTOM_DVE_ROW_BASE + len(dve_ops_mod.OPS) - 1)
    dve_ops_mod.CUSTOM_DVE_REFS[name] = op.ref
    return op


SNAP1 = _reg_op(
    "ANT_SNAP1",
    Spec(body=((Src0 > C0) + (Src0 > C1)) + ((Src0 > C2) + (Src0 > C3))),
    lambda in0, in1, s0, s1, imm2: (
        (in0 > s0).astype(np.float64) + (in0 > s1) + (in0 > imm2) + (in0 > in1)),
)


def _snap2_spec():
    c5 = Src0 > C0
    c6 = Src0 > C1
    n7 = Src0 > C2
    t3 = (c5 + c6) + (n7 + n7)
    return Spec(body=Src1 + (t3 + t3))


SNAP2 = _reg_op(
    "ANT_SNAP2",
    _snap2_spec(),
    lambda in0, in1, s0, s1, imm2: in1 + 2.0 * (
        (in0 > s0).astype(np.float64) + (in0 > s1) + 2.0 * (in0 > imm2)),
)

S = 4096
DM = 1024
NT = S // 128
DFF = 2816
EPS = 1e-5
OCS = [(0, 512), (512, 512), (1024, 512), (1536, 512), (2048, 512), (2560, 256)]
NOC = len(OCS)


def bc16(small_ap, nb, k=16):
    return small_ap.unsqueeze(2).broadcast_to([128, nb, k])


def r3(ap, k=16):
    return ap.rearrange("p (nb k) -> p nb k", k=k)


_DMA_METHODS = frozenset({"dma_start", "dma_start_transpose", "indirect_dma_start"})


NSLOT = 6


class W:

    def __init__(self, name, sem, dsems, serial):
        self.name = name
        self.sem = sem
        self.dsems = dsems
        self.serial = serial
        self.eng = None
        self.n = 0
        self.dn = 0

    def bind(self, eng):
        self.eng = eng
        return self

    def reset(self):
        self.eng = None
        self.n = 0
        self.dn = 0

    def unbind(self):
        self.eng = None

    def op(self, meth, *a, **k):
        if meth in _DMA_METHODS:
            slot = self.dn % NSLOT
            if self.eng is not None:
                prev = self.dn // NSLOT
                if prev > 0:
                    self.eng.wait_ge(self.dsems[slot], 16 * prev)
                inst = getattr(self.eng, meth)(*a, **k)
                inst.then_inc(self.dsems[slot], 16)
            self.dn += 1
            return ("D", self.dn)
        if self.eng is not None:
            if self.serial and self.n > 0:
                self.eng.wait_ge(self.sem, self.n)
            inst = getattr(self.eng, meth)(*a, **k)
            inst.then_inc(self.sem, 1)
        self.n += 1
        return self.n

    def __getattr__(self, meth):
        if meth.startswith("_ant"):
            raise AttributeError(meth)
        return lambda *a, **k: self.op(meth, *a, **k)

    def wait(self, other, val):
        if self.eng is None:
            return
        if isinstance(val, tuple):
            m = val[1]
            if m <= 0:
                return
            self.eng.wait_ge(other.dsems[(m - 1) % NSLOT],
                             16 * ((m - 1) // NSLOT + 1))
            return
        if val > 0:
            if other is self and self.serial:
                return
            self.eng.wait_ge(other.sem, val)

    def wait_dma_upto(self, other, dn):
        if self.eng is None or not dn:
            return
        for s in range(NSLOT):
            cnt = (dn - s + NSLOT - 1) // NSLOT
            if cnt > 0:
                self.eng.wait_ge(other.dsems[s], 16 * cnt)

    def wait_self_all(self):
        if self.eng is None:
            return
        if self.n > 0:
            self.eng.wait_ge(self.sem, self.n)
        for s in range(NSLOT):
            cnt = (self.dn - s + NSLOT - 1) // NSLOT
            if cnt > 0:
                self.eng.wait_ge(self.dsems[s], 16 * cnt)

    def mark(self):
        return self.n


class Ctx:
    def __init__(self, nc, es):
        self.nc = nc
        self.es = es
        self.sb = {}
        self.dram = {}

    def sbuf(self, name, shape, dtype):
        t = self.es.enter_context(self.nc.sbuf_tensor(name, shape, dtype))
        self.sb[name] = t
        return t

    def psum(self, name, shape, dtype=F32):
        return self.es.enter_context(self.nc.psum_tensor(name, shape, dtype))

    def sem(self, name):
        return self.es.enter_context(self.nc.semaphore(name))

    def dram_t(self, name, shape, dtype, kind):
        ap = self.nc.dram_tensor(name, shape, dtype, kind=kind).ap()
        self.dram[name] = ap
        return ap


class QuantHelper:

    def __init__(self, ctx):
        c = ctx
        self.a = c.sb["q_a"]
        self.xn = c.sb["q_xn"]
        self.g2 = c.sb["q_g2"]
        self.sg = c.sb["q_sg"]
        self.bm = c.sb["q_bm"]
        self.scl = c.sb["q_scl"]
        self.rec = c.sb["q_rec"]
        self.sclF = c.sb["q_sclF"]
        self.c175 = c.sb["cst"][:, 7:8]

    def v_part1(self, v, src, N, rstd_ap=None):
        nb = N // 16
        v.tensor_reduce(self.bm[:, :nb], r3(src), AX.X, ALU.max,
                        apply_absolute_value=True)
        v.tensor_scalar(self.scl[:, :nb], self.bm[:, :nb], 1.0 / 6.0, 1e-12,
                        ALU.mult, ALU.max)
        v.reciprocal(self.rec[:, :nb], self.scl[:, :nb])
        if rstd_ap is not None:
            v.tensor_scalar(self.sclF[:, :nb], self.scl[:, :nb], rstd_ap, 0.5,
                            ALU.mult, ALU.mult)
        else:
            v.tensor_scalar(self.sclF[:, :nb], self.scl[:, :nb], 0.5, None,
                            ALU.mult)

    def s_part(self, sc, src, N):
        sc.activation(self.a[:, :N], src, AF.Abs)
        sc.activation(self.sg[:, :N], src, AF.Sign)

    def v_part2(self, v, dst, N):
        nb = N // 16
        v.tensor_tensor(r3(self.xn[:, :N]), r3(self.a[:, :N]),
                        bc16(self.rec[:, :nb], nb), ALU.mult)
        v.op("_custom_dve", SNAP1, out=self.a[:, :N], in0=self.xn[:, :N],
             in1=self.c175, s0=0.25, s1=0.75, imm2=1.25)
        v.op("_custom_dve", SNAP2, out=self.g2[:, :N], in0=self.xn[:, :N],
             in1=self.a[:, :N], s0=2.5, s1=3.5, imm2=5.0)
        v.tensor_tensor(r3(self.a[:, :N]), r3(self.g2[:, :N]),
                        bc16(self.sclF[:, :nb], nb), ALU.mult)
        v.tensor_tensor(dst, self.a[:, :N], self.sg[:, :N], ALU.mult)


class Phase:
    def __init__(self, ctx, tag):
        self.ctx = ctx
        self.tag = tag
        self.mk = {}
        self.W = ctx.EW

    def emit_sync(self): pass
    def emit_scalar(self): pass
    def emit_vector(self): pass
    def emit_tensor(self): pass
    def emit_gpsimd(self): pass


class StatsPrepass(Phase):
    def __init__(self, ctx):
        super().__init__(ctx, "s0")

    def emit_sync(self):
        c, me = self.ctx, self.W["sync"]
        me.dma_start(out=c.dram["x_state"], in_=c.dram["x_in"])
        xi = c.dram["x_in"].rearrange("(t p) d -> t p d", p=128)
        for i in range(NT):
            if i >= 2:
                me.wait(self.W["vector"], self.mk.get(f"v{i-2}", 0))
                me.wait(self.W["scalar"], self.mk.get(f"s{i-2}", 0))
            self.mk[f"ld{i}"] = me.dma_start(
                out=c.sb["xt"][:, (i % 2) * DM:(i % 2) * DM + DM], in_=xi[i])
        me.wait_self_all()

    def emit_vector(self):
        c, me = self.ctx, self.W["vector"]
        for i in range(NT):
            me.wait(self.W["sync"], self.mk.get(f"ld{i}", 0))
            xt = c.sb["xt"][:, (i % 2) * DM:(i % 2) * DM + DM]
            me.reduce_sum(c.sb["msc"][:, 0:1], xt, AX.X)
            me.tensor_scalar(c.sb["stats_m0"][:, i:i + 1], c.sb["msc"][:, 0:1],
                             1.0 / DM, None, ALU.mult)
            self.mk[f"v{i}"] = me.mark()
        me.wait(self.W["scalar"], self.mk.get(f"s{NT-1}", 0))
        me.tensor_scalar(c.sb["tmpA"][:, 0:NT], c.sb["stats_sq0"][:, 0:NT],
                         1.0 / DM, None, ALU.mult)
        me.tensor_tensor(c.sb["tmpB"][:, 0:NT], c.sb["stats_m0"][:, 0:NT],
                         c.sb["stats_m0"][:, 0:NT], ALU.mult)
        me.tensor_tensor(c.sb["tmpA"][:, 0:NT], c.sb["tmpA"][:, 0:NT],
                         c.sb["tmpB"][:, 0:NT], ALU.subtract)
        self.mk["var"] = me.mark()

    def emit_scalar(self):
        c, me = self.ctx, self.W["scalar"]
        for i in range(NT):
            me.wait(self.W["sync"], self.mk.get(f"ld{i}", 0))
            xt = c.sb["xt"][:, (i % 2) * DM:(i % 2) * DM + DM]
            me.activation(c.sb["sqscr"][:, 0:DM], xt, AF.Square,
                          accum_out=c.sb["stats_sq0"][:, i:i + 1])
            self.mk[f"s{i}"] = me.mark()
        me.wait(self.W["vector"], self.mk.get("var", 0))
        me.activation(c.sb["tmpB"][:, 0:NT], c.sb["tmpA"][:, 0:NT], AF.Ln,
                      bias=EPS)
        me.activation(c.sb["stats_r0"][:, 0:NT], c.sb["tmpB"][:, 0:NT],
                      AF.Exp, scale=-0.5)


class FFNWeightPrep(Phase):
    def __init__(self, ctx):
        super().__init__(ctx, "w0")
        self.specs = [
            ("w_gate", DM, "Wg"),
            ("w_up", DM, "Wu"),
            ("w_down", DFF, "Wd"),
        ]
        self.tiles = []
        for k, (nm, cols, dst) in enumerate(self.specs):
            rows = DFF if cols == DM else DM
            for i in range(rows // 128):
                self.tiles.append((k, i))
        self.q = QuantHelper(ctx)

    @staticmethod
    def chunks(cols):
        out, o = [], 0
        while o < cols:
            wd = min(1024, cols - o)
            out.append((o, wd))
            o += wd
        return out

    def emit_sync(self):
        c, me = self.ctx, self.W["sync"]
        for n, (k, i) in enumerate(self.tiles):
            nm, cols, dst = self.specs[k]
            src = c.dram[nm].rearrange("(t p) d -> t p d", p=128)
            if n >= 1:
                me.wait(self.W["vector"], self.mk.get(f"vq{n-1}", 0))
                me.wait(self.W["scalar"], self.mk.get(f"sa{n-1}", 0))
            self.mk[f"ld{n}"] = me.dma_start(out=c.sb["wtile"][:, 0:cols],
                                             in_=src[i])
        me.wait(self.W["vector"], self.mk.get("copydone", 0))
        for dst, hbm in (("Wg", "WgqT_st"), ("Wu", "WuqT_st"), ("Wd", "WdqT_st")):
            me.dma_start(out=c.dram[hbm], in_=c.sb[dst][:])
        me.wait_self_all()

    def emit_scalar(self):
        c, me = self.ctx, self.W["scalar"]
        for n, (k, i) in enumerate(self.tiles):
            nm, cols, dst = self.specs[k]
            me.wait(self.W["sync"], self.mk.get(f"ld{n}", 0))
            for ci, (o0, wd) in enumerate(self.chunks(cols)):
                src = c.sb["wtile"][:, o0:o0 + wd]
                me.wait(self.W["vector"], self.mk.get(f"vq{n}_{ci-1}", 0))
                self.q.s_part(me, src, wd)
                self.mk[f"sa{n}_{ci}"] = me.mark()
            self.mk[f"sa{n}"] = me.mark()

    def emit_vector(self):
        c, me = self.ctx, self.W["vector"]
        for n, (k, i) in enumerate(self.tiles):
            nm, cols, dst = self.specs[k]
            me.wait(self.W["sync"], self.mk.get(f"ld{n}", 0))
            for ci, (o0, wd) in enumerate(self.chunks(cols)):
                src = c.sb["wtile"][:, o0:o0 + wd]
                self.q.v_part1(me, src, wd)
                me.wait(self.W["scalar"], self.mk.get(f"sa{n}_{ci}", 0))
                self.q.v_part2(me, c.sb["hqbf"][:, o0:o0 + wd], wd)
                self.mk[f"vq{n}_{ci}"] = me.mark()
            self.mk[f"vq{n}"] = me.mark()
            me.wait(self.W["tensor"], self.mk.get(f"tr{n}", 0))
            ncc = cols // 128
            dstv = c.sb[dst][:].rearrange("p (cchunk m) -> p cchunk m", cchunk=ncc)
            me.tensor_copy(dstv[:, :, i * 128:(i + 1) * 128],
                           c.pt_bf[:, 0:ncc * 128].rearrange(
                               "p (cc m) -> p cc m", cc=ncc))
            self.mk[f"cp{n}"] = me.mark()
        self.mk["copydone"] = me.mark()

    def emit_tensor(self):
        c, me = self.ctx, self.W["tensor"]
        for n, (k, i) in enumerate(self.tiles):
            nm, cols, dst = self.specs[k]
            me.wait(self.W["vector"], self.mk.get(f"vq{n}", 0))
            if n > 0:
                me.wait(self.W["vector"], self.mk.get(f"cp{n-1}", 0))
            for cc in range(cols // 128):
                me.transpose(c.pt_bf[:, cc * 128:(cc + 1) * 128],
                             c.sb["hqbf"][:, cc * 128:(cc + 1) * 128],
                             c.sb["identbf"][:])
            self.mk[f"tr{n}"] = me.mark()


class FFNPhase(Phase):
    def __init__(self, ctx, si, so, tag, load_stash=False):
        super().__init__(ctx, tag)
        self.si = si
        self.so = so
        self.load_stash = load_stash
        self.q = QuantHelper(ctx)

    def xt(self, i):
        return self.ctx.sb["xt"][:, (i % 2) * DM:(i % 2) * DM + DM]

    def xnew(self, i):
        return self.ctx.sb["xnew"][:, (i % 2) * DM:(i % 2) * DM + DM]

    def emit_sync(self):
        c, me = self.ctx, self.W["sync"]
        if self.load_stash:
            for dst, hbm in (("Wg", "WgqT_st"), ("Wu", "WuqT_st"), ("Wd", "WdqT_st")):
                self.mk["wl_" + dst] = me.dma_start(out=c.sb[dst][:],
                                                    in_=c.dram[hbm])
        xs = c.dram["x_state"].rearrange("(t p) d -> t p d", p=128)
        for i in range(2):
            self.mk[f"ld{i}"] = me.dma_start(out=self.xt(i), in_=xs[i])
        for i in range(NT):
            me.wait(self.W["vector"], self.mk.get(f"xn{i}", 0))
            self.mk[f"st{i}"] = me.dma_start(out=xs[i], in_=self.xnew(i))
            if i + 2 < NT:
                self.mk[f"ld{i+2}"] = me.dma_start(out=self.xt(i + 2),
                                                   in_=xs[i + 2])
        me.wait_self_all()

    def emit_scalar(self):
        c, me = self.ctx, self.W["scalar"]
        if self.load_stash:
            for dst in ("Wg", "Wu", "Wd"):
                me.wait(self.W["sync"], self.mk.get("wl_" + dst, 0))
        for i in range(NT):
            me.wait(self.W["vector"], self.mk.get(f"xc{i}", 0))
            self.q.s_part(me, c.sb["xc"][:], DM)
            self.mk[f"abs{i}"] = me.mark()
            for oc, (o0, ow) in enumerate(OCS):
                me.wait(self.W["tensor"], self.mk.get(f"mm{i}_{oc}", 0))
                hg = c.pgu[:, (oc % 2) * 1024:(oc % 2) * 1024 + ow]
                me.activation(c.sb["sgate"][:, 0:ow], hg, AF.Silu)
                self.mk[f"sil{i}_{oc}"] = me.mark()
                me.wait(self.W["vector"], self.mk.get(f"hp{i}_{oc}", 0))
                self.q.s_part(me, c.sb["hpf"][:, 0:ow], ow)
                self.mk[f"habs{i}_{oc}"] = me.mark()
            me.wait(self.W["vector"], self.mk.get(f"xn{i}", 0))
            me.activation(c.sb["sqscr"][:, 0:DM], self.xnew(i), AF.Square,
                          accum_out=c.sb["stats_sq" + self.so][:, i:i + 1])
            self.mk[f"sq{i}"] = me.mark()
        me.wait(self.W["vector"], self.mk.get("var", 0))
        me.activation(c.sb["tmpB"][:, 0:NT], c.sb["tmpA"][:, 0:NT], AF.Ln,
                      bias=EPS)
        me.activation(c.sb["stats_r" + self.so][:, 0:NT], c.sb["tmpB"][:, 0:NT],
                      AF.Exp, scale=-0.5)

    def emit_vector(self):
        c, me = self.ctx, self.W["vector"]
        if self.load_stash:
            for dst in ("Wg", "Wu", "Wd"):
                me.wait(self.W["sync"], self.mk.get("wl_" + dst, 0))
        for i in range(NT):
            me.wait(self.W["sync"], self.mk.get(f"ld{i}", 0))
            me.tensor_scalar(c.sb["xc"][:], self.xt(i),
                             c.sb["stats_m" + self.si][:, i:i + 1], None,
                             ALU.subtract)
            self.mk[f"xc{i}"] = me.mark()
            self.q.v_part1(me, c.sb["xc"][:], DM,
                           rstd_ap=c.sb["stats_r" + self.si][:, i:i + 1])
            me.wait(self.W["scalar"], self.mk.get(f"abs{i}", 0))
            self.q.v_part2(me, c.sb["xqbf"][:], DM)
            self.mk[f"xq{i}"] = me.mark()
            me.wait(self.W["tensor"], self.mk.get(f"tr{i}", 0))
            me.tensor_copy(c.sb["xqT"][:], c.pt_bf[:, 0:1024])
            self.mk[f"xqt{i}"] = me.mark()
            for oc, (o0, ow) in enumerate(OCS):
                me.wait(self.W["scalar"], self.mk.get(f"sil{i}_{oc}", 0))
                hu = c.pgu[:, (oc % 2) * 1024 + 512:(oc % 2) * 1024 + 512 + ow]
                me.tensor_tensor(c.sb["hpf"][:, 0:ow], c.sb["sgate"][:, 0:ow],
                                 hu, ALU.mult)
                self.mk[f"hp{i}_{oc}"] = me.mark()
                self.q.v_part1(me, c.sb["hpf"][:, 0:ow], ow)
                me.wait(self.W["scalar"], self.mk.get(f"habs{i}_{oc}", 0))
                self.q.v_part2(me, c.sb["hqbf"][:, o0:o0 + ow], ow)
                self.mk[f"hq{i}_{oc}"] = me.mark()
            for oc, (o0, ow) in enumerate(OCS):
                me.wait(self.W["tensor"], self.mk.get(f"tr2{i}_{oc}", 0))
                me.tensor_copy(c.sb["hqT"][:, o0:o0 + ow],
                               c.pt_bf[:, 1024:1024 + ow])
                self.mk[f"hqt{i}_{oc}"] = me.mark()
            me.wait(self.W["tensor"], self.mk.get(f"down{i}", 0))
            if i >= 2:
                me.wait(self.W["sync"], self.mk.get(f"st{i-2}", 0))
            me.op("_custom_dve", AFFINE_THEN_ADD, out=self.xnew(i),
                  in0=c.pgu[:, 0:1024], in1=self.xt(i), s0=0.5, s1=0.0)
            self.mk[f"xn{i}"] = me.mark()
            me.reduce_sum(c.sb["msc"][:, 0:1], self.xnew(i), AX.X)
            me.tensor_scalar(c.sb["stats_m" + self.so][:, i:i + 1],
                             c.sb["msc"][:, 0:1], 1.0 / DM, None, ALU.mult)
        me.wait(self.W["scalar"], self.mk.get(f"sq{NT-1}", 0))
        me.tensor_scalar(c.sb["tmpA"][:, 0:NT],
                         c.sb["stats_sq" + self.so][:, 0:NT],
                         1.0 / DM, None, ALU.mult)
        me.tensor_tensor(c.sb["tmpB"][:, 0:NT],
                         c.sb["stats_m" + self.so][:, 0:NT],
                         c.sb["stats_m" + self.so][:, 0:NT], ALU.mult)
        me.tensor_tensor(c.sb["tmpA"][:, 0:NT], c.sb["tmpA"][:, 0:NT],
                         c.sb["tmpB"][:, 0:NT], ALU.subtract)
        self.mk["var"] = me.mark()

    def emit_tensor(self):
        c, me = self.ctx, self.W["tensor"]
        if self.load_stash:
            for dst in ("Wg", "Wu", "Wd"):
                me.wait(self.W["sync"], self.mk.get("wl_" + dst, 0))
        for i in range(NT):
            me.wait(self.W["vector"], self.mk.get(f"xq{i}", 0))
            if i > 0:
                me.wait(self.W["vector"], self.mk.get(f"xqt{i-1}", 0))
            for cc in range(8):
                me.transpose(c.pt_bf[:, cc * 128:(cc + 1) * 128],
                             c.sb["xqbf"][:, cc * 128:(cc + 1) * 128],
                             c.sb["identbf"][:])
            self.mk[f"tr{i}"] = me.mark()
            me.wait(self.W["vector"], self.mk.get(f"xqt{i}", 0))
            for oc, (o0, ow) in enumerate(OCS):
                if oc < 2:
                    me.wait(self.W["vector"], self.mk.get(f"xn{i-1}", 0))
                    if i > 0:
                        me.wait(self.W["vector"], self.mk.get(f"hp{i-1}_5", 0))
                else:
                    me.wait(self.W["vector"], self.mk.get(f"hp{i}_{oc-2}", 0))
                pair = (oc % 2) * 1024
                hg = c.pgu[:, pair:pair + ow]
                hu = c.pgu[:, pair + 512:pair + 512 + ow]
                for cc in range(8):
                    me.matmul(hg, c.sb["xqT"][:, cc * 128:(cc + 1) * 128],
                              c.sb["Wg"][:, cc * DFF + o0:cc * DFF + o0 + ow],
                              start=(cc == 0), stop=(cc == 7))
                for cc in range(8):
                    me.matmul(hu, c.sb["xqT"][:, cc * 128:(cc + 1) * 128],
                              c.sb["Wu"][:, cc * DFF + o0:cc * DFF + o0 + ow],
                              start=(cc == 0), stop=(cc == 7))
                self.mk[f"mm{i}_{oc}"] = me.mark()
            for oc, (o0, ow) in enumerate(OCS):
                me.wait(self.W["vector"], self.mk.get(f"hq{i}_{oc}", 0))
                if not (i == 0 and oc == 0):
                    pi, po = (i, oc - 1) if oc > 0 else (i - 1, NOC - 1)
                    me.wait(self.W["vector"], self.mk.get(f"hqt{pi}_{po}", 0))
                for k in range(ow // 128):
                    me.transpose(c.pt_bf[:, 1024 + k * 128:1024 + (k + 1) * 128],
                                 c.sb["hqbf"][:, o0 + k * 128:o0 + (k + 1) * 128],
                                 c.sb["identbf"][:])
                self.mk[f"tr2{i}_{oc}"] = me.mark()
            me.wait(self.W["vector"], self.mk.get(f"hp{i}_4", 0))
            last_wait = -1
            for cc in range(22):
                oc_of = (cc * 128) // 512
                if oc_of != last_wait:
                    me.wait(self.W["vector"], self.mk.get(f"hqt{i}_{oc_of}", 0))
                    last_wait = oc_of
                me.matmul(c.pgu[:, 0:512], c.sb["hqT"][:, cc * 128:(cc + 1) * 128],
                          c.sb["Wd"][:, cc * DM:cc * DM + 512],
                          start=(cc == 0), stop=(cc == 21))
                me.matmul(c.pgu[:, 512:1024],
                          c.sb["hqT"][:, cc * 128:(cc + 1) * 128],
                          c.sb["Wd"][:, cc * DM + 512:cc * DM + 1024],
                          start=(cc == 0), stop=(cc == 21))
            self.mk[f"down{i}"] = me.mark()




NH = 16
HD = 64


class AttnPhase(Phase):

    def __init__(self, ctx, si, so, tag):
        super().__init__(ctx, tag)
        self.si = si
        self.so = so

    def xt(self, i):
        return self.ctx.sb["xt"][:, (i % 2) * DM:(i % 2) * DM + DM]

    def xnew(self, i):
        return self.ctx.sb["xnew"][:, (i % 2) * DM:(i % 2) * DM + DM]

    def emit_gpsimd(self):
        c, me = self.ctx, self.W["gpsimd"]
        for k, nm in enumerate(("WqT", "WkT", "WvT", "WfcT")):
            self.mk["w_" + nm] = me.dma_start(
                out=c.sb["Wat"][:, k * 8192:(k + 1) * 8192].rearrange(
                    "p (a b) -> p a b", b=1024),
                in_=c.dram[nm].rearrange("(a p) b -> p a b", p=128))

    def emit_sync(self, E=None):
        c, me = self.ctx, self.W["sync"]
        xs = c.dram["x_state"].rearrange("(t p) d -> t p d", p=128)
        for i in range(2):
            self.mk[f"ld{i}"] = me.dma_start(out=self.xt(i), in_=xs[i])
        for i in range(NT):
            me.wait(self.W["vector"], self.mk.get(f"xn{i}", 0))
            self.mk[f"st{i}"] = me.dma_start(out=xs[i], in_=self.xnew(i))
            if i + 2 < NT:
                self.mk[f"ld{i+2}"] = me.dma_start(out=self.xt(i + 2), in_=xs[i + 2])
        me.wait_self_all()

    def emit_scalar(self, E=None):
        c, me = self.ctx, self.W["scalar"]
        for i in range(NT):
            me.wait(self.W["vector"], self.mk.get(f"shift{i}", 0))
            me.activation(c.sb["att_E"][:, 0:256], c.sb["att_sc"][:, 0:256], AF.Exp)
            self.mk[f"exp{i}"] = me.mark()
            me.wait(self.W["vector"], self.mk.get(f"xn{i}", 0))
            me.activation(c.sb["sqscr"][:, 0:DM], self.xnew(i), AF.Square,
                          accum_out=c.sb["stats_sq" + self.so][:, i:i + 1])
            self.mk[f"sq{i}"] = me.mark()
        me.wait(self.W["vector"], self.mk.get("var", 0))
        me.activation(c.sb["tmpB"][:, 0:NT], c.sb["tmpA"][:, 0:NT], AF.Ln,
                      bias=EPS)
        me.activation(c.sb["stats_r" + self.so][:, 0:NT], c.sb["tmpB"][:, 0:NT],
                      AF.Exp, scale=-0.5)

    def emit_vector(self, E=None):
        c, me = self.ctx, self.W["vector"]
        for i in range(NT):
            me.wait(self.W["sync"], self.mk.get(f"ld{i}", 0))
            me.tensor_scalar(c.sb["xnbf"][:], self.xt(i),
                             c.sb["stats_m" + self.si][:, i:i + 1],
                             c.sb["stats_r" + self.si][:, i:i + 1],
                             ALU.subtract, ALU.mult)
            self.mk[f"xn_r{i}"] = me.mark()
            me.wait(self.W["tensor"], self.mk.get(f"tr{i}", 0))
            me.tensor_copy(c.sb["xnT"][:], c.pt_bf[:, 2048:3072])
            self.mk[f"xnt{i}"] = me.mark()
            me.wait(self.W["tensor"], self.mk.get(f"qkv{i}", 0))
            me.tensor_copy(c.sb["qbf"][:], c.pgu[:, 0:1024])
            me.tensor_copy(c.sb["kbf"][:], c.pgu[:, 1024:2048])
            me.tensor_copy(c.sb["vbf"][:], c.ptf[:, 0:1024])
            self.mk[f"ev{i}"] = me.mark()
            q4 = c.sb["qbf"][:].rearrange("p (h d) -> p h d", d=HD)
            k4 = c.sb["kbf"][:].rearrange("p (t d) -> p t d", d=HD)
            sc3 = c.sb["att_sc"][:, 0:256].rearrange("p (h t) -> p h t", t=NH)
            for hg in range(4):
                qv = q4[:, hg * 4:(hg + 1) * 4, :].unsqueeze(2).broadcast_to(
                    [128, 4, NH, HD])
                kv = k4.unsqueeze(1).broadcast_to([128, 4, NH, HD])
                m4 = c.sb["att_m4"][:, 0:4096].rearrange(
                    "p (h t d) -> p h t d", t=NH, d=HD)
                me.tensor_tensor(m4, qv, kv, ALU.mult)
                me.tensor_reduce(sc3[:, hg * 4:(hg + 1) * 4, :], m4, AX.X,
                                 ALU.add)
            me.tensor_reduce(c.sb["att_mx"][:, 0:NH], sc3, AX.X, ALU.max)
            mxb = c.sb["att_mx"][:, 0:NH].unsqueeze(2).broadcast_to([128, NH, NH])
            me.tensor_tensor(sc3, sc3, mxb, ALU.subtract)
            me.tensor_scalar(c.sb["att_sc"][:, 0:256], c.sb["att_sc"][:, 0:256],
                             0.125, None, ALU.mult)
            self.mk[f"shift{i}"] = me.mark()
            me.wait(self.W["scalar"], self.mk.get(f"exp{i}", 0))
            E3 = c.sb["att_E"][:, 0:256].rearrange("p (h t) -> p h t", t=NH)
            me.tensor_reduce(c.sb["att_sm"][:, 0:NH], E3, AX.X, ALU.add)
            me.reciprocal(c.sb["att_rc"][:, 0:NH], c.sb["att_sm"][:, 0:NH])
            rcb = c.sb["att_rc"][:, 0:NH].unsqueeze(2).broadcast_to([128, NH, NH])
            me.tensor_tensor(c.sb["att_A"][:, 0:256].rearrange(
                "p (h t) -> p h t", t=NH), E3, rcb, ALU.mult)
            A3 = c.sb["att_A"][:, 0:256].rearrange("p (h t) -> p h t", t=NH)
            v3 = c.sb["vbf"][:].rearrange("p (t d) -> p t d", d=HD)
            vperm = v3.transpose([0, 2, 1])
            ao3 = c.sb["aof"][:].rearrange("p (h d) -> p h d", d=HD)
            for hg in range(4):
                av = A3[:, hg * 4:(hg + 1) * 4, :].unsqueeze(2).broadcast_to(
                    [128, 4, HD, NH])
                vv = vperm.unsqueeze(1).broadcast_to([128, 4, HD, NH])
                m4 = c.sb["att_m4"][:, 0:4096].rearrange(
                    "p (h d t) -> p h d t", d=HD, t=NH)
                me.tensor_tensor(m4, av, vv, ALU.mult)
                me.tensor_reduce(ao3[:, hg * 4:(hg + 1) * 4, :], m4, AX.X,
                                 ALU.add)
            me.tensor_copy(c.sb["abf"][:], c.sb["aof"][:])
            self.mk[f"ao{i}"] = me.mark()
            me.wait(self.W["tensor"], self.mk.get(f"atr{i}", 0))
            me.tensor_copy(c.sb["aT"][:], c.pt_bf[:, 2048:3072])
            self.mk[f"aT{i}"] = me.mark()
            me.wait(self.W["tensor"], self.mk.get(f"fc{i}", 0))
            if i >= 2:
                me.wait(self.W["sync"], self.mk.get(f"st{i-2}", 0))
                me.wait(self.W["scalar"], self.mk.get(f"sq{i-2}", 0))
            me.op("_custom_dve", AFFINE_THEN_ADD, out=self.xnew(i),
                  in0=c.pgu[:, 0:1024], in1=self.xt(i), s0=1.0, s1=0.0)
            self.mk[f"xn{i}"] = me.mark()
            me.reduce_sum(c.sb["msc"][:, 0:1], self.xnew(i), AX.X)
            me.tensor_scalar(c.sb["stats_m" + self.so][:, i:i + 1],
                             c.sb["msc"][:, 0:1], 1.0 / DM, None, ALU.mult)
        me.wait(self.W["scalar"], self.mk.get(f"sq{NT-1}", 0))
        me.tensor_scalar(c.sb["tmpA"][:, 0:NT],
                         c.sb["stats_sq" + self.so][:, 0:NT],
                         1.0 / DM, None, ALU.mult)
        me.tensor_tensor(c.sb["tmpB"][:, 0:NT], c.sb["stats_m" + self.so][:, 0:NT],
                         c.sb["stats_m" + self.so][:, 0:NT], ALU.mult)
        me.tensor_tensor(c.sb["tmpA"][:, 0:NT], c.sb["tmpA"][:, 0:NT],
                         c.sb["tmpB"][:, 0:NT], ALU.subtract)
        self.mk["var"] = me.mark()

    def emit_tensor(self, E=None):
        c, me = self.ctx, self.W["tensor"]
        for k in ("WqT", "WkT", "WvT", "WfcT"):
            me.wait(self.W["gpsimd"], self.mk.get("w_" + k, 0))
        for i in range(NT):
            me.wait(self.W["vector"], self.mk.get(f"xn_r{i}", 0))
            if i > 0:
                me.wait(self.W["vector"], self.mk.get(f"xnt{i-1}", 0))
            for cc in range(8):
                me.transpose(c.pt_bf[:, 2048 + cc * 128:2048 + (cc + 1) * 128],
                             c.sb["xnbf"][:, cc * 128:(cc + 1) * 128],
                             c.sb["identbf"][:])
            self.mk[f"tr{i}"] = me.mark()
            me.wait(self.W["vector"], self.mk.get(f"xnt{i}", 0))
            if i > 0:
                me.wait(self.W["vector"], self.mk.get(f"ev{i-1}", 0))
                me.wait(self.W["vector"], self.mk.get(f"xn{i-1}", 0))
            for w, out in ((0, c.pgu[:, 0:1024]), (1, c.pgu[:, 1024:2048]),
                           (2, c.ptf[:, 0:1024])):
                for half in range(2):
                    for cc in range(8):
                        me.matmul(out[:, half * 512:half * 512 + 512],
                                  c.sb["xnT"][:, cc * 128:(cc + 1) * 128],
                                  c.sb["Wat"][:, w * 8192 + cc * 1024 + half * 512:
                                              w * 8192 + cc * 1024 + half * 512 + 512],
                                  start=(cc == 0), stop=(cc == 7))
            self.mk[f"qkv{i}"] = me.mark()
            me.wait(self.W["vector"], self.mk.get(f"ao{i}", 0))
            for cc in range(8):
                me.transpose(c.pt_bf[:, 2048 + cc * 128:2048 + (cc + 1) * 128],
                             c.sb["abf"][:, cc * 128:(cc + 1) * 128],
                             c.sb["identbf"][:])
            self.mk[f"atr{i}"] = me.mark()
            me.wait(self.W["vector"], self.mk.get(f"aT{i}", 0))
            for half in range(2):
                for cc in range(8):
                    me.matmul(c.pgu[:, half * 512:half * 512 + 512],
                              c.sb["aT"][:, cc * 128:(cc + 1) * 128],
                              c.sb["Wat"][:, 3 * 8192 + cc * 1024 + half * 512:
                                          3 * 8192 + cc * 1024 + half * 512 + 512],
                              start=(cc == 0), stop=(cc == 7))
            self.mk[f"fc{i}"] = me.mark()




NEXP = 64
MOE_SUPERS = [list(range(s, min(s + 3, NT))) for s in range(0, NT, 3)]


class MoEWeightPrep(Phase):

    def __init__(self, ctx):
        super().__init__(ctx, "wm")

    def emit_gpsimd(self):
        c, me = self.ctx, self.W["gpsimd"]
        vR = c.dram["valuesR"].rearrange("p (e v) -> p e v", v=DM)
        vS = c.dram["values_st"].rearrange("p (e v) -> p e v", v=DM)
        for e in range(NEXP):
            if e >= 2:
                me.wait(me, self.mk.get(f"st{e-2}", 0))
            self.mk[f"ld{e}"] = me.dma_start(
                out=c.sb["vstg"][:, (e % 2) * DM:(e % 2) * DM + DM],
                in_=vR[:, e, :])
            me.wait(me, self.mk[f"ld{e}"])
            self.mk[f"st{e}"] = me.dma_start(
                out=vS[:, e, :],
                in_=c.sb["vstg"][:, (e % 2) * DM:(e % 2) * DM + DM])
        me.wait_self_all()


class MoEPhase(Phase):

    def __init__(self, ctx, si, so, tag):
        super().__init__(ctx, tag)
        self.si = si
        self.so = so

    def xt(self, i):
        return self.ctx.sb["m_xt"][:, i % 4, :]

    def xnew(self, i):
        return self.ctx.sb["xnew"][:, (i % 2) * DM:(i % 2) * DM + DM]

    def emit_gpsimd(self):
        c, me = self.ctx, self.W["gpsimd"]
        kR = c.dram["keysR"]
        for s8 in range(8):
            self.mk[f"keys{s8}"] = me.dma_start(
                out=c.sb["keysb"][:, s8 * 8192:(s8 + 1) * 8192],
                in_=kR[:, s8 * 8192:(s8 + 1) * 8192])
        self.mk["selT"] = me.dma_start(
            out=c.sb["selTb"][:],
            in_=c.dram["selT"].rearrange("(a p) b -> p a b", p=128))
        vS = c.dram["values_st"].rearrange("p (e v) -> p e v", v=DM)
        n = 0
        for sx, tiles in enumerate(MOE_SUPERS):
            for e in range(NEXP):
                me.wait(self.W["tensor"], self.mk.get(("dnl", n - 2), 0))
                self.mk[("val", sx, e)] = me.dma_start(
                    out=c.sb["vbuf"][:, (n % 2) * DM:(n % 2) * DM + DM],
                    in_=vS[:, e, :])
                n += 1
        me.wait_self_all()

    def emit_sync(self, E=None):
        c, me = self.ctx, self.W["sync"]
        xs = c.dram["x_state"].rearrange("(t p) d -> t p d", p=128)
        for i in MOE_SUPERS[0]:
            self.mk[f"ld{i}"] = me.dma_start(out=self.xt(i), in_=xs[i])
        for sx, tiles in enumerate(MOE_SUPERS):
            if sx + 1 < len(MOE_SUPERS):
                for i in MOE_SUPERS[sx + 1]:
                    me.wait(self.W["vector"], self.mk.get(f"xn{i-4}", 0))
                    self.mk[f"ld{i}"] = me.dma_start(out=self.xt(i), in_=xs[i])
            for i in tiles:
                me.wait(self.W["vector"], self.mk.get(f"xn{i}", 0))
                self.mk[f"st{i}"] = me.dma_start(out=xs[i], in_=self.xnew(i))
        me.wait_self_all()

    def emit_scalar(self, E=None):
        c, me = self.ctx, self.W["scalar"]
        for sx, tiles in enumerate(MOE_SUPERS):
            for tl, i in enumerate(tiles):
                me.wait(self.W["tensor"], self.mk.get(f"lg{i}", 0))
                me.activation(c.sb["selsb"][:, 0:NEXP], c.ptf[:, 1408:1472],
                              AF.Sigmoid)
                self.mk[f"sig{i}"] = me.mark()
                me.wait(self.W["vector"], self.mk.get(f"gt{i}", 0))
            for tl, i in enumerate(tiles):
                me.wait(self.W["vector"], self.mk.get(f"xn{i}", 0))
                me.activation(c.sb["sqscr"][:, 0:DM], self.xnew(i), AF.Square,
                              accum_out=c.sb["stats_sq" + self.so][:, i:i + 1])
                self.mk[f"sq{i}"] = me.mark()
        me.wait(self.W["vector"], self.mk.get("var", 0))
        me.activation(c.sb["tmpB"][:, 0:NT], c.sb["tmpA"][:, 0:NT], AF.Ln,
                      bias=EPS)
        me.activation(c.sb["stats_r" + self.so][:, 0:NT], c.sb["tmpB"][:, 0:NT],
                      AF.Exp, scale=-0.5)

    def emit_vector(self, E=None):
        c, me = self.ctx, self.W["vector"]
        nval = 0
        for sx, tiles in enumerate(MOE_SUPERS):
            ntl = len(tiles)
            for tl, i in enumerate(tiles):
                me.wait(self.W["sync"], self.mk.get(f"ld{i}", 0))
                me.tensor_scalar(c.sb["m_xnbf"][:], self.xt(i),
                                 c.sb["stats_m" + self.si][:, i:i + 1],
                                 c.sb["stats_r" + self.si][:, i:i + 1],
                                 ALU.subtract, ALU.mult)
                self.mk[f"xn_r{i}"] = me.mark()
                me.wait(self.W["tensor"], self.mk.get(f"tr{i}", 0))
                me.tensor_copy(c.sb["xnTs"][:, :, tl * 128:(tl + 1) * 128],
                               c.pgu_bf[:, 0:1024].rearrange(
                                   "p (a b) -> p a b", b=128))
                self.mk[f"xnt{i}"] = me.mark()
                me.wait(self.W["scalar"], self.mk.get(f"sig{i}", 0))
                me.tensor_copy(c.sb["selw"][:, 0:NEXP], c.sb["selsb"][:, 0:NEXP])
                for r in range(4):
                    me.reduce_max(c.sb["msc"][:, 1:2], c.sb["selw"][:, 0:NEXP],
                                  AX.X)
                    if r < 3:
                        me.tensor_scalar(c.sb["selm"][:, 0:NEXP],
                                         c.sb["selw"][:, 0:NEXP],
                                         c.sb["msc"][:, 1:2], -1e30,
                                         ALU.is_ge, ALU.mult)
                        me.tensor_tensor(c.sb["selw"][:, 0:NEXP],
                                         c.sb["selw"][:, 0:NEXP],
                                         c.sb["selm"][:, 0:NEXP], ALU.add)
                me.tensor_scalar(c.sb["selm"][:, 0:NEXP], c.sb["selsb"][:, 0:NEXP],
                                 c.sb["msc"][:, 1:2], None, ALU.is_ge)
                me.tensor_tensor(c.sb["gates"][:, tl * 64:tl * 64 + 64],
                                 c.sb["selm"][:, 0:NEXP],
                                 c.sb["selsb"][:, 0:NEXP], ALU.mult)
                self.mk[f"gt{i}"] = me.mark()
            for e in range(NEXP):
                for tl, i in enumerate(tiles):
                    me.wait(self.W["tensor"], self.mk.get(("h", sx, e, tl), 0))
                    me.tensor_scalar(
                        c.sb["scb"][:, tl * 128:tl * 128 + 128],
                        c.ptf[:, 1024 + tl * 128:1024 + tl * 128 + 128],
                        0.0, c.sb["gates"][:, tl * 64 + e:tl * 64 + e + 1],
                        ALU.max, ALU.mult)
                    self.mk[("sc", sx, e, tl)] = me.mark()
                for tl, i in enumerate(tiles):
                    me.wait(self.W["tensor"], self.mk.get(("tr", sx, e, tl), 0))
                    me.tensor_copy(c.sb["scT"][:, 0:128], c.pt_bf[:, 2816:2944])
                    self.mk[("scT", sx, e, tl)] = me.mark()
            for tl, i in enumerate(tiles):
                me.wait(self.W["tensor"], self.mk.get(("dn", sx, NEXP - 1), 0))
                if i >= 2:
                    me.wait(self.W["sync"], self.mk.get(f"st{i-2}", 0))
                    me.wait(self.W["scalar"], self.mk.get(f"sq{i-2}", 0))
                outp = (c.pgu[:, 0:1024], c.pgu[:, 1024:2048],
                        c.ptf[:, 0:1024])[tl]
                me.op("_custom_dve", AFFINE_THEN_ADD, out=self.xnew(i),
                      in0=outp, in1=self.xt(i), s0=0.5, s1=0.0)
                self.mk[f"xn{i}"] = me.mark()
                me.reduce_sum(c.sb["msc"][:, 0:1], self.xnew(i), AX.X)
                me.tensor_scalar(c.sb["stats_m" + self.so][:, i:i + 1],
                                 c.sb["msc"][:, 0:1], 1.0 / DM, None, ALU.mult)
            self.mk[f"sup{sx}"] = me.mark()
        me.wait(self.W["scalar"], self.mk.get(f"sq{NT-1}", 0))
        me.tensor_scalar(c.sb["tmpA"][:, 0:NT],
                         c.sb["stats_sq" + self.so][:, 0:NT],
                         1.0 / DM, None, ALU.mult)
        me.tensor_tensor(c.sb["tmpB"][:, 0:NT], c.sb["stats_m" + self.so][:, 0:NT],
                         c.sb["stats_m" + self.so][:, 0:NT], ALU.mult)
        me.tensor_tensor(c.sb["tmpA"][:, 0:NT], c.sb["tmpA"][:, 0:NT],
                         c.sb["tmpB"][:, 0:NT], ALU.subtract)
        self.mk["var"] = me.mark()

    def emit_tensor(self, E=None):
        c, me = self.ctx, self.W["tensor"]
        for s8 in range(8):
            me.wait(self.W["gpsimd"], self.mk.get(f"keys{s8}", 0))
        me.wait(self.W["gpsimd"], self.mk.get("selT", 0))
        nval = 0
        for sx, tiles in enumerate(MOE_SUPERS):
            ntl = len(tiles)
            for tl, i in enumerate(tiles):
                me.wait(self.W["vector"], self.mk.get(f"xn_r{i}", 0))
                if tl == 0 and sx > 0:
                    me.wait(self.W["vector"], self.mk.get(f"sup{sx-1}", 0))
                if sx + tl > 0:
                    me.wait(self.W["vector"],
                            self.mk.get(f"xnt{i-1}", 0))
                for cc in range(8):
                    me.transpose(c.pgu_bf[:, cc * 128:(cc + 1) * 128],
                                 c.sb["m_xnbf"][:, cc * 128:(cc + 1) * 128],
                                 c.sb["identbf"][:])
                self.mk[f"tr{i}"] = me.mark()
                me.wait(self.W["vector"], self.mk.get(f"xnt{i}", 0))
                if sx + tl > 0:
                    me.wait(self.W["scalar"], self.mk.get(f"sig{i-1}", 0))
                for cc in range(8):
                    me.matmul(c.ptf[:, 1408:1472],
                              c.sb["xnTs"][:, cc, tl * 128:(tl + 1) * 128],
                              c.sb["selTb"][:, cc, :],
                              start=(cc == 0), stop=(cc == 7))
                self.mk[f"lg{i}"] = me.mark()
            for e in range(NEXP):
                me.wait(self.W["gpsimd"], self.mk.get(("val", sx, e), 0))
                for tl, i in enumerate(tiles):
                    me.wait(self.W["vector"], self.mk.get(f"gt{i}", 0))
                    if e == 0 and tl == 0:
                        me.wait(self.W["vector"],
                                self.mk.get(f"gt{tiles[-1]}", 0))
                    if tl > 0:
                        me.wait(self.W["vector"],
                                self.mk.get(("sc", sx, e, tl - 1), 0))
                    elif not (sx == 0 and e == 0):
                        pe = (sx, e - 1) if e > 0 else (sx - 1, NEXP - 1)
                        me.wait(self.W["vector"],
                                self.mk.get(("scT",) + pe +
                                            (len(MOE_SUPERS[pe[0]]) - 1,), 0))
                    for cc in range(8):
                        me.matmul(c.ptf[:, 1024 + tl * 128:1024 + tl * 128 + 128],
                                  c.sb["xnTs"][:, cc, tl * 128:(tl + 1) * 128],
                                  c.sb["keysb"][:, e * 1024 + cc * 128:
                                                e * 1024 + cc * 128 + 128],
                                  start=(cc == 0), stop=(cc == 7))
                    self.mk[("h", sx, e, tl)] = me.mark()
                for tl, i in enumerate(tiles):
                    me.wait(self.W["vector"], self.mk.get(("sc", sx, e, tl), 0))
                    if tl == 0:
                        me.wait(self.W["vector"],
                                self.mk.get(("sc", sx, e, len(tiles) - 1), 0))
                    if not (sx == 0 and e == 0 and tl == 0):
                        pv = ("scT", sx, e, tl - 1) if tl > 0 else (
                            ("scT", sx, e - 1, ntl - 1) if e > 0 else
                            ("scT", sx - 1, NEXP - 1,
                             len(MOE_SUPERS[sx - 1]) - 1))
                        me.wait(self.W["vector"], self.mk.get(pv, 0))
                    me.transpose(c.pt_bf[:, 2816:2944],
                                 c.sb["scb"][:, tl * 128:tl * 128 + 128],
                                 c.sb["identbf"][:])
                    self.mk[("tr", sx, e, tl)] = me.mark()
                    me.wait(self.W["vector"], self.mk.get(("scT", sx, e, tl), 0))
                    if e == 0:
                        me.wait(self.W["vector"], self.mk.get(f"sup{sx-1}", 0))
                    outp = (c.pgu[:, 0:1024], c.pgu[:, 1024:2048],
                            c.ptf[:, 0:1024])[tl]
                    vb = c.sb["vbuf"][:, (nval % 2) * DM:(nval % 2) * DM + DM]
                    for half in range(2):
                        me.matmul(outp[:, half * 512:half * 512 + 512],
                                  c.sb["scT"][:, 0:128],
                                  vb[:, half * 512:half * 512 + 512],
                                  start=(e == 0), stop=(e == NEXP - 1))
                self.mk[("dn", sx, e)] = me.mark()
                self.mk[("dnl", nval)] = me.mark()
                nval += 1





MSTAGES = ("E0", "E1", "E2", "E2X", "E2SCAN", "E2OUT")
NSEG = 8
SEGT = 512
SEGTILES = SEGT // 128


class MambaPhase(Phase):

    def __init__(self, ctx, si, so, tag):
        super().__init__(ctx, tag)
        self.si = si
        self.so = so

    def xt(self, i):
        return self.ctx.sb["e_xt"][:, i % 4, :]

    def xnew(self, i):
        return self.ctx.sb["xnew"][:, (i % 2) * DM:(i % 2) * DM + DM]

    def emit_gpsimd(self):
        c, me = self.ctx, self.W["gpsimd"]
        self.mk["win"] = me.dma_start(
            out=c.sb["Win_b"][:].rearrange("p a b -> p (a b)"),
            in_=c.dram["WinR"])
        self.mk["wout"] = me.dma_start(
            out=c.sb["Wout_b"][:].rearrange("p a b -> p (a b)"),
            in_=c.dram["WoutR"])
        self.mk["wxp"] = me.dma_start(
            out=c.sb["Wxp_b"][:].rearrange("p a b -> p (a b)"),
            in_=c.dram["WxpR"])
        self.mk["wdt"] = me.dma_start(
            out=c.sb["Wdt_b"][0:64, :, :].rearrange("p a b -> p (a b)"),
            in_=c.dram["WdtR"])
        xr = c.dram["xraw_st"].rearrange("p (a b) -> p a b", b=S)
        for sg in range(NSEG if "E0" in MSTAGES else 0):
            for dc in range(16):
                me.wait(self.W["vector"], self.mk.get(("xrev", sg, dc), 0))
                self.mk[("xrw", sg, dc)] = me.dma_start(
                    out=xr[:, dc, sg * SEGT:(sg + 1) * SEGT],
                    in_=c.sb["xrstage"][:, (dc % 2) * SEGT:(dc % 2) * SEGT + SEGT])
        self.mk["e0_dn"] = me.dn
        for dcz in range(16 if "E1" in MSTAGES else 0):
            me.wait(self.W["vector"], self.mk.get(("cvz", dcz), 0) if dcz > 0
                    else self.mk.get("cbz", 0))
            self.mk[("cvl", dcz)] = me.dma_start(
                out=c.sb["cbuf"][:, 1:S + 1], in_=xr[:, dcz, :])
            me.wait(self.W["scalar"], self.mk.get(("gel", dcz), 0))
            dst = c.dram["u_st"] if dcz < 8 else c.dram["z_st"]
            self.mk[("cgw", dcz)] = me.dma_start(
                out=dst.rearrange("p (a b) -> p a b", b=S)[:, dcz % 8, :],
                in_=c.sb["cgout"][:])
        us = c.dram["u_st"].rearrange("p (a b) -> p a b", b=S)
        zs = c.dram["z_st"].rearrange("p (a b) -> p a b", b=S)
        for sg in range(NSEG if "E2" in MSTAGES else 0):
            if sg == 0:
                for dcz in range(16):
                    me.wait(me, self.mk.get(("cgw", dcz), 0))
            me.wait(self.W["vector"], self.mk.get(("useg_done", sg - 1), 0))
            self.mk[("ubf", sg)] = me.dma_start(
                out=c.sb["ubf"][:], in_=us[:, :, sg * SEGT:(sg + 1) * SEGT])
            self.mk[("zbf", sg)] = me.dma_start(
                out=c.sb["zbf"][:], in_=zs[:, :, sg * SEGT:(sg + 1) * SEGT])
            if "E2X" not in MSTAGES:
                continue
            me.wait(self.W["vector"], self.mk.get(("bc_ev", sg), 0))
            if sg > 0:
                me.wait(me, self.mk.get(("bbc", sg - 1, 15), 0))
                me.wait(me, self.mk.get(("cbc", sg - 1, 15), 0))
            self.mk[("bst", sg)] = me.dma_start(out=c.dram["bc_st"],
                                                in_=c.sb["Bsb"][0:16, :])
            self.mk[("cst", sg)] = me.dma_start(out=c.dram["cc_st"],
                                                in_=c.sb["Csb"][0:16, :])
            me.wait(me, self.mk[("bst", sg)])
            me.wait(me, self.mk[("cst", sg)])
            for n in range(16 if "E2SCAN" in MSTAGES else 0):
                me.wait(self.W["vector"], self.mk.get(("nfree", sg, n - 1), 0))
                self.mk[("bbc", sg, n)] = me.dma_start(
                    out=c.sb["bbc"][:],
                    in_=c.dram["bc_st"][n:n + 1, :].broadcast_to([128, SEGT]))
                self.mk[("cbc", sg, n)] = me.dma_start(
                    out=c.sb["cbc"][:],
                    in_=c.dram["cc_st"][n:n + 1, :].broadcast_to([128, SEGT]))
        me.wait_self_all()

    def emit_sync(self, E=None):
        c, me = self.ctx, self.W["sync"]
        xs = c.dram["x_state"].rearrange("(t p) d -> t p d", p=128)
        for nm, dst in (("KxR", "kx"), ("KzR", "kz"), ("bxR", "bx"),
                        ("bzR", "bz"), ("AlogR", "alog"), ("DR", "dsb"),
                        ("bdtR", "bdt")):
            self.mk["p_" + dst] = me.dma_start(out=c.sb[dst][:], in_=c.dram[nm])
        for i in range(NT if "E0" in MSTAGES else 0):
            if i >= 4:
                me.wait(self.W["vector"], self.mk.get(f"e0xn{i-4}", 0))
            self.mk[f"ld{i}"] = me.dma_start(out=self.xt(i), in_=xs[i])
        for sg in range(NSEG if "E2" in MSTAGES else 0):
            for tl in range(SEGTILES):
                i = sg * SEGTILES + tl
                if i < 4:
                    me.wait(self.W["vector"], self.mk.get(f"e0xn{NT-1}", 0))
                else:
                    me.wait(self.W["vector"], self.mk.get(f"xn{i-4}", 0))
                self.mk[f"ld2_{i}"] = me.dma_start(out=self.xt(i), in_=xs[i])
            for tl in range(SEGTILES):
                i = sg * SEGTILES + tl
                me.wait(self.W["vector"], self.mk.get(f"xn{i}", 0))
                self.mk[f"st{i}"] = me.dma_start(out=xs[i], in_=self.xnew(i))
        me.wait_self_all()

    def emit_scalar(self, E=None):
        c, me = self.ctx, self.W["scalar"]
        me.wait(self.W["sync"], self.mk.get("p_alog", 0))
        me.activation(c.sb["asb"][:].rearrange("p a b -> p (a b)"),
                      c.sb["alog"][:].rearrange("p a b -> p (a b)"), AF.Exp)
        self.mk["aexp"] = me.mark()
        for dcz in range(16 if "E1" in MSTAGES else 0):
            me.wait(self.W["vector"], self.mk.get(("cv4", dcz), 0))
            if dcz > 0:
                me.wait(self.W["gpsimd"], self.mk.get(("cgw", dcz - 1), 0))
            b = (c.sb["bx"] if dcz < 8 else c.sb["bz"])[:, dcz % 8:dcz % 8 + 1]
            me.activation(c.sb["cgout"][:], c.sb["ct2"][:], AF.Gelu, bias=b)
            self.mk[("gel", dcz)] = me.mark()
        for sg in range(NSEG if ("E2" in MSTAGES and "E2X" in MSTAGES) else 0):
            for dc in range(8):
                me.wait(self.W["tensor"], self.mk.get(("dtp", sg, dc), 0))
                bcol = c.sb["bdt"][:, dc:dc + 1]
                me.activation(c.sb["spa"][:], c.ptf[:, 0:SEGT], AF.Abs, bias=bcol)
                me.activation(c.sb["spa"][:], c.sb["spa"][:], AF.Exp, scale=-1.0)
                me.activation(c.sb["spa"][:], c.sb["spa"][:], AF.Ln, bias=1.0)
                self.mk[("spl", sg, dc)] = me.mark()
                me.wait(self.W["vector"], self.mk.get(("spd", sg, dc), 0))
            for n in range(16 if "E2SCAN" in MSTAGES else 0):
                for dc in range(8):
                    if dc > 0:
                        pk = ("dAfree", sg, n, dc - 1)
                    elif n > 0:
                        pk = ("dAfree", sg, n - 1, 7)
                    else:
                        pk = ("dAfree", sg - 1, 15, 7)
                    me.wait(self.W["vector"], self.mk.get(pk, 0))
                    me.activation(c.sb["dA"][:], c.sb["delta"][:, dc, :], AF.Exp,
                                  scale=c.sb["asb"][:, dc, n:n + 1])
                    self.mk[("dA", sg, n, dc)] = me.mark()
            for tl in range(SEGTILES if "E2OUT" in MSTAGES else 0):
                i = sg * SEGTILES + tl
                me.wait(self.W["vector"], self.mk.get(f"xn{i}", 0))
                me.activation(c.sb["sqscr"][:, 0:DM], self.xnew(i), AF.Square,
                              accum_out=c.sb["stats_sq" + self.so][:, i:i + 1])
                self.mk[f"sq{i}"] = me.mark()
        if "E2" in MSTAGES:
            me.wait(self.W["vector"], self.mk.get("var", 0))
            me.activation(c.sb["tmpB"][:, 0:NT], c.sb["tmpA"][:, 0:NT], AF.Ln,
                          bias=EPS)
            me.activation(c.sb["stats_r" + self.so][:, 0:NT],
                          c.sb["tmpB"][:, 0:NT], AF.Exp, scale=-0.5)

    def emit_vector(self, E=None):
        c, me = self.ctx, self.W["vector"]
        me.memset(c.sb["carry"][:], 0.0)
        me.wait(self.W["scalar"], self.mk.get("aexp", 0))
        me.tensor_scalar(c.sb["asb"][:].rearrange("p a b -> p (a b)"),
                         c.sb["asb"][:].rearrange("p a b -> p (a b)"),
                         -1.0, None, ALU.mult)
        self.mk["aneg"] = me.mark()
        for sg in range(NSEG if "E0" in MSTAGES else 0):
            for tl in range(SEGTILES):
                i = sg * SEGTILES + tl
                me.wait(self.W["sync"], self.mk.get(f"ld{i}", 0))
                me.tensor_scalar(c.sb["m_xnbf"][:], self.xt(i),
                                 c.sb["stats_m" + self.si][:, i:i + 1],
                                 c.sb["stats_r" + self.si][:, i:i + 1],
                                 ALU.subtract, ALU.mult)
                self.mk[f"e0xn{i}"] = me.mark()
                me.wait(self.W["tensor"], self.mk.get(f"e0tr{i}", 0))
                if tl == 0 and sg > 0:
                    me.wait(self.W["tensor"], self.mk.get(("xzmm", sg - 1, 15), 0))
                me.tensor_copy(c.sb["xnTseg"][:, :, tl * 128:(tl + 1) * 128],
                               c.pgu_bf[:, 0:1024].rearrange(
                                   "p (a b) -> p a b", b=128))
                self.mk[f"e0t{i}"] = me.mark()
            for dc in range(16):
                me.wait(self.W["tensor"], self.mk.get(("xzmm", sg, dc), 0))
                if dc >= 2:
                    me.wait(self.W["gpsimd"], self.mk.get(("xrw", sg, dc - 2), 0))
                elif sg > 0:
                    me.wait(self.W["gpsimd"], self.mk.get(("xrw", sg - 1, 14 + dc), 0))
                me.tensor_copy(
                    c.sb["xrstage"][:, (dc % 2) * SEGT:(dc % 2) * SEGT + SEGT],
                    c.pgu[:, (dc % 2) * 512:(dc % 2) * 512 + 512])
                self.mk[("xrev", sg, dc)] = me.mark()
        me.wait_dma_upto(self.W["gpsimd"], self.mk.get("e0_dn", 0))
        me.memset(c.sb["cbuf"][:], 0.0)
        self.mk["cbz"] = me.mark()
        self.mk[("cvz", 0)] = self.mk["cbz"]
        for dcz in range(16 if "E1" in MSTAGES else 0):
            me.wait(self.W["gpsimd"], self.mk.get(("cvl", dcz), 0))
            kk = (c.sb["kx"] if dcz < 8 else c.sb["kz"])
            dc8 = dcz % 8
            me.tensor_single_scalar(c.sb["ct1"][:], c.sb["cbuf"][:, 0:S],
                                    kk[:, dc8, 0:1], ALU.mult)
            me.scalar_tensor_tensor(c.sb["ct2"][:], c.sb["cbuf"][:, 1:S + 1],
                                    kk[:, dc8, 1:2], c.sb["ct1"][:],
                                    ALU.mult, ALU.add)
            me.scalar_tensor_tensor(c.sb["ct1"][:], c.sb["cbuf"][:, 2:S + 2],
                                    kk[:, dc8, 2:3], c.sb["ct2"][:],
                                    ALU.mult, ALU.add)
            me.scalar_tensor_tensor(c.sb["ct2"][:], c.sb["cbuf"][:, 3:S + 3],
                                    kk[:, dc8, 3:4], c.sb["ct1"][:],
                                    ALU.mult, ALU.add)
            self.mk[("cv4", dcz)] = me.mark()
            me.wait(self.W["scalar"], self.mk.get(("gel", dcz), 0))
            self.mk[("cvz", dcz + 1)] = me.mark()
        for sg in range(NSEG if "E2" in MSTAGES else 0):
            if "E2X" not in MSTAGES:
                continue
            me.wait(self.W["tensor"], self.mk.get(("xdbl", sg), 0))
            me.tensor_copy(c.sb["dtlow"][0:64, :], c.pgu[0:64, 0:512])
            me.tensor_copy(c.sb["Bsb"][0:16, :], c.pgu[0:16, 512:1024])
            me.tensor_copy(c.sb["Csb"][0:16, :], c.pgu[0:16, 1536:2048])
            self.mk[("bc_ev", sg)] = me.mark()
            for dc in range(8):
                me.wait(self.W["scalar"], self.mk.get(("spl", sg, dc), 0))
                me.tensor_scalar(c.sb["spr"][:], c.ptf[:, 0:SEGT],
                                 c.sb["bdt"][:, dc:dc + 1], 0.0,
                                 ALU.add, ALU.max)
                me.tensor_tensor(c.sb["delta"][:, dc, :], c.sb["spa"][:],
                                 c.sb["spr"][:], ALU.add)
                self.mk[("spd", sg, dc)] = me.mark()
            me.wait(self.W["gpsimd"], self.mk.get(("ubf", sg), 0))
            for dc in range(8):
                me.tensor_tensor(c.sb["du"][:, dc, :], c.sb["delta"][:, dc, :],
                                 c.sb["ubf"][:, dc, :], ALU.mult)
            self.mk[("du", sg)] = me.mark()
            for n in range(16 if "E2SCAN" in MSTAGES else 0):
                me.wait(self.W["gpsimd"], self.mk.get(("bbc", sg, n), 0))
                me.wait(self.W["gpsimd"], self.mk.get(("cbc", sg, n), 0))
                for dc in range(8):
                    me.wait(self.W["scalar"], self.mk.get(("dA", sg, n, dc), 0))
                    me.tensor_tensor(c.sb["dBu"][:], c.sb["du"][:, dc, :],
                                     c.sb["bbc"][:], ALU.mult)
                    nc_ = sg
                    me.op("tensor_tensor_scan", c.sb["hbuf"][:], c.sb["dA"][:],
                          c.sb["dBu"][:], c.sb["carry"][:, dc * 16 + n:
                                                        dc * 16 + n + 1],
                          ALU.mult, ALU.add)
                    self.mk[("dAfree", sg, n, dc)] = me.mark()
                    me.tensor_copy(c.sb["carry"][:, dc * 16 + n:dc * 16 + n + 1],
                                   c.sb["hbuf"][:, SEGT - 1:SEGT])
                    me.tensor_tensor(c.sb["hcb"][:], c.sb["hbuf"][:],
                                     c.sb["cbc"][:], ALU.mult)
                    if n == 0:
                        me.tensor_copy(c.sb["y"][:, dc, :], c.sb["hcb"][:])
                    else:
                        me.tensor_tensor(c.sb["y"][:, dc, :], c.sb["y"][:, dc, :],
                                         c.sb["hcb"][:], ALU.add)
                self.mk[("nfree", sg, n)] = me.mark()
            for dc in range(8):
                me.scalar_tensor_tensor(c.sb["y"][:, dc, :], c.sb["ubf"][:, dc, :],
                                        c.sb["dsb"][:, dc:dc + 1],
                                        c.sb["y"][:, dc, :], ALU.mult, ALU.add)
                me.tensor_copy(c.sb["ybf"][:, dc, :], c.sb["y"][:, dc, :])
            me.wait(self.W["gpsimd"], self.mk.get(("zbf", sg), 0))
            self.mk[("ydone", sg)] = me.mark()
            for tl in range(SEGTILES if "E2OUT" in MSTAGES else 0):
                i = sg * SEGTILES + tl
                me.wait(self.W["sync"], self.mk.get(f"ld2_{i}", 0))
                me.wait(self.W["tensor"], self.mk.get(("omm", sg, tl), 0))
                if i >= 2:
                    me.wait(self.W["sync"], self.mk.get(f"st{i-2}", 0))
                    me.wait(self.W["scalar"], self.mk.get(f"sq{i-2}", 0))
                me.op("_custom_dve", AFFINE_THEN_ADD, out=self.xnew(i),
                      in0=c.ptf[:, 512:1536], in1=self.xt(i), s0=1.0, s1=0.0)
                self.mk[f"xn{i}"] = me.mark()
                me.reduce_sum(c.sb["msc"][:, 0:1], self.xnew(i), AX.X)
                me.tensor_scalar(c.sb["stats_m" + self.so][:, i:i + 1],
                                 c.sb["msc"][:, 0:1], 1.0 / DM, None, ALU.mult)
            self.mk[("useg_done", sg)] = me.mark()
        if "E2" in MSTAGES:
            me.wait(self.W["scalar"], self.mk.get(f"sq{NT-1}", 0))
            me.tensor_scalar(c.sb["tmpA"][:, 0:NT],
                             c.sb["stats_sq" + self.so][:, 0:NT],
                             1.0 / DM, None, ALU.mult)
            me.tensor_tensor(c.sb["tmpB"][:, 0:NT],
                             c.sb["stats_m" + self.so][:, 0:NT],
                             c.sb["stats_m" + self.so][:, 0:NT], ALU.mult)
            me.tensor_tensor(c.sb["tmpA"][:, 0:NT], c.sb["tmpA"][:, 0:NT],
                             c.sb["tmpB"][:, 0:NT], ALU.subtract)
            self.mk["var"] = me.mark()

    def emit_tensor(self, E=None):
        c, me = self.ctx, self.W["tensor"]
        me.wait(self.W["gpsimd"], self.mk.get("win", 0))
        for sg in range(NSEG if "E0" in MSTAGES else 0):
            for tl in range(SEGTILES):
                i = sg * SEGTILES + tl
                me.wait(self.W["vector"], self.mk.get(f"e0xn{i}", 0))
                if i > 0:
                    me.wait(self.W["vector"], self.mk.get(f"e0t{i-1}", 0))
                for cc in range(8):
                    me.transpose(c.pgu_bf[:, cc * 128:(cc + 1) * 128],
                                 c.sb["m_xnbf"][:, cc * 128:(cc + 1) * 128],
                                 c.sb["identbf"][:])
                self.mk[f"e0tr{i}"] = me.mark()
            me.wait(self.W["vector"], self.mk.get(f"e0t{sg * SEGTILES + SEGTILES - 1}", 0))
            for dc in range(16):
                if dc >= 2:
                    me.wait(self.W["vector"], self.mk.get(("xrev", sg, dc - 2), 0))
                elif sg > 0:
                    me.wait(self.W["vector"], self.mk.get(("xrev", sg - 1, 14 + dc), 0))
                for cc in range(8):
                    me.matmul(c.pgu[:, (dc % 2) * 512:(dc % 2) * 512 + 512],
                              c.sb["Win_b"][:, cc, dc * 128:(dc + 1) * 128],
                              c.sb["xnTseg"][:, cc, :],
                              start=(cc == 0), stop=(cc == 7))
                self.mk[("xzmm", sg, dc)] = me.mark()
        me.wait(self.W["gpsimd"], self.mk.get("wxp", 0))
        me.wait(self.W["gpsimd"], self.mk.get("wdt", 0))
        me.wait(self.W["gpsimd"], self.mk.get("wout", 0))
        for sg in range(NSEG if ("E2" in MSTAGES and "E2X" in MSTAGES) else 0):
            me.wait(self.W["gpsimd"], self.mk.get(("ubf", sg), 0))
            if sg > 0:
                me.wait(self.W["vector"], self.mk.get(("useg_done", sg - 1), 0))
            for (o0, w, bank) in ((0, 64, 0), (64, 16, 512), (80, 16, 1536)):
                for cc in range(8):
                    me.matmul(c.pgu[0:w, bank:bank + SEGT],
                              c.sb["Wxp_b"][:, cc, o0:o0 + w],
                              c.sb["ubf"][:, cc, :],
                              start=(cc == 0), stop=(cc == 7))
            self.mk[("xdbl", sg)] = me.mark()
            me.wait(self.W["vector"], self.mk.get(("bc_ev", sg), 0))
            for dc in range(8):
                if dc > 0:
                    me.wait(self.W["vector"], self.mk.get(("spd", sg, dc - 1), 0))
                    me.wait(self.W["scalar"], self.mk.get(("spl", sg, dc - 1), 0))
                me.matmul(c.ptf[:, 0:SEGT], c.sb["Wdt_b"][0:64, dc, :],
                          c.sb["dtlow"][0:64, :], start=True, stop=True)
                self.mk[("dtp", sg, dc)] = me.mark()
            me.wait(self.W["vector"], self.mk.get(("ydone", sg), 0))
            me.wait(self.W["scalar"], self.mk.get(("spl", sg, 7), 0))
            me.wait(self.W["vector"], self.mk.get(("spd", sg, 7), 0))
            for tl in range(SEGTILES if "E2OUT" in MSTAGES else 0):
                if not (sg == 0 and tl == 0):
                    pi = sg * SEGTILES + tl - 1
                    me.wait(self.W["vector"], self.mk.get(f"xn{pi}", 0))
                for dcz in range(16):
                    lhs = (c.sb["ybf"] if dcz < 8 else c.sb["zbf"])[
                        :, dcz % 8, tl * 128:(tl + 1) * 128]
                    for half in range(2):
                        me.matmul(c.ptf[:, 512 + half * 512:512 + half * 512 + 512],
                                  lhs, c.sb["Wout_b"][:, dcz,
                                                      half * 512:half * 512 + 512],
                                  start=(dcz == 0), stop=(dcz == 15))
                self.mk[("omm", sg, tl)] = me.mark()




def make_phases(ctx, names):
    out = []
    cur = "0"
    for pn in names:
        nxt = "1" if cur == "0" else "0"
        if pn == "S0":
            out.append(StatsPrepass(ctx))
            continue
        elif pn == "W0":
            out.append(FFNWeightPrep(ctx))
            continue
        elif pn == "A":
            out.append(FFNPhase(ctx, cur, nxt, "ffa"))
        elif pn == "B":
            out.append(AttnPhase(ctx, cur, nxt, "atb"))
        elif pn == "C":
            out.append(FFNPhase(ctx, cur, nxt, "ffc", load_stash=True))
        elif pn == "WM":
            out.append(MoEWeightPrep(ctx))
            continue
        elif pn == "D":
            out.append(MoEPhase(ctx, cur, nxt, "moed"))
        elif pn == "F":
            out.append(MoEPhase(ctx, cur, nxt, "moef"))
        elif pn == "E":
            out.append(MambaPhase(ctx, cur, nxt, "mam"))
        else:
            raise ValueError(pn)
        cur = nxt
    return out


def build_nc(phase_names):
    nc = bass.Bass("TRN2", target_bir_lowering=False)
    es = ExitStack()
    c = Ctx(nc, es)

    c.dram_t("x_in", [S, DM], F32, "ExternalInput")
    c.dram_t("x_state", [S, DM], F32, "ExternalOutput")
    for nm, shp in [("w_gate", [DFF, DM]), ("w_up", [DFF, DM]),
                    ("w_down", [DM, DFF]),
                    ("d_ident", [128, 128]), ("d_cst", [128, 8])]:
        c.dram_t(nm, shp, F32, "ExternalInput")
    for nm in ("WqT", "WkT", "WvT", "WfcT"):
        c.dram_t(nm, [DM, DM], F32, "ExternalInput")
    c.dram_t("keysR", [128, NEXP * 1024], F32, "ExternalInput")
    c.dram_t("valuesR", [128, NEXP * 1024], F32, "ExternalInput")
    c.dram_t("selT", [DM, 64], F32, "ExternalInput")
    c.dram_t("values_st", [128, NEXP * 1024], BF16, "Internal")
    c.dram_t("WinR", [128, 16384], F32, "ExternalInput")
    c.dram_t("WoutR", [128, 16384], F32, "ExternalInput")
    c.dram_t("WxpR", [128, 768], F32, "ExternalInput")
    c.dram_t("WdtR", [64, 1024], F32, "ExternalInput")
    c.dram_t("KxR", [128, 32], F32, "ExternalInput")
    c.dram_t("KzR", [128, 32], F32, "ExternalInput")
    c.dram_t("bxR", [128, 8], F32, "ExternalInput")
    c.dram_t("bzR", [128, 8], F32, "ExternalInput")
    c.dram_t("AlogR", [128, 128], F32, "ExternalInput")
    c.dram_t("DR", [128, 8], F32, "ExternalInput")
    c.dram_t("bdtR", [128, 8], F32, "ExternalInput")
    c.dram_t("xraw_st", [128, 16 * S], F32, "Internal")
    c.dram_t("u_st", [128, 8 * S], BF16, "Internal")
    c.dram_t("z_st", [128, 8 * S], BF16, "Internal")
    c.dram_t("bc_st", [16, SEGT], F32, "Internal")
    c.dram_t("cc_st", [16, SEGT], F32, "Internal")
    c.dram_t("WgqT_st", [128, 8 * DFF], BF16, "Internal")
    c.dram_t("WuqT_st", [128, 8 * DFF], BF16, "Internal")
    c.dram_t("WdqT_st", [128, 22 * DM], BF16, "Internal")

    c.sbuf("cst", [128, 8], F32)
    c.sbuf("identbf", [128, 128], BF16)
    c.sbuf("onesbf", [128, 128], BF16)
    c.sbuf("xt", [128, 2 * DM], F32)
    c.sbuf("xnew", [128, 2 * DM], F32)
    c.sbuf("msc", [128, 4], F32)
    c.sbuf("sqscr", [128, DM], F32)
    c.sbuf("tmpA", [128, NT], F32)
    c.sbuf("tmpB", [128, NT], F32)
    c.sbuf("mk2", [128, 16], F32)
    for t in ("0", "1"):
        c.sbuf("stats_m" + t, [128, NT], F32)
        c.sbuf("stats_sq" + t, [128, NT], F32)
        c.sbuf("stats_r" + t, [128, NT], F32)
    arena = c.sbuf("arena", [128, 186 * 1024], mybir.dt.uint8)
    arv = arena[:]

    def carve(name, off, shape, dtype):
        n = int(np.prod(shape[1:]))
        ap = arv[:, off:off + n * mybir.dt.size(dtype)].bitcast(dtype)
        if len(shape) == 3:
            ap = ap.rearrange("p (a b) -> p a b", b=shape[2])
        elif len(shape) == 4:
            ap = ap.rearrange("p (a b cd) -> p a b cd", b=shape[2], cd=shape[3])
        c.sb[name] = ap
        return ap

    carve("Wg", 0, [128, 8 * DFF], BF16)
    carve("Wu", 45056, [128, 8 * DFF], BF16)
    carve("Wd", 90112, [128, 22 * DM], BF16)
    carve("wtile", 135168, [128, DFF], F32)
    carve("xqbf", 146432, [128, DM], BF16)
    carve("xqT", 148480, [128, DM], BF16)
    carve("hqbf", 150528, [128, DFF], BF16)
    carve("hqT", 156160, [128, DFF], BF16)
    carve("sgate", 161792, [128, 512], F32)
    carve("hpf", 163840, [128, 512], F32)
    carve("xc", 165888, [128, DM], F32)
    carve("q_a", 169984, [128, DM], F32)
    carve("q_xn", 174080, [128, DM], F32)
    carve("q_g2", 178176, [128, DM], F32)
    carve("q_sg", 182272, [128, DM], F32)
    carve("q_bm", 186368, [128, 64], F32)
    carve("q_scl", 186624, [128, 64], F32)
    carve("q_rec", 186880, [128, 64], F32)
    carve("q_sclF", 187136, [128, 64], F32)
    carve("Wat", 0, [128, 4 * 8 * 1024], BF16)
    carve("xnbf", 65536, [128, DM], BF16)
    carve("xnT", 67584, [128, DM], BF16)
    carve("qbf", 69632, [128, DM], BF16)
    carve("kbf", 71680, [128, DM], BF16)
    carve("vbf", 73728, [128, DM], BF16)
    carve("abf", 75776, [128, DM], BF16)
    carve("aT", 77824, [128, DM], BF16)
    carve("att_m4", 79872, [128, 4096], BF16)
    carve("att_sc", 88064, [128, 256], F32)
    carve("att_E", 89088, [128, 256], F32)
    carve("att_A", 90112, [128, 256], BF16)
    carve("att_mx", 90624, [128, 16], F32)
    carve("att_sm", 90688, [128, 16], F32)
    carve("att_rc", 90752, [128, 16], F32)
    carve("aof", 91136, [128, DM], F32)

    carve("keysb", 0, [128, NEXP * 1024], BF16)
    carve("vbuf", 131072, [128, 2 * DM], BF16)
    carve("vstg", 135168, [128, 2 * DM], BF16)
    carve("selTb", 139264, [128, 8, 64], BF16)
    carve("xnTs", 140288, [128, 8, 384], BF16)
    carve("selsb", 146432, [128, 64], F32)
    carve("selw", 146688, [128, 64], F32)
    carve("selm", 146944, [128, 64], F32)
    carve("gates", 147200, [128, 192], F32)
    carve("scb", 147968, [128, 384], BF16)
    carve("scT", 148736, [128, 128], BF16)
    carve("m_xnbf", 148992, [128, DM], BF16)
    carve("m_xt", 151040, [128, 4, DM], F32)

    carve("Win_b", 0, [128, 8, 2048], BF16)
    carve("Wout_b", 32768, [128, 16, 1024], BF16)
    carve("Wxp_b", 65536, [128, 8, 96], BF16)
    carve("Wdt_b", 67072, [128, 8, 128], BF16)
    carve("kx", 69120, [128, 8, 4], F32)
    carve("kz", 69248, [128, 8, 4], F32)
    carve("bx", 69376, [128, 8], F32)
    carve("bz", 69440, [128, 8], F32)
    carve("alog", 69504, [128, 8, 16], F32)
    carve("asb", 70016, [128, 8, 16], F32)
    carve("dsb", 70528, [128, 8], F32)
    carve("bdt", 70560, [128, 8], F32)
    carve("carry", 70592, [128, 128], F32)
    carve("bbc", 71104, [128, SEGT], F32)
    carve("cbc", 73152, [128, SEGT], F32)
    carve("dtlow", 75200, [128, SEGT], BF16)
    carve("Bsb", 76224, [128, SEGT], F32)
    carve("Csb", 78272, [128, SEGT], F32)
    carve("spa", 80320, [128, SEGT], F32)
    carve("spr", 82368, [128, SEGT], F32)
    carve("dA", 84416, [128, SEGT], F32)
    carve("dBu", 86464, [128, SEGT], F32)
    carve("hbuf", 88512, [128, SEGT], F32)
    carve("hcb", 90560, [128, SEGT], F32)
    carve("ubf", 92608, [128, 8, SEGT], BF16)
    carve("zbf", 100800, [128, 8, SEGT], BF16)
    carve("ybf", 108992, [128, 8, SEGT], BF16)
    carve("delta", 117184, [128, 8, SEGT], F32)
    carve("du", 133568, [128, 8, SEGT], F32)
    carve("y", 149952, [128, 8, SEGT], F32)
    carve("e_xt", 166336, [128, 4, DM], F32)
    carve("xnTseg", 92608, [128, 8, SEGT], BF16)
    carve("xrstage", 100800, [128, 2 * SEGT], F32)
    carve("ct1", 80320, [128, S], F32)
    carve("ct2", 96704, [128, S], F32)
    carve("cbuf", 113088, [128, 4112], F32)
    carve("cgout", 129536, [128, S], BF16)

    pt = c.psum("PT", [128, 1536], F32)
    c.pt_bf = pt[:].bitcast(BF16)
    c.ptf = pt[:]
    pgu = c.psum("PGU", [128, 2048], F32)
    c.pgu = pgu[:]
    c.pgu_bf = pgu[:].bitcast(BF16)

    cstv = [0.0, 1.0, 0.5, -0.5, -1.0, 0.125, EPS, 1.75]
    for k, v in enumerate(cstv):
        nc.const_aps.register(F32, v, c.sb["cst"][:, k:k + 1])

    c.EW = {}
    for n in ("sync", "scalar", "vector", "tensor", "gpsimd"):
        dsems = [c.sem(f"d_{n}_{s}") for s in range(NSLOT)] \
            if n in ("sync", "gpsimd") else []
        c.EW[n] = W(n, c.sem(f"c_{n}"), dsems,
                    serial=(n in ("vector", "scalar", "gpsimd")))
    phases = make_phases(c, phase_names)
    PB = c.sem("phase_barrier")
    cst_sem = c.sem("cst_load")
    id_sem = c.sem("ident_load")

    counts = {}
    for ph in phases:
        ph.emit_sync()
        ph.emit_scalar()
        ph.emit_vector()
        ph.emit_tensor()
        ph.emit_gpsimd()
        for n in ph.W:
            counts[(ph.tag, n)] = (ph.W[n].n, ph.W[n].dn)
    for w in c.EW.values():
        w.reset()

    def run_phases(eng, name, emit_name):
        wrap = c.EW[name].bind(eng)
        for k, ph in enumerate(phases):
            eng.wait_ge(PB, 5 * k)
            getattr(ph, emit_name)()
            assert (wrap.n, wrap.dn) == counts[(ph.tag, name)], (
                ph.tag, name, wrap.n, wrap.dn, counts[(ph.tag, name)])
            wrap.wait_self_all()
            eng.sem_inc(PB, 1)
        wrap.unbind()

    with nc.Block() as block:
        @block.sync
        def _(eng):
            eng.dma_start(out=c.sb["cst"][:], in_=c.dram["d_cst"]).then_inc(cst_sem, 16)
            eng.wait_ge(cst_sem, 16)
            eng.wait_ge(id_sem, 16)
            run_phases(eng, "sync", "emit_sync")

        @block.scalar
        def _(eng):
            eng.wait_ge(cst_sem, 16)
            eng.wait_ge(id_sem, 16)
            run_phases(eng, "scalar", "emit_scalar")

        @block.vector
        def _(eng):
            eng.wait_ge(cst_sem, 16)
            eng.wait_ge(id_sem, 16)
            run_phases(eng, "vector", "emit_vector")

        @block.tensor
        def _(eng):
            eng.wait_ge(cst_sem, 16)
            eng.wait_ge(id_sem, 16)
            run_phases(eng, "tensor", "emit_tensor")

        @block.gpsimd
        def _(eng):
            eng.dma_start(out=c.sb["identbf"][:], in_=c.dram["d_ident"]).then_inc(id_sem, 16)
            eng.wait_ge(cst_sem, 16)
            run_phases(eng, "gpsimd", "emit_gpsimd")

    mybir.codegen_inst_isa_subclasses(nc)
    return nc, es


def prep_host_ffn(inputs):
    f32 = np.float32
    x = np.ascontiguousarray(inputs["L0_attn_x"], dtype=f32)
    shared = {
        "w_gate": np.ascontiguousarray(inputs["L0_ffn_w_gate"], f32),
        "w_up": np.ascontiguousarray(inputs["L0_ffn_w_up"], f32),
        "w_down": np.ascontiguousarray(inputs["L0_ffn_w_down"], f32),
        "WqT": np.ascontiguousarray(inputs["L0_attn_Wq"].T, f32),
        "WkT": np.ascontiguousarray(inputs["L0_attn_Wk"].T, f32),
        "WvT": np.ascontiguousarray(inputs["L0_attn_Wv"].T, f32),
        "WfcT": np.ascontiguousarray(inputs["L0_attn_Wfc"].T, f32),
        "keysR": np.ascontiguousarray(
            inputs["L1_ffn_keys"].reshape(NEXP, 8, 128, 128).transpose(2, 0, 1, 3)
            .reshape(128, NEXP * 1024), f32),
        "valuesR": np.ascontiguousarray(
            inputs["L1_ffn_values"].transpose(1, 0, 2).reshape(128, NEXP * 1024), f32),
        "selT": np.ascontiguousarray(inputs["L1_ffn_expert_sel"].T, f32),
        "WinR": np.ascontiguousarray(
            inputs["L1_attn_W_in"].reshape(8, 128, 2048).transpose(1, 0, 2)
            .reshape(128, 16384), f32),
        "WoutR": np.ascontiguousarray(
            inputs["L1_attn_W_out"].reshape(16, 128, 1024).transpose(1, 0, 2)
            .reshape(128, 16384), f32),
        "WxpR": np.ascontiguousarray(
            inputs["L1_attn_W_xproj"].reshape(8, 128, 96).transpose(1, 0, 2)
            .reshape(128, 768), f32),
        "WdtR": np.ascontiguousarray(inputs["L1_attn_W_dt"], f32),
        "KxR": np.ascontiguousarray(
            inputs["L1_attn_K_x"].reshape(8, 128, 4).transpose(1, 0, 2)
            .reshape(128, 32), f32),
        "KzR": np.ascontiguousarray(
            inputs["L1_attn_K_z"].reshape(8, 128, 4).transpose(1, 0, 2)
            .reshape(128, 32), f32),
        "bxR": np.ascontiguousarray(
            inputs["L1_attn_b_x"].reshape(8, 128).T, f32),
        "bzR": np.ascontiguousarray(
            inputs["L1_attn_b_z"].reshape(8, 128).T, f32),
        "AlogR": np.ascontiguousarray(
            inputs["L1_attn_A_log"].reshape(8, 128, 16).transpose(1, 0, 2)
            .reshape(128, 128), f32),
        "DR": np.ascontiguousarray(
            inputs["L1_attn_D"].reshape(8, 128).T, f32),
        "bdtR": np.ascontiguousarray(
            inputs["L1_attn_b_dt"].reshape(8, 128).T, f32),
        "d_ident": np.eye(128, dtype=f32),
        "d_cst": np.tile(np.array([0.0, 1.0, 0.5, -0.5, -1.0, 0.125, EPS, 1.75],
                                  f32), (128, 1)),
    }
    in_maps = []
    for cix in range(x.shape[0]):
        m = dict(shared)
        m["x_in"] = np.ascontiguousarray(x[cix])
        in_maps.append(m)
    return in_maps


def run_ffn_test(inputs, ncores=8):
    in_maps = prep_host_ffn(inputs)[:ncores]
    nc, es = build_nc(["S0", "W0", "A"])
    try:
        res = run_bass_kernel_spmd(nc, in_maps, core_ids=list(range(ncores)))
    finally:
        es.close()
    return np.stack([r["x_state"] for r in res.results])


def run_attn_test(inputs, x_prev, ncores=8):
    in_maps = prep_host_ffn(inputs)[:ncores]
    for cix in range(ncores):
        in_maps[cix]["x_in"] = np.ascontiguousarray(x_prev[cix].astype(np.float32))
    nc, es = build_nc(["S0", "B"])
    try:
        res = run_bass_kernel_spmd(nc, in_maps, core_ids=list(range(ncores)))
    finally:
        es.close()
    return np.stack([r["x_state"] for r in res.results])


FULL_CHAIN = ["S0", "W0", "WM", "A", "B", "C", "D", "E", "F"]


def kernel(**inputs):
    in_maps = prep_host_ffn(inputs)
    nc, es = build_nc(FULL_CHAIN)
    try:
        res = run_bass_kernel_spmd(nc, in_maps, core_ids=list(range(8)))
    finally:
        es.close()
    return np.stack([r["x_state"] for r in res.results]).astype(np.float32)


# version 04
# speedup vs baseline: 1.4872x; 1.4872x over previous
import numpy as np
import ml_dtypes
from contextlib import ExitStack

import concourse.bass as bass
import concourse.mybir as mybir
from concourse.bass_utils import run_bass_kernel_spmd
from concourse.dve_op import DveOp
from concourse.dve_spec import Spec, Src0, Src1, C0, C1, C2, C3, lower as dve_lower
from concourse.dve_table_gen import dve_ver_for
import concourse.dve_ops as dve_ops_mod
from concourse.dve_ops import AFFINE_THEN_ADD

F32 = mybir.dt.float32
BF16 = mybir.dt.bfloat16
I32 = mybir.dt.int32
AF = mybir.ActivationFunctionType
ALU = mybir.AluOpType
AX = mybir.AxisListType

_VER = dve_ver_for("TRN2")
DEBUG_ATTN = False


def _reg_op(name, spec, reference):
    for op in dve_ops_mod.OPS:
        if op.name == name:
            return op
    sha = dve_lower(spec, ver=_VER, perf_en=False, name=name).sha(_VER)
    op = DveOp(name, spec, False, {_VER: sha}, reference=reference)
    dve_ops_mod.OPS.append(op)
    dve_ops_mod._SUB_OPCODE_FOR_NAME[name] = (
        dve_ops_mod._CUSTOM_DVE_ROW_BASE + len(dve_ops_mod.OPS) - 1)
    dve_ops_mod.CUSTOM_DVE_REFS[name] = op.ref
    return op


SNAP1 = _reg_op(
    "ANT_SNAP1",
    Spec(body=((Src0 > C0) + (Src0 > C1)) + ((Src0 > C2) + (Src0 > C3))),
    lambda in0, in1, s0, s1, imm2: (
        (in0 > s0).astype(np.float64) + (in0 > s1) + (in0 > imm2) + (in0 > in1)),
)


def _snap2_spec():
    c5 = Src0 > C0
    c6 = Src0 > C1
    n7 = Src0 > C2
    t3 = (c5 + c6) + (n7 + n7)
    return Spec(body=Src1 + (t3 + t3))


SNAP2 = _reg_op(
    "ANT_SNAP2",
    _snap2_spec(),
    lambda in0, in1, s0, s1, imm2: in1 + 2.0 * (
        (in0 > s0).astype(np.float64) + (in0 > s1) + 2.0 * (in0 > imm2)),
)

S = 4096
DM = 1024
NT = S // 128
DFF = 2816
EPS = 1e-5
OCS = [(0, 512), (512, 512), (1024, 512), (1536, 512), (2048, 512), (2560, 256)]
NOC = len(OCS)


def bc16(small_ap, nb, k=16):
    return small_ap.unsqueeze(2).broadcast_to([128, nb, k])


def r3(ap, k=16):
    return ap.rearrange("p (nb k) -> p nb k", k=k)


_DMA_METHODS = frozenset({"dma_start", "dma_start_transpose", "indirect_dma_start"})


NSLOT = 6


class W:

    def __init__(self, name, sem, dsems, serial):
        self.name = name
        self.sem = sem
        self.dsems = dsems
        self.serial = serial
        self.eng = None
        self.n = 0
        self.dn = 0

    def bind(self, eng):
        self.eng = eng
        return self

    def reset(self):
        self.eng = None
        self.n = 0
        self.dn = 0

    def unbind(self):
        self.eng = None

    def op(self, meth, *a, **k):
        if meth in _DMA_METHODS:
            slot = self.dn % NSLOT
            if self.eng is not None:
                prev = self.dn // NSLOT
                if prev > 0:
                    self.eng.wait_ge(self.dsems[slot], 16 * prev)
                inst = getattr(self.eng, meth)(*a, **k)
                inst.then_inc(self.dsems[slot], 16)
            self.dn += 1
            return ("D", self.dn)
        if self.eng is not None:
            if self.serial and self.n > 0:
                self.eng.wait_ge(self.sem, self.n)
            inst = getattr(self.eng, meth)(*a, **k)
            inst.then_inc(self.sem, 1)
        self.n += 1
        return self.n

    def __getattr__(self, meth):
        if meth.startswith("_ant"):
            raise AttributeError(meth)
        return lambda *a, **k: self.op(meth, *a, **k)

    def wait(self, other, val):
        if self.eng is None:
            return
        if isinstance(val, tuple):
            m = val[1]
            if m <= 0:
                return
            self.eng.wait_ge(other.dsems[(m - 1) % NSLOT],
                             16 * ((m - 1) // NSLOT + 1))
            return
        if val > 0:
            if other is self and self.serial:
                return
            self.eng.wait_ge(other.sem, val)

    def wait_dma_upto(self, other, dn):
        if self.eng is None or not dn:
            return
        for s in range(NSLOT):
            cnt = (dn - s + NSLOT - 1) // NSLOT
            if cnt > 0:
                self.eng.wait_ge(other.dsems[s], 16 * cnt)

    def wait_self_all(self):
        if self.eng is None:
            return
        if self.n > 0:
            self.eng.wait_ge(self.sem, self.n)
        for s in range(NSLOT):
            cnt = (self.dn - s + NSLOT - 1) // NSLOT
            if cnt > 0:
                self.eng.wait_ge(self.dsems[s], 16 * cnt)

    def mark(self):
        return self.n


class Ctx:
    def __init__(self, nc, es):
        self.nc = nc
        self.es = es
        self.sb = {}
        self.dram = {}

    def sbuf(self, name, shape, dtype):
        t = self.es.enter_context(self.nc.sbuf_tensor(name, shape, dtype))
        self.sb[name] = t
        return t

    def psum(self, name, shape, dtype=F32):
        return self.es.enter_context(self.nc.psum_tensor(name, shape, dtype))

    def sem(self, name):
        return self.es.enter_context(self.nc.semaphore(name))

    def dram_t(self, name, shape, dtype, kind):
        ap = self.nc.dram_tensor(name, shape, dtype, kind=kind).ap()
        self.dram[name] = ap
        return ap


class QuantHelper:

    def __init__(self, ctx):
        c = ctx
        self.a = c.sb["q_a"]
        self.xn = c.sb["q_xn"]
        self.g2 = c.sb["q_g2"]
        self.sg = c.sb["q_sg"]
        self.bm = c.sb["q_bm"]
        self.scl = c.sb["q_scl"]
        self.rec = c.sb["q_rec"]
        self.sclF = c.sb["q_sclF"]
        self.c175 = c.sb["cst"][:, 7:8]

    def v_part1(self, v, src, N, rstd_ap=None):
        nb = N // 16
        v.tensor_reduce(self.bm[:, :nb], r3(src), AX.X, ALU.max,
                        apply_absolute_value=True)
        v.tensor_scalar(self.scl[:, :nb], self.bm[:, :nb], 1.0 / 6.0, 1e-12,
                        ALU.mult, ALU.max)
        v.reciprocal(self.rec[:, :nb], self.scl[:, :nb])
        if rstd_ap is not None:
            v.tensor_scalar(self.sclF[:, :nb], self.scl[:, :nb], rstd_ap, 0.5,
                            ALU.mult, ALU.mult)
        else:
            v.tensor_scalar(self.sclF[:, :nb], self.scl[:, :nb], 0.5, None,
                            ALU.mult)

    def s_part(self, sc, src, N):
        sc.activation(self.a[:, :N], src, AF.Abs)
        sc.activation(self.sg[:, :N], src, AF.Sign)

    def v_part2(self, v, dst, N):
        nb = N // 16
        v.tensor_tensor(r3(self.xn[:, :N]), r3(self.a[:, :N]),
                        bc16(self.rec[:, :nb], nb), ALU.mult)
        v.op("_custom_dve", SNAP1, out=self.a[:, :N], in0=self.xn[:, :N],
             in1=self.c175, s0=0.25, s1=0.75, imm2=1.25)
        v.op("_custom_dve", SNAP2, out=self.g2[:, :N], in0=self.xn[:, :N],
             in1=self.a[:, :N], s0=2.5, s1=3.5, imm2=5.0)
        v.tensor_tensor(r3(self.a[:, :N]), r3(self.g2[:, :N]),
                        bc16(self.sclF[:, :nb], nb), ALU.mult)
        v.tensor_tensor(dst, self.a[:, :N], self.sg[:, :N], ALU.mult)


class Phase:
    def __init__(self, ctx, tag):
        self.ctx = ctx
        self.tag = tag
        self.mk = {}
        self.W = ctx.EW

    def emit_sync(self): pass
    def emit_scalar(self): pass
    def emit_vector(self): pass
    def emit_tensor(self): pass
    def emit_gpsimd(self): pass


class StatsPrepass(Phase):
    def __init__(self, ctx):
        super().__init__(ctx, "s0")

    def emit_sync(self):
        c, me = self.ctx, self.W["sync"]
        me.dma_start(out=c.dram["x_state"], in_=c.dram["x_in"])
        xi = c.dram["x_in"].rearrange("(t p) d -> t p d", p=128)
        for i in range(NT):
            if i >= 2:
                me.wait(self.W["vector"], self.mk.get(f"v{i-2}", 0))
                me.wait(self.W["scalar"], self.mk.get(f"s{i-2}", 0))
            self.mk[f"ld{i}"] = me.dma_start(
                out=c.sb["xt"][:, (i % 2) * DM:(i % 2) * DM + DM], in_=xi[i])
        me.wait_self_all()

    def emit_vector(self):
        c, me = self.ctx, self.W["vector"]
        for i in range(NT):
            me.wait(self.W["sync"], self.mk.get(f"ld{i}", 0))
            xt = c.sb["xt"][:, (i % 2) * DM:(i % 2) * DM + DM]
            me.reduce_sum(c.sb["msc"][:, 0:1], xt, AX.X)
            me.tensor_scalar(c.sb["stats_m0"][:, i:i + 1], c.sb["msc"][:, 0:1],
                             1.0 / DM, None, ALU.mult)
            self.mk[f"v{i}"] = me.mark()
        me.wait(self.W["scalar"], self.mk.get(f"s{NT-1}", 0))
        me.tensor_scalar(c.sb["tmpA"][:, 0:NT], c.sb["stats_sq0"][:, 0:NT],
                         1.0 / DM, None, ALU.mult)
        me.tensor_tensor(c.sb["tmpB"][:, 0:NT], c.sb["stats_m0"][:, 0:NT],
                         c.sb["stats_m0"][:, 0:NT], ALU.mult)
        me.tensor_tensor(c.sb["tmpA"][:, 0:NT], c.sb["tmpA"][:, 0:NT],
                         c.sb["tmpB"][:, 0:NT], ALU.subtract)
        self.mk["var"] = me.mark()

    def emit_scalar(self):
        c, me = self.ctx, self.W["scalar"]
        for i in range(NT):
            me.wait(self.W["sync"], self.mk.get(f"ld{i}", 0))
            xt = c.sb["xt"][:, (i % 2) * DM:(i % 2) * DM + DM]
            me.activation(c.sb["sqscr"][:, 0:DM], xt, AF.Square,
                          accum_out=c.sb["stats_sq0"][:, i:i + 1])
            self.mk[f"s{i}"] = me.mark()
        me.wait(self.W["vector"], self.mk.get("var", 0))
        me.activation(c.sb["tmpB"][:, 0:NT], c.sb["tmpA"][:, 0:NT], AF.Ln,
                      bias=EPS)
        me.activation(c.sb["stats_r0"][:, 0:NT], c.sb["tmpB"][:, 0:NT],
                      AF.Exp, scale=-0.5)


class FFNWeightPrep(Phase):
    def __init__(self, ctx):
        super().__init__(ctx, "w0")
        self.specs = [
            ("w_gate", DM, "Wg"),
            ("w_up", DM, "Wu"),
            ("w_down", DFF, "Wd"),
        ]
        self.tiles = []
        for k, (nm, cols, dst) in enumerate(self.specs):
            rows = DFF if cols == DM else DM
            for i in range(rows // 128):
                self.tiles.append((k, i))
        self.q = QuantHelper(ctx)

    @staticmethod
    def chunks(cols):
        out, o = [], 0
        while o < cols:
            wd = min(1024, cols - o)
            out.append((o, wd))
            o += wd
        return out

    def emit_sync(self):
        c, me = self.ctx, self.W["sync"]
        for n, (k, i) in enumerate(self.tiles):
            nm, cols, dst = self.specs[k]
            src = c.dram[nm].rearrange("(t p) d -> t p d", p=128)
            if n >= 1:
                me.wait(self.W["vector"], self.mk.get(f"vq{n-1}", 0))
                me.wait(self.W["scalar"], self.mk.get(f"sa{n-1}", 0))
            self.mk[f"ld{n}"] = me.dma_start(out=c.sb["wtile"][:, 0:cols],
                                             in_=src[i])
        me.wait(self.W["vector"], self.mk.get("copydone", 0))
        for dst, hbm in (("Wg", "WgqT_st"), ("Wu", "WuqT_st"), ("Wd", "WdqT_st")):
            me.dma_start(out=c.dram[hbm], in_=c.sb[dst][:])
        me.wait_self_all()

    def emit_scalar(self):
        c, me = self.ctx, self.W["scalar"]
        for n, (k, i) in enumerate(self.tiles):
            nm, cols, dst = self.specs[k]
            me.wait(self.W["sync"], self.mk.get(f"ld{n}", 0))
            for ci, (o0, wd) in enumerate(self.chunks(cols)):
                src = c.sb["wtile"][:, o0:o0 + wd]
                me.wait(self.W["vector"], self.mk.get(f"vq{n}_{ci-1}", 0))
                self.q.s_part(me, src, wd)
                self.mk[f"sa{n}_{ci}"] = me.mark()
            self.mk[f"sa{n}"] = me.mark()

    def emit_vector(self):
        c, me = self.ctx, self.W["vector"]
        for n, (k, i) in enumerate(self.tiles):
            nm, cols, dst = self.specs[k]
            me.wait(self.W["sync"], self.mk.get(f"ld{n}", 0))
            for ci, (o0, wd) in enumerate(self.chunks(cols)):
                src = c.sb["wtile"][:, o0:o0 + wd]
                self.q.v_part1(me, src, wd)
                me.wait(self.W["scalar"], self.mk.get(f"sa{n}_{ci}", 0))
                self.q.v_part2(me, c.sb["hqbf"][:, o0:o0 + wd], wd)
                self.mk[f"vq{n}_{ci}"] = me.mark()
            self.mk[f"vq{n}"] = me.mark()
            me.wait(self.W["tensor"], self.mk.get(f"tr{n}", 0))
            ncc = cols // 128
            dstv = c.sb[dst][:].rearrange("p (cchunk m) -> p cchunk m", cchunk=ncc)
            me.tensor_copy(dstv[:, :, i * 128:(i + 1) * 128],
                           c.pt_bf[:, 0:ncc * 128].rearrange(
                               "p (cc m) -> p cc m", cc=ncc))
            self.mk[f"cp{n}"] = me.mark()
        self.mk["copydone"] = me.mark()

    def emit_tensor(self):
        c, me = self.ctx, self.W["tensor"]
        for n, (k, i) in enumerate(self.tiles):
            nm, cols, dst = self.specs[k]
            me.wait(self.W["vector"], self.mk.get(f"vq{n}", 0))
            if n > 0:
                me.wait(self.W["vector"], self.mk.get(f"cp{n-1}", 0))
            for cc in range(cols // 128):
                me.transpose(c.pt_bf[:, cc * 128:(cc + 1) * 128],
                             c.sb["hqbf"][:, cc * 128:(cc + 1) * 128],
                             c.sb["identbf"][:])
            self.mk[f"tr{n}"] = me.mark()


class FFNPhase(Phase):
    def __init__(self, ctx, si, so, tag, load_stash=False):
        super().__init__(ctx, tag)
        self.si = si
        self.so = so
        self.load_stash = load_stash
        self.q = QuantHelper(ctx)

    def xt(self, i):
        return self.ctx.sb["xt"][:, (i % 2) * DM:(i % 2) * DM + DM]

    def xnew(self, i):
        return self.ctx.sb["xnew"][:, (i % 2) * DM:(i % 2) * DM + DM]

    def emit_sync(self):
        c, me = self.ctx, self.W["sync"]
        if self.load_stash:
            for dst, hbm in (("Wg", "WgqT_st"), ("Wu", "WuqT_st"), ("Wd", "WdqT_st")):
                self.mk["wl_" + dst] = me.dma_start(out=c.sb[dst][:],
                                                    in_=c.dram[hbm])
        xs = c.dram["x_state"].rearrange("(t p) d -> t p d", p=128)
        for i in range(2):
            self.mk[f"ld{i}"] = me.dma_start(out=self.xt(i), in_=xs[i])
        for i in range(NT):
            me.wait(self.W["vector"], self.mk.get(f"xn{i}", 0))
            self.mk[f"st{i}"] = me.dma_start(out=xs[i], in_=self.xnew(i))
            if i + 2 < NT:
                self.mk[f"ld{i+2}"] = me.dma_start(out=self.xt(i + 2),
                                                   in_=xs[i + 2])
        me.wait_self_all()

    def emit_scalar(self):
        c, me = self.ctx, self.W["scalar"]
        if self.load_stash:
            for dst in ("Wg", "Wu", "Wd"):
                me.wait(self.W["sync"], self.mk.get("wl_" + dst, 0))
        for i in range(NT):
            me.wait(self.W["vector"], self.mk.get(f"xc{i}", 0))
            self.q.s_part(me, c.sb["xc"][:], DM)
            self.mk[f"abs{i}"] = me.mark()
            for oc, (o0, ow) in enumerate(OCS):
                me.wait(self.W["tensor"], self.mk.get(f"mm{i}_{oc}", 0))
                hg = c.pgu[:, (oc % 2) * 1024:(oc % 2) * 1024 + ow]
                me.activation(c.sb["sgate"][:, 0:ow], hg, AF.Silu)
                self.mk[f"sil{i}_{oc}"] = me.mark()
                me.wait(self.W["vector"], self.mk.get(f"hp{i}_{oc}", 0))
                self.q.s_part(me, c.sb["hpf"][:, 0:ow], ow)
                self.mk[f"habs{i}_{oc}"] = me.mark()
            me.wait(self.W["vector"], self.mk.get(f"xn{i}", 0))
            me.activation(c.sb["sqscr"][:, 0:DM], self.xnew(i), AF.Square,
                          accum_out=c.sb["stats_sq" + self.so][:, i:i + 1])
            self.mk[f"sq{i}"] = me.mark()
        me.wait(self.W["vector"], self.mk.get("var", 0))
        me.activation(c.sb["tmpB"][:, 0:NT], c.sb["tmpA"][:, 0:NT], AF.Ln,
                      bias=EPS)
        me.activation(c.sb["stats_r" + self.so][:, 0:NT], c.sb["tmpB"][:, 0:NT],
                      AF.Exp, scale=-0.5)

    def emit_vector(self):
        c, me = self.ctx, self.W["vector"]
        if self.load_stash:
            for dst in ("Wg", "Wu", "Wd"):
                me.wait(self.W["sync"], self.mk.get("wl_" + dst, 0))
        for i in range(NT):
            me.wait(self.W["sync"], self.mk.get(f"ld{i}", 0))
            me.tensor_scalar(c.sb["xc"][:], self.xt(i),
                             c.sb["stats_m" + self.si][:, i:i + 1], None,
                             ALU.subtract)
            self.mk[f"xc{i}"] = me.mark()
            self.q.v_part1(me, c.sb["xc"][:], DM,
                           rstd_ap=c.sb["stats_r" + self.si][:, i:i + 1])
            me.wait(self.W["scalar"], self.mk.get(f"abs{i}", 0))
            self.q.v_part2(me, c.sb["xqbf"][:], DM)
            self.mk[f"xq{i}"] = me.mark()
            me.wait(self.W["tensor"], self.mk.get(f"tr{i}", 0))
            me.tensor_copy(c.sb["xqT"][:], c.pt_bf[:, 0:1024])
            self.mk[f"xqt{i}"] = me.mark()
            for oc, (o0, ow) in enumerate(OCS):
                me.wait(self.W["scalar"], self.mk.get(f"sil{i}_{oc}", 0))
                hu = c.pgu[:, (oc % 2) * 1024 + 512:(oc % 2) * 1024 + 512 + ow]
                me.tensor_tensor(c.sb["hpf"][:, 0:ow], c.sb["sgate"][:, 0:ow],
                                 hu, ALU.mult)
                self.mk[f"hp{i}_{oc}"] = me.mark()
                self.q.v_part1(me, c.sb["hpf"][:, 0:ow], ow)
                me.wait(self.W["scalar"], self.mk.get(f"habs{i}_{oc}", 0))
                self.q.v_part2(me, c.sb["hqbf"][:, o0:o0 + ow], ow)
                self.mk[f"hq{i}_{oc}"] = me.mark()
            for oc, (o0, ow) in enumerate(OCS):
                me.wait(self.W["tensor"], self.mk.get(f"tr2{i}_{oc}", 0))
                me.tensor_copy(c.sb["hqT"][:, o0:o0 + ow],
                               c.pt_bf[:, 1024:1024 + ow])
                self.mk[f"hqt{i}_{oc}"] = me.mark()
            me.wait(self.W["tensor"], self.mk.get(f"down{i}", 0))
            if i >= 2:
                me.wait(self.W["sync"], self.mk.get(f"st{i-2}", 0))
            me.op("_custom_dve", AFFINE_THEN_ADD, out=self.xnew(i),
                  in0=c.pgu[:, 0:1024], in1=self.xt(i), s0=0.5, s1=0.0)
            self.mk[f"xn{i}"] = me.mark()
            me.reduce_sum(c.sb["msc"][:, 0:1], self.xnew(i), AX.X)
            me.tensor_scalar(c.sb["stats_m" + self.so][:, i:i + 1],
                             c.sb["msc"][:, 0:1], 1.0 / DM, None, ALU.mult)
        me.wait(self.W["scalar"], self.mk.get(f"sq{NT-1}", 0))
        me.tensor_scalar(c.sb["tmpA"][:, 0:NT],
                         c.sb["stats_sq" + self.so][:, 0:NT],
                         1.0 / DM, None, ALU.mult)
        me.tensor_tensor(c.sb["tmpB"][:, 0:NT],
                         c.sb["stats_m" + self.so][:, 0:NT],
                         c.sb["stats_m" + self.so][:, 0:NT], ALU.mult)
        me.tensor_tensor(c.sb["tmpA"][:, 0:NT], c.sb["tmpA"][:, 0:NT],
                         c.sb["tmpB"][:, 0:NT], ALU.subtract)
        self.mk["var"] = me.mark()

    def emit_tensor(self):
        c, me = self.ctx, self.W["tensor"]
        if self.load_stash:
            for dst in ("Wg", "Wu", "Wd"):
                me.wait(self.W["sync"], self.mk.get("wl_" + dst, 0))
        for i in range(NT):
            me.wait(self.W["vector"], self.mk.get(f"xq{i}", 0))
            if i > 0:
                me.wait(self.W["vector"], self.mk.get(f"xqt{i-1}", 0))
            for cc in range(8):
                me.transpose(c.pt_bf[:, cc * 128:(cc + 1) * 128],
                             c.sb["xqbf"][:, cc * 128:(cc + 1) * 128],
                             c.sb["identbf"][:])
            self.mk[f"tr{i}"] = me.mark()
            me.wait(self.W["vector"], self.mk.get(f"xqt{i}", 0))
            for oc, (o0, ow) in enumerate(OCS):
                if oc < 2:
                    me.wait(self.W["vector"], self.mk.get(f"xn{i-1}", 0))
                    if i > 0:
                        me.wait(self.W["vector"], self.mk.get(f"hp{i-1}_5", 0))
                else:
                    me.wait(self.W["vector"], self.mk.get(f"hp{i}_{oc-2}", 0))
                pair = (oc % 2) * 1024
                hg = c.pgu[:, pair:pair + ow]
                hu = c.pgu[:, pair + 512:pair + 512 + ow]
                for cc in range(8):
                    me.matmul(hg, c.sb["xqT"][:, cc * 128:(cc + 1) * 128],
                              c.sb["Wg"][:, cc * DFF + o0:cc * DFF + o0 + ow],
                              start=(cc == 0), stop=(cc == 7))
                for cc in range(8):
                    me.matmul(hu, c.sb["xqT"][:, cc * 128:(cc + 1) * 128],
                              c.sb["Wu"][:, cc * DFF + o0:cc * DFF + o0 + ow],
                              start=(cc == 0), stop=(cc == 7))
                self.mk[f"mm{i}_{oc}"] = me.mark()
            for oc, (o0, ow) in enumerate(OCS):
                me.wait(self.W["vector"], self.mk.get(f"hq{i}_{oc}", 0))
                if not (i == 0 and oc == 0):
                    pi, po = (i, oc - 1) if oc > 0 else (i - 1, NOC - 1)
                    me.wait(self.W["vector"], self.mk.get(f"hqt{pi}_{po}", 0))
                for k in range(ow // 128):
                    me.transpose(c.pt_bf[:, 1024 + k * 128:1024 + (k + 1) * 128],
                                 c.sb["hqbf"][:, o0 + k * 128:o0 + (k + 1) * 128],
                                 c.sb["identbf"][:])
                self.mk[f"tr2{i}_{oc}"] = me.mark()
            me.wait(self.W["vector"], self.mk.get(f"hp{i}_4", 0))
            last_wait = -1
            for cc in range(22):
                oc_of = (cc * 128) // 512
                if oc_of != last_wait:
                    me.wait(self.W["vector"], self.mk.get(f"hqt{i}_{oc_of}", 0))
                    last_wait = oc_of
                me.matmul(c.pgu[:, 0:512], c.sb["hqT"][:, cc * 128:(cc + 1) * 128],
                          c.sb["Wd"][:, cc * DM:cc * DM + 512],
                          start=(cc == 0), stop=(cc == 21))
                me.matmul(c.pgu[:, 512:1024],
                          c.sb["hqT"][:, cc * 128:(cc + 1) * 128],
                          c.sb["Wd"][:, cc * DM + 512:cc * DM + 1024],
                          start=(cc == 0), stop=(cc == 21))
            self.mk[f"down{i}"] = me.mark()




NH = 16
HD = 64


class AttnPhase(Phase):

    def __init__(self, ctx, si, so, tag):
        super().__init__(ctx, tag)
        self.si = si
        self.so = so

    def xt(self, i):
        return self.ctx.sb["xt"][:, (i % 2) * DM:(i % 2) * DM + DM]

    def xnew(self, i):
        return self.ctx.sb["xnew"][:, (i % 2) * DM:(i % 2) * DM + DM]

    def emit_gpsimd(self):
        c, me = self.ctx, self.W["gpsimd"]
        for k, nm in enumerate(("WqT", "WkT", "WvT", "WfcT")):
            self.mk["w_" + nm] = me.dma_start(
                out=c.sb["Wat"][:, k * 8192:(k + 1) * 8192].rearrange(
                    "p (a b) -> p a b", b=1024),
                in_=c.dram[nm].rearrange("(a p) b -> p a b", p=128))

    def emit_sync(self, E=None):
        c, me = self.ctx, self.W["sync"]
        xs = c.dram["x_state"].rearrange("(t p) d -> t p d", p=128)
        for i in range(2):
            self.mk[f"ld{i}"] = me.dma_start(out=self.xt(i), in_=xs[i])
        for i in range(NT):
            me.wait(self.W["vector"], self.mk.get(f"xn{i}", 0))
            self.mk[f"st{i}"] = me.dma_start(out=xs[i], in_=self.xnew(i))
            if i + 2 < NT:
                self.mk[f"ld{i+2}"] = me.dma_start(out=self.xt(i + 2), in_=xs[i + 2])
        me.wait_self_all()

    def emit_scalar(self, E=None):
        c, me = self.ctx, self.W["scalar"]
        for i in range(NT):
            me.wait(self.W["vector"], self.mk.get(f"shift{i}", 0))
            me.activation(c.sb["att_E"][:, 0:256], c.sb["att_sc"][:, 0:256], AF.Exp)
            self.mk[f"exp{i}"] = me.mark()
            me.wait(self.W["vector"], self.mk.get(f"xn{i}", 0))
            me.activation(c.sb["sqscr"][:, 0:DM], self.xnew(i), AF.Square,
                          accum_out=c.sb["stats_sq" + self.so][:, i:i + 1])
            self.mk[f"sq{i}"] = me.mark()
        me.wait(self.W["vector"], self.mk.get("var", 0))
        me.activation(c.sb["tmpB"][:, 0:NT], c.sb["tmpA"][:, 0:NT], AF.Ln,
                      bias=EPS)
        me.activation(c.sb["stats_r" + self.so][:, 0:NT], c.sb["tmpB"][:, 0:NT],
                      AF.Exp, scale=-0.5)

    def emit_vector(self, E=None):
        c, me = self.ctx, self.W["vector"]
        for i in range(NT):
            me.wait(self.W["sync"], self.mk.get(f"ld{i}", 0))
            me.tensor_scalar(c.sb["xnbf"][:], self.xt(i),
                             c.sb["stats_m" + self.si][:, i:i + 1],
                             c.sb["stats_r" + self.si][:, i:i + 1],
                             ALU.subtract, ALU.mult)
            self.mk[f"xn_r{i}"] = me.mark()
            me.wait(self.W["tensor"], self.mk.get(f"tr{i}", 0))
            me.tensor_copy(c.sb["xnT"][:], c.pt_bf[:, 2048:3072])
            self.mk[f"xnt{i}"] = me.mark()
            me.wait(self.W["tensor"], self.mk.get(f"qkv{i}", 0))
            me.tensor_copy(c.sb["qbf"][:], c.pgu[:, 0:1024])
            me.tensor_copy(c.sb["kbf"][:], c.pgu[:, 1024:2048])
            me.tensor_copy(c.sb["vbf"][:], c.ptf[:, 0:1024])
            self.mk[f"ev{i}"] = me.mark()
            q4 = c.sb["qbf"][:].rearrange("p (h d) -> p h d", d=HD)
            k4 = c.sb["kbf"][:].rearrange("p (t d) -> p t d", d=HD)
            sc3 = c.sb["att_sc"][:, 0:256].rearrange("p (h t) -> p h t", t=NH)
            for hg in range(4):
                qv = q4[:, hg * 4:(hg + 1) * 4, :].unsqueeze(2).broadcast_to(
                    [128, 4, NH, HD])
                kv = k4.unsqueeze(1).broadcast_to([128, 4, NH, HD])
                m4 = c.sb["att_m4"][:, 0:4096].rearrange(
                    "p (h t d) -> p h t d", t=NH, d=HD)
                me.tensor_tensor(m4, qv, kv, ALU.mult)
                me.tensor_reduce(sc3[:, hg * 4:(hg + 1) * 4, :], m4, AX.X,
                                 ALU.add)
            me.tensor_reduce(c.sb["att_mx"][:, 0:NH], sc3, AX.X, ALU.max)
            mxb = c.sb["att_mx"][:, 0:NH].unsqueeze(2).broadcast_to([128, NH, NH])
            me.tensor_tensor(sc3, sc3, mxb, ALU.subtract)
            me.tensor_scalar(c.sb["att_sc"][:, 0:256], c.sb["att_sc"][:, 0:256],
                             0.125, None, ALU.mult)
            self.mk[f"shift{i}"] = me.mark()
            me.wait(self.W["scalar"], self.mk.get(f"exp{i}", 0))
            E3 = c.sb["att_E"][:, 0:256].rearrange("p (h t) -> p h t", t=NH)
            me.tensor_reduce(c.sb["att_sm"][:, 0:NH], E3, AX.X, ALU.add)
            me.reciprocal(c.sb["att_rc"][:, 0:NH], c.sb["att_sm"][:, 0:NH])
            rcb = c.sb["att_rc"][:, 0:NH].unsqueeze(2).broadcast_to([128, NH, NH])
            me.tensor_tensor(c.sb["att_A"][:, 0:256].rearrange(
                "p (h t) -> p h t", t=NH), E3, rcb, ALU.mult)
            A3 = c.sb["att_A"][:, 0:256].rearrange("p (h t) -> p h t", t=NH)
            v3 = c.sb["vbf"][:].rearrange("p (t d) -> p t d", d=HD)
            vperm = v3.transpose([0, 2, 1])
            ao3 = c.sb["aof"][:].rearrange("p (h d) -> p h d", d=HD)
            for hg in range(4):
                av = A3[:, hg * 4:(hg + 1) * 4, :].unsqueeze(2).broadcast_to(
                    [128, 4, HD, NH])
                vv = vperm.unsqueeze(1).broadcast_to([128, 4, HD, NH])
                m4 = c.sb["att_m4"][:, 0:4096].rearrange(
                    "p (h d t) -> p h d t", d=HD, t=NH)
                me.tensor_tensor(m4, av, vv, ALU.mult)
                me.tensor_reduce(ao3[:, hg * 4:(hg + 1) * 4, :], m4, AX.X,
                                 ALU.add)
            me.tensor_copy(c.sb["abf"][:], c.sb["aof"][:])
            self.mk[f"ao{i}"] = me.mark()
            me.wait(self.W["tensor"], self.mk.get(f"atr{i}", 0))
            me.tensor_copy(c.sb["aT"][:], c.pt_bf[:, 2048:3072])
            self.mk[f"aT{i}"] = me.mark()
            me.wait(self.W["tensor"], self.mk.get(f"fc{i}", 0))
            if i >= 2:
                me.wait(self.W["sync"], self.mk.get(f"st{i-2}", 0))
                me.wait(self.W["scalar"], self.mk.get(f"sq{i-2}", 0))
            me.op("_custom_dve", AFFINE_THEN_ADD, out=self.xnew(i),
                  in0=c.pgu[:, 0:1024], in1=self.xt(i), s0=1.0, s1=0.0)
            self.mk[f"xn{i}"] = me.mark()
            me.reduce_sum(c.sb["msc"][:, 0:1], self.xnew(i), AX.X)
            me.tensor_scalar(c.sb["stats_m" + self.so][:, i:i + 1],
                             c.sb["msc"][:, 0:1], 1.0 / DM, None, ALU.mult)
        me.wait(self.W["scalar"], self.mk.get(f"sq{NT-1}", 0))
        me.tensor_scalar(c.sb["tmpA"][:, 0:NT],
                         c.sb["stats_sq" + self.so][:, 0:NT],
                         1.0 / DM, None, ALU.mult)
        me.tensor_tensor(c.sb["tmpB"][:, 0:NT], c.sb["stats_m" + self.so][:, 0:NT],
                         c.sb["stats_m" + self.so][:, 0:NT], ALU.mult)
        me.tensor_tensor(c.sb["tmpA"][:, 0:NT], c.sb["tmpA"][:, 0:NT],
                         c.sb["tmpB"][:, 0:NT], ALU.subtract)
        self.mk["var"] = me.mark()

    def emit_tensor(self, E=None):
        c, me = self.ctx, self.W["tensor"]
        for k in ("WqT", "WkT", "WvT", "WfcT"):
            me.wait(self.W["gpsimd"], self.mk.get("w_" + k, 0))
        for i in range(NT):
            me.wait(self.W["vector"], self.mk.get(f"xn_r{i}", 0))
            if i > 0:
                me.wait(self.W["vector"], self.mk.get(f"xnt{i-1}", 0))
            for cc in range(8):
                me.transpose(c.pt_bf[:, 2048 + cc * 128:2048 + (cc + 1) * 128],
                             c.sb["xnbf"][:, cc * 128:(cc + 1) * 128],
                             c.sb["identbf"][:])
            self.mk[f"tr{i}"] = me.mark()
            me.wait(self.W["vector"], self.mk.get(f"xnt{i}", 0))
            if i > 0:
                me.wait(self.W["vector"], self.mk.get(f"ev{i-1}", 0))
                me.wait(self.W["vector"], self.mk.get(f"xn{i-1}", 0))
            for w, out in ((0, c.pgu[:, 0:1024]), (1, c.pgu[:, 1024:2048]),
                           (2, c.ptf[:, 0:1024])):
                for half in range(2):
                    for cc in range(8):
                        me.matmul(out[:, half * 512:half * 512 + 512],
                                  c.sb["xnT"][:, cc * 128:(cc + 1) * 128],
                                  c.sb["Wat"][:, w * 8192 + cc * 1024 + half * 512:
                                              w * 8192 + cc * 1024 + half * 512 + 512],
                                  start=(cc == 0), stop=(cc == 7))
            self.mk[f"qkv{i}"] = me.mark()
            me.wait(self.W["vector"], self.mk.get(f"ao{i}", 0))
            for cc in range(8):
                me.transpose(c.pt_bf[:, 2048 + cc * 128:2048 + (cc + 1) * 128],
                             c.sb["abf"][:, cc * 128:(cc + 1) * 128],
                             c.sb["identbf"][:])
            self.mk[f"atr{i}"] = me.mark()
            me.wait(self.W["vector"], self.mk.get(f"aT{i}", 0))
            for half in range(2):
                for cc in range(8):
                    me.matmul(c.pgu[:, half * 512:half * 512 + 512],
                              c.sb["aT"][:, cc * 128:(cc + 1) * 128],
                              c.sb["Wat"][:, 3 * 8192 + cc * 1024 + half * 512:
                                          3 * 8192 + cc * 1024 + half * 512 + 512],
                              start=(cc == 0), stop=(cc == 7))
            self.mk[f"fc{i}"] = me.mark()




NEXP = 64
MOE_SUPERS = [list(range(s, min(s + 6, NT))) for s in range(0, NT, 6)]


class MoEWeightPrep(Phase):

    def __init__(self, ctx):
        super().__init__(ctx, "wm")
        self.chunks = ([("keysR", "keys_st", s) for s in range(8)] +
                       [("valuesR", "values_st", s) for s in range(8)])

    def emit_sync(self):
        c, me = self.ctx, self.W["sync"]
        for ci, (src, dst, s) in enumerate(self.chunks):
            me.wait(self.W["vector"], self.mk.get(("ca", ci - 2), 0))
            self.mk[("rd", ci)] = me.dma_start(
                out=c.sb["wmA"][:, ci % 2, :],
                in_=c.dram[src][:, s * 8192:(s + 1) * 8192])
        me.wait_self_all()

    def emit_vector(self):
        c, me = self.ctx, self.W["vector"]
        for ci in range(len(self.chunks)):
            me.wait(self.W["sync"], self.mk.get(("rd", ci), 0))
            me.wait(self.W["gpsimd"], self.mk.get(("wr", ci - 2), 0))
            me.tensor_copy(c.sb["wmB"][:, ci % 2, :], c.sb["wmA"][:, ci % 2, :])
            self.mk[("ca", ci)] = me.mark()

    def emit_gpsimd(self):
        c, me = self.ctx, self.W["gpsimd"]
        for ci, (src, dst, s) in enumerate(self.chunks):
            me.wait(self.W["vector"], self.mk.get(("ca", ci), 0))
            self.mk[("wr", ci)] = me.dma_start(
                out=c.dram[dst][:, s * 8192:(s + 1) * 8192],
                in_=c.sb["wmB"][:, ci % 2, :])
        me.wait_self_all()


class MoEPhase(Phase):

    def __init__(self, ctx, si, so, tag):
        super().__init__(ctx, tag)
        self.si = si
        self.so = so
        sup = MOE_SUPERS
        self.sup = sup
        self.chk = []
        self.base = []
        for sx, Ts in enumerate(sup):
            self.base.append(len(self.chk))
            for g4 in range(16):
                for ti, t in enumerate(Ts):
                    self.chk.append((sx, g4, ti, t))
        self.TOT = len(self.chk)
        self.sqpos = {}
        for sx in range(len(sup) - 1):
            Xe = self.base[sx] + 16 * len(sup[sx])
            for ti, t in enumerate(sup[sx]):
                self.sqpos.setdefault(Xe - len(sup[sx]) + ti + 5, []).append(t)
        self.ptile = {}
        for sx, Ts in enumerate(sup):
            for tipos, t in enumerate(Ts):
                step = 0 if sx == 0 else self.base[sx] - len(sup[sx - 1]) + tipos
                self.ptile.setdefault(step, []).append((sx, tipos, t))
        self.items = []
        self.prep_items = {}
        self.sc_item = {}
        self.xn_item2 = {}
        self.xn_prev_h = {}
        for sx, Ts in enumerate(sup):
            if sx == 0:
                continue
            pTs = sup[sx - 1]
            for tipos, t in enumerate(Ts):
                if tipos < len(pTs):
                    self.xn_prev_h[t] = self.base[sx - 1] + 15 * len(pTs) + tipos
        for X in range(self.TOT + 4):
            if 0 <= X - 2 < self.TOT:
                self.sc_item[X - 2] = len(self.items)
                self.items.append(("sc", X - 2))
            if X in self.ptile:
                ids = []
                for sx, tipos, t in self.ptile[X]:
                    for half in range(2):
                        ids.append(len(self.items))
                        self.items.append(("xn", t, tipos, half))
                        if half == 1:
                            self.xn_item2[t] = ids[-1]
                    ids.append(len(self.items))
                    self.items.append(("lg", t, tipos))
                self.prep_items[X] = ids

    def xt2(self, i):
        return self.ctx.sb["xt"][:, (i % 2) * DM:(i % 2) * DM + DM]

    def xnew(self, i):
        return self.ctx.sb["xnew"][:, (i % 2) * DM:(i % 2) * DM + DM]


    def _t_item(self, me, iid):
        c = self.ctx
        it = self.items[iid]
        me.wait(self.W["scalar"], self.mk.get(("cpT", iid - 2), 0))
        if it[0] == "lg":
            _, t, tipos = it
            me.wait(self.W["gpsimd"], self.mk.get("selT", 0))
            me.wait(self.W["scalar"], self.mk.get(("xtc", t), 0))
            lb = 1024 + (iid % 2) * 512 + 256
            for cc in range(8):
                me.matmul(c.ptf[:, lb:lb + 64],
                          c.sb["xnTs"][:, cc, tipos * 128:(tipos + 1) * 128],
                          c.sb["selTb"][:, cc, :],
                          start=(cc == 0), stop=(cc == 7))
            self.mk[("lgi", t)] = me.mark()
            return
        tb = 2048 + (iid % 2) * 1024
        if it[0] == "sc":
            X = it[1]
            me.wait(self.W["vector"], self.mk.get(("gat", X), 0))
            for k in range(4):
                me.transpose(c.pt_bf[:, tb + k * 128:tb + (k + 1) * 128],
                             c.sb["scb"][:, X % 2, k * 128:(k + 1) * 128],
                             c.sb["identbf"][:])
        else:
            _, t, tipos, half = it
            me.wait(self.W["vector"], self.mk.get(("xn", t), 0))
            for k in range(4):
                cc = half * 4 + k
                me.transpose(c.pt_bf[:, tb + k * 128:tb + (k + 1) * 128],
                             c.sb["m_xnbf"][:, cc * 128:(cc + 1) * 128],
                             c.sb["identbf"][:])
        self.mk[("tri", iid)] = me.mark()

    def _t_h(self, me, X):
        c = self.ctx
        sx, g4, ti, t = self.chk[X]
        if X == 0:
            for s8 in range(6):
                me.wait(self.W["sync"], self.mk.get(("k", s8), 0))
            for s8 in range(6, 8):
                me.wait(self.W["gpsimd"], self.mk.get(("k", s8), 0))
        me.wait(self.W["scalar"], self.mk.get(("xtc", t), 0))
        me.wait(self.W["scalar"], self.mk.get(("hev", X - 2), 0))
        b = (X % 2) * 512
        for cc in range(8):
            me.matmul(c.ptf[:, b:b + 512],
                      c.sb["xnTs"][:, cc, ti * 128:(ti + 1) * 128],
                      c.sb["keysb"][:, cc, g4 * 512:(g4 + 1) * 512],
                      start=(cc == 0), stop=(cc == 7))
        self.mk[("h", X)] = me.mark()

    def _t_vmm(self, me, X):
        c = self.ctx
        sx, g4, ti, t = self.chk[X]
        g4g = sx * 16 + g4
        if ti == 0:
            me.wait(self.W["sync"], self.mk.get(("vd", 2 * g4g), 0))
            me.wait(self.W["gpsimd"], self.mk.get(("vd", 2 * g4g + 1), 0))
        me.wait(self.W["scalar"], self.mk.get(("cpT", self.sc_item[X]), 0))
        me.wait(self.W["vector"], self.mk.get(("ev", X - 2), 0))
        pb = (X % 2) * 1024
        for half in range(2):
            for e4 in range(4):
                pglob = 2 * g4g + e4 // 2
                vs = (e4 % 2) * 1024 + half * 512
                me.matmul(c.pgu[:, pb + half * 512:pb + half * 512 + 512],
                          c.sb["scT"][:, X % 2, e4 * 128:(e4 + 1) * 128],
                          c.sb["vbuf"][:, pglob % 4, vs:vs + 512],
                          start=(e4 == 0), stop=(e4 == 3))
        self.mk[("vmm", X)] = me.mark()
        if ti == len(self.sup[sx]) - 1:
            self.mk[("vmG", g4g)] = me.mark()

    def emit_tensor(self, E=None):
        me = self.W["tensor"]
        for X in range(self.TOT + 4):
            if X == 0:
                for iid in self.prep_items.get(0, []):
                    self._t_item(me, iid)
            if X < self.TOT:
                self._t_h(me, X)
            if 0 <= X - 2 < self.TOT:
                self._t_item(me, self.sc_item[X - 2])
            if 0 <= X - 4 < self.TOT:
                self._t_vmm(me, X - 4)
            if X > 0:
                for iid in self.prep_items.get(X, []):
                    self._t_item(me, iid)


    def _s_item(self, me, iid):
        c = self.ctx
        it = self.items[iid]
        if it[0] == "lg":
            _, t, tipos = it
            me.wait(self.W["tensor"], self.mk.get(("lgi", t), 0))
            me.wait(self.W["vector"], self.mk.get(("gt", t - 2), 0))
            lb = 1024 + (iid % 2) * 512 + 256
            me.activation(c.sb["selsb"][:, t % 2, :], c.ptf[:, lb:lb + 64],
                          AF.Sigmoid)
            self.mk[("sig", t)] = me.mark()
            self.mk[("cpT", iid)] = me.mark()
            return
        me.wait(self.W["tensor"], self.mk.get(("tri", iid), 0))
        tb = 2048 + (iid % 2) * 1024
        src = c.pt_bf[:, tb:tb + 512]
        if it[0] == "sc":
            X = it[1]
            me.wait(self.W["tensor"], self.mk.get(("vmm", X - 2), 0))
            me.activation(c.sb["scT"][:, X % 2, :], src, AF.Relu)
        else:
            _, t, tipos, half = it
            ph = self.xn_prev_h.get(t)
            if ph is not None:
                me.wait(self.W["tensor"], self.mk.get(("h", ph), 0))
            me.activation(
                c.sb["xnTs"][:, half * 4:half * 4 + 4,
                             tipos * 128:(tipos + 1) * 128],
                src.rearrange("p (a b) -> p a b", b=128), AF.Copy)
            if half == 1:
                self.mk[("xtc", t)] = me.mark()
        self.mk[("cpT", iid)] = me.mark()

    def emit_scalar(self, E=None):
        c, me = self.ctx, self.W["scalar"]
        for X in range(self.TOT + 4):
            for t in self.sqpos.get(X, []):
                me.wait(self.W["vector"], self.mk.get(("af", t), 0))
                me.activation(c.sb["sqscr"][:, 0:DM], self.xnew(t), AF.Square,
                              accum_out=c.sb["stats_sq" + self.so][:, t:t + 1])
                self.mk[("sq", t)] = me.mark()
            if X == 0:
                for iid in self.prep_items.get(0, []):
                    self._s_item(me, iid)
            if X < self.TOT:
                me.wait(self.W["tensor"], self.mk.get(("h", X), 0))
                me.wait(self.W["tensor"],
                        self.mk.get(("tri", self.sc_item.get(X - 2, -9)), 0))
                me.activation(c.sb["scb"][:, X % 2, :],
                              c.ptf[:, (X % 2) * 512:(X % 2) * 512 + 512],
                              AF.Relu)
                self.mk[("hev", X)] = me.mark()
            if 0 <= X - 2 < self.TOT:
                self._s_item(me, self.sc_item[X - 2])
            if X > 0:
                for iid in self.prep_items.get(X, []):
                    self._s_item(me, iid)
        for t in self.sup[-1]:
            me.wait(self.W["vector"], self.mk.get(("af", t), 0))
            me.activation(c.sb["sqscr"][:, 0:DM], self.xnew(t), AF.Square,
                          accum_out=c.sb["stats_sq" + self.so][:, t:t + 1])
            self.mk[("sq", t)] = me.mark()
        me.wait(self.W["vector"], self.mk.get("var", 0))
        me.activation(c.sb["tmpB"][:, 0:NT], c.sb["tmpA"][:, 0:NT], AF.Ln,
                      bias=EPS)
        me.activation(c.sb["stats_r" + self.so][:, 0:NT], c.sb["tmpB"][:, 0:NT],
                      AF.Exp, scale=-0.5)


    def _v_gating(self, me, sx, tipos, t):
        c = self.ctx
        me.wait(self.W["scalar"], self.mk.get(("sig", t), 0))
        sl = c.sb["selsb"][:, t % 2, :]
        me.tensor_copy(c.sb["selw"][:], sl)
        for r in range(4):
            me.reduce_max(c.sb["msc"][:, 1:2], c.sb["selw"][:], AX.X)
            if r < 3:
                me.tensor_scalar(c.sb["selm"][:], c.sb["selw"][:],
                                 c.sb["msc"][:, 1:2], -1e30, ALU.is_ge,
                                 ALU.mult)
                me.tensor_tensor(c.sb["selw"][:], c.sb["selw"][:],
                                 c.sb["selm"][:], ALU.add)
        me.tensor_scalar(c.sb["selm"][:], sl, c.sb["msc"][:, 1:2], None,
                         ALU.is_ge)
        me.tensor_tensor(
            c.sb["gates_bf"][:, sx % 2, tipos * 64:tipos * 64 + 64],
            c.sb["selm"][:], sl, ALU.mult)
        self.mk[("gt", t)] = me.mark()

    def emit_vector(self, E=None):
        c, me = self.ctx, self.W["vector"]
        sup, chk = self.sup, self.chk
        def vprep(X):
            for sx, tipos, t in self.ptile.get(X, []):
                me.wait(self.W["sync"], self.mk.get(("ld", t), 0))
                me.wait(self.W["tensor"],
                        self.mk.get(("tri", self.xn_item2.get(t - 1, -9)), 0))
                me.tensor_scalar(c.sb["m_xnbf"][:],
                                 c.sb["m_xt"][:, t % 2, :],
                                 c.sb["stats_m" + self.si][:, t:t + 1],
                                 c.sb["stats_r" + self.si][:, t:t + 1],
                                 ALU.subtract, ALU.mult)
                self.mk[("xn", t)] = me.mark()
                self._v_gating(me, sx, tipos, t)

        for X in range(self.TOT + 4):
            if X == 0:
                vprep(0)
            if X < self.TOT:
                sx, g4, ti, t = chk[X]
                me.wait(self.W["scalar"], self.mk.get(("hev", X), 0))
                g = c.sb["gates_bf"][:, sx % 2,
                                     ti * 64 + 4 * g4:ti * 64 + 4 * g4 + 4]
                sc3 = c.sb["scb"][:, X % 2, :].rearrange("p (a b) -> p a b",
                                                         b=128)
                me.tensor_tensor(sc3, sc3,
                                 g.unsqueeze(2).broadcast_to([128, 4, 128]),
                                 ALU.mult)
                self.mk[("gat", X)] = me.mark()
            if X > 0:
                vprep(X)
            if 0 <= X - 4 < self.TOT:
                Xe = X - 4
                sx, g4, ti, t = chk[Xe]
                me.wait(self.W["tensor"], self.mk.get(("vmm", Xe), 0))
                oa = c.sb["outac"][:, ti, :]
                pg = c.pgu[:, (Xe % 2) * 1024:(Xe % 2) * 1024 + 1024]
                if g4 == 0:
                    me.tensor_copy(oa, pg)
                else:
                    me.tensor_tensor(oa, oa, pg, ALU.add)
                if g4 == 15:
                    me.wait(self.W["gpsimd"], self.mk.get(("ld2", t), 0))
                    me.wait(self.W["gpsimd"], self.mk.get(("st", t - 2), 0))
                    me.wait(self.W["scalar"], self.mk.get(("sq", t - 2), 0))
                    me.op("_custom_dve", AFFINE_THEN_ADD, out=self.xnew(t),
                          in0=oa, in1=self.xt2(t),
                          s0=0.5, s1=0.0)
                    self.mk[("af", t)] = me.mark()
                    me.reduce_sum(c.sb["msc"][:, 0:1], self.xnew(t), AX.X)
                    me.tensor_scalar(c.sb["stats_m" + self.so][:, t:t + 1],
                                     c.sb["msc"][:, 0:1], 1.0 / DM, None,
                                     ALU.mult)
                self.mk[("ev", Xe)] = me.mark()
        me.wait(self.W["scalar"], self.mk.get(("sq", NT - 1), 0))
        me.tensor_scalar(c.sb["tmpA"][:, 0:NT],
                         c.sb["stats_sq" + self.so][:, 0:NT],
                         1.0 / DM, None, ALU.mult)
        me.tensor_tensor(c.sb["tmpB"][:, 0:NT],
                         c.sb["stats_m" + self.so][:, 0:NT],
                         c.sb["stats_m" + self.so][:, 0:NT], ALU.mult)
        me.tensor_tensor(c.sb["tmpA"][:, 0:NT], c.sb["tmpA"][:, 0:NT],
                         c.sb["tmpB"][:, 0:NT], ALU.subtract)
        self.mk["var"] = me.mark()


    def emit_sync(self, E=None):
        c, me = self.ctx, self.W["sync"]
        sup, chk = self.sup, self.chk
        xs = c.dram["x_state"].rearrange("(t p) d -> t p d", p=128)
        ldpos = {}
        for step, entries in self.ptile.items():
            for sx, tipos, t in entries:
                ldpos.setdefault(max(0, step - 2), []).append(t)
        for X in range(self.TOT + 4):
            if X == 0:
                for s8 in range(6):
                    self.mk[("k", s8)] = me.dma_start(
                        out=c.sb["keysb"][:, s8, :],
                        in_=c.dram["keys_st"][:, s8 * 8192:(s8 + 1) * 8192])
            if X in ldpos:
                for t in ldpos[X]:
                    me.wait(self.W["vector"], self.mk.get(("xn", t - 2), 0))
                    self.mk[("ld", t)] = me.dma_start(
                        out=c.sb["m_xt"][:, t % 2, :], in_=xs[t])
            if X < self.TOT:
                sx, g4, ti, t = chk[X]
                if ti == 0:
                    g4g = sx * 16 + g4
                    pglob = 2 * g4g
                    me.wait(self.W["tensor"],
                            self.mk.get(("vmG", (pglob - 4) // 2), 0))
                    e0 = 4 * g4
                    self.mk[("vd", pglob)] = me.dma_start(
                        out=c.sb["vbuf"][:, pglob % 4, :],
                        in_=c.dram["values_st"][:, e0 * 1024:(e0 + 2) * 1024])
        me.wait_self_all()

    def emit_gpsimd(self, E=None):
        c, me = self.ctx, self.W["gpsimd"]
        sup, chk = self.sup, self.chk
        xs = c.dram["x_state"].rearrange("(t p) d -> t p d", p=128)
        self.mk["selT"] = me.dma_start(
            out=c.sb["selTb"][:],
            in_=c.dram["selT"].rearrange("(a p) b -> p a b", p=128))
        for s8 in range(6, 8):
            self.mk[("k", s8)] = me.dma_start(
                out=c.sb["keysb"][:, s8, :],
                in_=c.dram["keys_st"][:, s8 * 8192:(s8 + 1) * 8192])
        stpos = {}
        for sx in range(len(sup)):
            Xe = self.base[sx] + 16 * len(sup[sx])
            stpos[Xe - len(sup[sx])] = sx
        for X in range(self.TOT + 4):
            if X < self.TOT:
                sx, g4, ti, t = chk[X]
                if ti == 0:
                    g4g = sx * 16 + g4
                    pglob = 2 * g4g + 1
                    me.wait(self.W["tensor"],
                            self.mk.get(("vmG", (pglob - 4) // 2), 0))
                    e0 = 4 * g4 + 2
                    self.mk[("vd", pglob)] = me.dma_start(
                        out=c.sb["vbuf"][:, pglob % 4, :],
                        in_=c.dram["values_st"][:, e0 * 1024:(e0 + 2) * 1024])
            if X in stpos:
                Ts = sup[stpos[X]]
                seq = []
                for k2, t in enumerate(Ts):
                    seq.append(("ld2", t))
                    if k2 >= 2:
                        seq.append(("st", Ts[k2 - 2]))
                seq.append(("st", Ts[-2]))
                seq.append(("st", Ts[-1]))
                for kind, t in seq:
                    if kind == "ld2":
                        me.wait(self.W["vector"], self.mk.get(("af", t - 2), 0))
                        self.mk[("ld2", t)] = me.dma_start(
                            out=self.xt2(t), in_=xs[t])
                    else:
                        me.wait(self.W["vector"], self.mk.get(("af", t), 0))
                        self.mk[("st", t)] = me.dma_start(
                            out=xs[t], in_=self.xnew(t))
        me.wait_self_all()




MSTAGES = ("E0", "E1", "E2", "E2X", "E2SCAN", "E2OUT")
NSEG = 8
SEGT = 512
SEGTILES = SEGT // 128


class MambaPhase(Phase):

    def __init__(self, ctx, si, so, tag):
        super().__init__(ctx, tag)
        self.si = si
        self.so = so

    def xt(self, i):
        return self.ctx.sb["e_xt"][:, i % 4, :]

    def xnew(self, i):
        return self.ctx.sb["xnew"][:, (i % 2) * DM:(i % 2) * DM + DM]

    def emit_gpsimd(self):
        c, me = self.ctx, self.W["gpsimd"]
        self.mk["win"] = me.dma_start(
            out=c.sb["Win_b"][:].rearrange("p a b -> p (a b)"),
            in_=c.dram["WinR"])
        self.mk["wout"] = me.dma_start(
            out=c.sb["Wout_b"][:].rearrange("p a b -> p (a b)"),
            in_=c.dram["WoutR"])
        self.mk["wxp"] = me.dma_start(
            out=c.sb["Wxp_b"][:].rearrange("p a b -> p (a b)"),
            in_=c.dram["WxpR"])
        self.mk["wdt"] = me.dma_start(
            out=c.sb["Wdt_b"][0:64, :, :].rearrange("p a b -> p (a b)"),
            in_=c.dram["WdtR"])
        xr = c.dram["xraw_st"].rearrange("p (a b) -> p a b", b=S)
        for sg in range(NSEG if "E0" in MSTAGES else 0):
            for dc in range(16):
                me.wait(self.W["vector"], self.mk.get(("xrev", sg, dc), 0))
                self.mk[("xrw", sg, dc)] = me.dma_start(
                    out=xr[:, dc, sg * SEGT:(sg + 1) * SEGT],
                    in_=c.sb["xrstage"][:, (dc % 2) * SEGT:(dc % 2) * SEGT + SEGT])
        self.mk["e0_dn"] = me.dn
        for dcz in range(16 if "E1" in MSTAGES else 0):
            me.wait(self.W["vector"], self.mk.get(("cvz", dcz), 0) if dcz > 0
                    else self.mk.get("cbz", 0))
            self.mk[("cvl", dcz)] = me.dma_start(
                out=c.sb["cbuf"][:, 1:S + 1], in_=xr[:, dcz, :])
            me.wait(self.W["scalar"], self.mk.get(("gel", dcz), 0))
            dst = c.dram["u_st"] if dcz < 8 else c.dram["z_st"]
            self.mk[("cgw", dcz)] = me.dma_start(
                out=dst.rearrange("p (a b) -> p a b", b=S)[:, dcz % 8, :],
                in_=c.sb["cgout"][:])
        us = c.dram["u_st"].rearrange("p (a b) -> p a b", b=S)
        zs = c.dram["z_st"].rearrange("p (a b) -> p a b", b=S)
        for sg in range(NSEG if "E2" in MSTAGES else 0):
            if sg == 0:
                for dcz in range(16):
                    me.wait(me, self.mk.get(("cgw", dcz), 0))
            me.wait(self.W["vector"], self.mk.get(("useg_done", sg - 1), 0))
            self.mk[("ubf", sg)] = me.dma_start(
                out=c.sb["ubf"][:], in_=us[:, :, sg * SEGT:(sg + 1) * SEGT])
            self.mk[("zbf", sg)] = me.dma_start(
                out=c.sb["zbf"][:], in_=zs[:, :, sg * SEGT:(sg + 1) * SEGT])
            if "E2X" not in MSTAGES:
                continue
            me.wait(self.W["vector"], self.mk.get(("bc_ev", sg), 0))
            if sg > 0:
                me.wait(me, self.mk.get(("bbc", sg - 1, 15), 0))
                me.wait(me, self.mk.get(("cbc", sg - 1, 15), 0))
            self.mk[("bst", sg)] = me.dma_start(out=c.dram["bc_st"],
                                                in_=c.sb["Bsb"][0:16, :])
            self.mk[("cst", sg)] = me.dma_start(out=c.dram["cc_st"],
                                                in_=c.sb["Csb"][0:16, :])
            me.wait(me, self.mk[("bst", sg)])
            me.wait(me, self.mk[("cst", sg)])
            for n in range(16 if "E2SCAN" in MSTAGES else 0):
                me.wait(self.W["vector"], self.mk.get(("nfree", sg, n - 1), 0))
                self.mk[("bbc", sg, n)] = me.dma_start(
                    out=c.sb["bbc"][:],
                    in_=c.dram["bc_st"][n:n + 1, :].broadcast_to([128, SEGT]))
                self.mk[("cbc", sg, n)] = me.dma_start(
                    out=c.sb["cbc"][:],
                    in_=c.dram["cc_st"][n:n + 1, :].broadcast_to([128, SEGT]))
        me.wait_self_all()

    def emit_sync(self, E=None):
        c, me = self.ctx, self.W["sync"]
        xs = c.dram["x_state"].rearrange("(t p) d -> t p d", p=128)
        for nm, dst in (("KxR", "kx"), ("KzR", "kz"), ("bxR", "bx"),
                        ("bzR", "bz"), ("AlogR", "alog"), ("DR", "dsb"),
                        ("bdtR", "bdt")):
            self.mk["p_" + dst] = me.dma_start(out=c.sb[dst][:], in_=c.dram[nm])
        for i in range(NT if "E0" in MSTAGES else 0):
            if i >= 4:
                me.wait(self.W["vector"], self.mk.get(f"e0xn{i-4}", 0))
            self.mk[f"ld{i}"] = me.dma_start(out=self.xt(i), in_=xs[i])
        for sg in range(NSEG if "E2" in MSTAGES else 0):
            for tl in range(SEGTILES):
                i = sg * SEGTILES + tl
                if i < 4:
                    me.wait(self.W["vector"], self.mk.get(f"e0xn{NT-1}", 0))
                else:
                    me.wait(self.W["vector"], self.mk.get(f"xn{i-4}", 0))
                self.mk[f"ld2_{i}"] = me.dma_start(out=self.xt(i), in_=xs[i])
            for tl in range(SEGTILES):
                i = sg * SEGTILES + tl
                me.wait(self.W["vector"], self.mk.get(f"xn{i}", 0))
                self.mk[f"st{i}"] = me.dma_start(out=xs[i], in_=self.xnew(i))
        me.wait_self_all()

    def emit_scalar(self, E=None):
        c, me = self.ctx, self.W["scalar"]
        me.wait(self.W["sync"], self.mk.get("p_alog", 0))
        me.activation(c.sb["asb"][:].rearrange("p a b -> p (a b)"),
                      c.sb["alog"][:].rearrange("p a b -> p (a b)"), AF.Exp)
        self.mk["aexp"] = me.mark()
        for dcz in range(16 if "E1" in MSTAGES else 0):
            me.wait(self.W["vector"], self.mk.get(("cv4", dcz), 0))
            if dcz > 0:
                me.wait(self.W["gpsimd"], self.mk.get(("cgw", dcz - 1), 0))
            b = (c.sb["bx"] if dcz < 8 else c.sb["bz"])[:, dcz % 8:dcz % 8 + 1]
            me.activation(c.sb["cgout"][:], c.sb["ct2"][:], AF.Gelu, bias=b)
            self.mk[("gel", dcz)] = me.mark()
        for sg in range(NSEG if ("E2" in MSTAGES and "E2X" in MSTAGES) else 0):
            for dc in range(8):
                me.wait(self.W["tensor"], self.mk.get(("dtp", sg, dc), 0))
                bcol = c.sb["bdt"][:, dc:dc + 1]
                me.activation(c.sb["spa"][:], c.ptf[:, 0:SEGT], AF.Abs, bias=bcol)
                me.activation(c.sb["spa"][:], c.sb["spa"][:], AF.Exp, scale=-1.0)
                me.activation(c.sb["spa"][:], c.sb["spa"][:], AF.Ln, bias=1.0)
                self.mk[("spl", sg, dc)] = me.mark()
                me.wait(self.W["vector"], self.mk.get(("spd", sg, dc), 0))
            for n in range(16 if "E2SCAN" in MSTAGES else 0):
                for dc in range(8):
                    if dc > 0:
                        pk = ("dAfree", sg, n, dc - 1)
                    elif n > 0:
                        pk = ("dAfree", sg, n - 1, 7)
                    else:
                        pk = ("dAfree", sg - 1, 15, 7)
                    me.wait(self.W["vector"], self.mk.get(pk, 0))
                    me.activation(c.sb["dA"][:], c.sb["delta"][:, dc, :], AF.Exp,
                                  scale=c.sb["asb"][:, dc, n:n + 1])
                    self.mk[("dA", sg, n, dc)] = me.mark()
            for tl in range(SEGTILES if "E2OUT" in MSTAGES else 0):
                i = sg * SEGTILES + tl
                me.wait(self.W["vector"], self.mk.get(f"xn{i}", 0))
                me.activation(c.sb["sqscr"][:, 0:DM], self.xnew(i), AF.Square,
                              accum_out=c.sb["stats_sq" + self.so][:, i:i + 1])
                self.mk[f"sq{i}"] = me.mark()
        if "E2" in MSTAGES:
            me.wait(self.W["vector"], self.mk.get("var", 0))
            me.activation(c.sb["tmpB"][:, 0:NT], c.sb["tmpA"][:, 0:NT], AF.Ln,
                          bias=EPS)
            me.activation(c.sb["stats_r" + self.so][:, 0:NT],
                          c.sb["tmpB"][:, 0:NT], AF.Exp, scale=-0.5)

    def emit_vector(self, E=None):
        c, me = self.ctx, self.W["vector"]
        me.memset(c.sb["carry"][:], 0.0)
        me.wait(self.W["scalar"], self.mk.get("aexp", 0))
        me.tensor_scalar(c.sb["asb"][:].rearrange("p a b -> p (a b)"),
                         c.sb["asb"][:].rearrange("p a b -> p (a b)"),
                         -1.0, None, ALU.mult)
        self.mk["aneg"] = me.mark()
        for sg in range(NSEG if "E0" in MSTAGES else 0):
            for tl in range(SEGTILES):
                i = sg * SEGTILES + tl
                me.wait(self.W["sync"], self.mk.get(f"ld{i}", 0))
                me.tensor_scalar(c.sb["m_xnbf"][:], self.xt(i),
                                 c.sb["stats_m" + self.si][:, i:i + 1],
                                 c.sb["stats_r" + self.si][:, i:i + 1],
                                 ALU.subtract, ALU.mult)
                self.mk[f"e0xn{i}"] = me.mark()
                me.wait(self.W["tensor"], self.mk.get(f"e0tr{i}", 0))
                if tl == 0 and sg > 0:
                    me.wait(self.W["tensor"], self.mk.get(("xzmm", sg - 1, 15), 0))
                me.tensor_copy(c.sb["xnTseg"][:, :, tl * 128:(tl + 1) * 128],
                               c.pgu_bf[:, 0:1024].rearrange(
                                   "p (a b) -> p a b", b=128))
                self.mk[f"e0t{i}"] = me.mark()
            for dc in range(16):
                me.wait(self.W["tensor"], self.mk.get(("xzmm", sg, dc), 0))
                if dc >= 2:
                    me.wait(self.W["gpsimd"], self.mk.get(("xrw", sg, dc - 2), 0))
                elif sg > 0:
                    me.wait(self.W["gpsimd"], self.mk.get(("xrw", sg - 1, 14 + dc), 0))
                me.tensor_copy(
                    c.sb["xrstage"][:, (dc % 2) * SEGT:(dc % 2) * SEGT + SEGT],
                    c.pgu[:, (dc % 2) * 512:(dc % 2) * 512 + 512])
                self.mk[("xrev", sg, dc)] = me.mark()
        me.wait_dma_upto(self.W["gpsimd"], self.mk.get("e0_dn", 0))
        me.memset(c.sb["cbuf"][:], 0.0)
        self.mk["cbz"] = me.mark()
        self.mk[("cvz", 0)] = self.mk["cbz"]
        for dcz in range(16 if "E1" in MSTAGES else 0):
            me.wait(self.W["gpsimd"], self.mk.get(("cvl", dcz), 0))
            kk = (c.sb["kx"] if dcz < 8 else c.sb["kz"])
            dc8 = dcz % 8
            me.tensor_single_scalar(c.sb["ct1"][:], c.sb["cbuf"][:, 0:S],
                                    kk[:, dc8, 0:1], ALU.mult)
            me.scalar_tensor_tensor(c.sb["ct2"][:], c.sb["cbuf"][:, 1:S + 1],
                                    kk[:, dc8, 1:2], c.sb["ct1"][:],
                                    ALU.mult, ALU.add)
            me.scalar_tensor_tensor(c.sb["ct1"][:], c.sb["cbuf"][:, 2:S + 2],
                                    kk[:, dc8, 2:3], c.sb["ct2"][:],
                                    ALU.mult, ALU.add)
            me.scalar_tensor_tensor(c.sb["ct2"][:], c.sb["cbuf"][:, 3:S + 3],
                                    kk[:, dc8, 3:4], c.sb["ct1"][:],
                                    ALU.mult, ALU.add)
            self.mk[("cv4", dcz)] = me.mark()
            me.wait(self.W["scalar"], self.mk.get(("gel", dcz), 0))
            self.mk[("cvz", dcz + 1)] = me.mark()
        for sg in range(NSEG if "E2" in MSTAGES else 0):
            if "E2X" not in MSTAGES:
                continue
            me.wait(self.W["tensor"], self.mk.get(("xdbl", sg), 0))
            me.tensor_copy(c.sb["dtlow"][0:64, :], c.pgu[0:64, 0:512])
            me.tensor_copy(c.sb["Bsb"][0:16, :], c.pgu[0:16, 512:1024])
            me.tensor_copy(c.sb["Csb"][0:16, :], c.pgu[0:16, 1536:2048])
            self.mk[("bc_ev", sg)] = me.mark()
            for dc in range(8):
                me.wait(self.W["scalar"], self.mk.get(("spl", sg, dc), 0))
                me.tensor_scalar(c.sb["spr"][:], c.ptf[:, 0:SEGT],
                                 c.sb["bdt"][:, dc:dc + 1], 0.0,
                                 ALU.add, ALU.max)
                me.tensor_tensor(c.sb["delta"][:, dc, :], c.sb["spa"][:],
                                 c.sb["spr"][:], ALU.add)
                self.mk[("spd", sg, dc)] = me.mark()
            me.wait(self.W["gpsimd"], self.mk.get(("ubf", sg), 0))
            for dc in range(8):
                me.tensor_tensor(c.sb["du"][:, dc, :], c.sb["delta"][:, dc, :],
                                 c.sb["ubf"][:, dc, :], ALU.mult)
            self.mk[("du", sg)] = me.mark()
            for n in range(16 if "E2SCAN" in MSTAGES else 0):
                me.wait(self.W["gpsimd"], self.mk.get(("bbc", sg, n), 0))
                me.wait(self.W["gpsimd"], self.mk.get(("cbc", sg, n), 0))
                for dc in range(8):
                    me.wait(self.W["scalar"], self.mk.get(("dA", sg, n, dc), 0))
                    me.tensor_tensor(c.sb["dBu"][:], c.sb["du"][:, dc, :],
                                     c.sb["bbc"][:], ALU.mult)
                    nc_ = sg
                    me.op("tensor_tensor_scan", c.sb["hbuf"][:], c.sb["dA"][:],
                          c.sb["dBu"][:], c.sb["carry"][:, dc * 16 + n:
                                                        dc * 16 + n + 1],
                          ALU.mult, ALU.add)
                    self.mk[("dAfree", sg, n, dc)] = me.mark()
                    me.tensor_copy(c.sb["carry"][:, dc * 16 + n:dc * 16 + n + 1],
                                   c.sb["hbuf"][:, SEGT - 1:SEGT])
                    me.tensor_tensor(c.sb["hcb"][:], c.sb["hbuf"][:],
                                     c.sb["cbc"][:], ALU.mult)
                    if n == 0:
                        me.tensor_copy(c.sb["y"][:, dc, :], c.sb["hcb"][:])
                    else:
                        me.tensor_tensor(c.sb["y"][:, dc, :], c.sb["y"][:, dc, :],
                                         c.sb["hcb"][:], ALU.add)
                self.mk[("nfree", sg, n)] = me.mark()
            for dc in range(8):
                me.scalar_tensor_tensor(c.sb["y"][:, dc, :], c.sb["ubf"][:, dc, :],
                                        c.sb["dsb"][:, dc:dc + 1],
                                        c.sb["y"][:, dc, :], ALU.mult, ALU.add)
                me.tensor_copy(c.sb["ybf"][:, dc, :], c.sb["y"][:, dc, :])
            me.wait(self.W["gpsimd"], self.mk.get(("zbf", sg), 0))
            self.mk[("ydone", sg)] = me.mark()
            for tl in range(SEGTILES if "E2OUT" in MSTAGES else 0):
                i = sg * SEGTILES + tl
                me.wait(self.W["sync"], self.mk.get(f"ld2_{i}", 0))
                me.wait(self.W["tensor"], self.mk.get(("omm", sg, tl), 0))
                if i >= 2:
                    me.wait(self.W["sync"], self.mk.get(f"st{i-2}", 0))
                    me.wait(self.W["scalar"], self.mk.get(f"sq{i-2}", 0))
                me.op("_custom_dve", AFFINE_THEN_ADD, out=self.xnew(i),
                      in0=c.ptf[:, 512:1536], in1=self.xt(i), s0=1.0, s1=0.0)
                self.mk[f"xn{i}"] = me.mark()
                me.reduce_sum(c.sb["msc"][:, 0:1], self.xnew(i), AX.X)
                me.tensor_scalar(c.sb["stats_m" + self.so][:, i:i + 1],
                                 c.sb["msc"][:, 0:1], 1.0 / DM, None, ALU.mult)
            self.mk[("useg_done", sg)] = me.mark()
        if "E2" in MSTAGES:
            me.wait(self.W["scalar"], self.mk.get(f"sq{NT-1}", 0))
            me.tensor_scalar(c.sb["tmpA"][:, 0:NT],
                             c.sb["stats_sq" + self.so][:, 0:NT],
                             1.0 / DM, None, ALU.mult)
            me.tensor_tensor(c.sb["tmpB"][:, 0:NT],
                             c.sb["stats_m" + self.so][:, 0:NT],
                             c.sb["stats_m" + self.so][:, 0:NT], ALU.mult)
            me.tensor_tensor(c.sb["tmpA"][:, 0:NT], c.sb["tmpA"][:, 0:NT],
                             c.sb["tmpB"][:, 0:NT], ALU.subtract)
            self.mk["var"] = me.mark()

    def emit_tensor(self, E=None):
        c, me = self.ctx, self.W["tensor"]
        me.wait(self.W["gpsimd"], self.mk.get("win", 0))
        for sg in range(NSEG if "E0" in MSTAGES else 0):
            for tl in range(SEGTILES):
                i = sg * SEGTILES + tl
                me.wait(self.W["vector"], self.mk.get(f"e0xn{i}", 0))
                if i > 0:
                    me.wait(self.W["vector"], self.mk.get(f"e0t{i-1}", 0))
                for cc in range(8):
                    me.transpose(c.pgu_bf[:, cc * 128:(cc + 1) * 128],
                                 c.sb["m_xnbf"][:, cc * 128:(cc + 1) * 128],
                                 c.sb["identbf"][:])
                self.mk[f"e0tr{i}"] = me.mark()
            me.wait(self.W["vector"], self.mk.get(f"e0t{sg * SEGTILES + SEGTILES - 1}", 0))
            for dc in range(16):
                if dc >= 2:
                    me.wait(self.W["vector"], self.mk.get(("xrev", sg, dc - 2), 0))
                elif sg > 0:
                    me.wait(self.W["vector"], self.mk.get(("xrev", sg - 1, 14 + dc), 0))
                for cc in range(8):
                    me.matmul(c.pgu[:, (dc % 2) * 512:(dc % 2) * 512 + 512],
                              c.sb["Win_b"][:, cc, dc * 128:(dc + 1) * 128],
                              c.sb["xnTseg"][:, cc, :],
                              start=(cc == 0), stop=(cc == 7))
                self.mk[("xzmm", sg, dc)] = me.mark()
        me.wait(self.W["gpsimd"], self.mk.get("wxp", 0))
        me.wait(self.W["gpsimd"], self.mk.get("wdt", 0))
        me.wait(self.W["gpsimd"], self.mk.get("wout", 0))
        for sg in range(NSEG if ("E2" in MSTAGES and "E2X" in MSTAGES) else 0):
            me.wait(self.W["gpsimd"], self.mk.get(("ubf", sg), 0))
            if sg > 0:
                me.wait(self.W["vector"], self.mk.get(("useg_done", sg - 1), 0))
            for (o0, w, bank) in ((0, 64, 0), (64, 16, 512), (80, 16, 1536)):
                for cc in range(8):
                    me.matmul(c.pgu[0:w, bank:bank + SEGT],
                              c.sb["Wxp_b"][:, cc, o0:o0 + w],
                              c.sb["ubf"][:, cc, :],
                              start=(cc == 0), stop=(cc == 7))
            self.mk[("xdbl", sg)] = me.mark()
            me.wait(self.W["vector"], self.mk.get(("bc_ev", sg), 0))
            for dc in range(8):
                if dc > 0:
                    me.wait(self.W["vector"], self.mk.get(("spd", sg, dc - 1), 0))
                    me.wait(self.W["scalar"], self.mk.get(("spl", sg, dc - 1), 0))
                me.matmul(c.ptf[:, 0:SEGT], c.sb["Wdt_b"][0:64, dc, :],
                          c.sb["dtlow"][0:64, :], start=True, stop=True)
                self.mk[("dtp", sg, dc)] = me.mark()
            me.wait(self.W["vector"], self.mk.get(("ydone", sg), 0))
            me.wait(self.W["scalar"], self.mk.get(("spl", sg, 7), 0))
            me.wait(self.W["vector"], self.mk.get(("spd", sg, 7), 0))
            for tl in range(SEGTILES if "E2OUT" in MSTAGES else 0):
                if not (sg == 0 and tl == 0):
                    pi = sg * SEGTILES + tl - 1
                    me.wait(self.W["vector"], self.mk.get(f"xn{pi}", 0))
                for dcz in range(16):
                    lhs = (c.sb["ybf"] if dcz < 8 else c.sb["zbf"])[
                        :, dcz % 8, tl * 128:(tl + 1) * 128]
                    for half in range(2):
                        me.matmul(c.ptf[:, 512 + half * 512:512 + half * 512 + 512],
                                  lhs, c.sb["Wout_b"][:, dcz,
                                                      half * 512:half * 512 + 512],
                                  start=(dcz == 0), stop=(dcz == 15))
                self.mk[("omm", sg, tl)] = me.mark()




def make_phases(ctx, names):
    out = []
    cur = "0"
    for pn in names:
        nxt = "1" if cur == "0" else "0"
        if pn == "S0":
            out.append(StatsPrepass(ctx))
            continue
        elif pn == "W0":
            out.append(FFNWeightPrep(ctx))
            continue
        elif pn == "A":
            out.append(FFNPhase(ctx, cur, nxt, "ffa"))
        elif pn == "B":
            out.append(AttnPhase(ctx, cur, nxt, "atb"))
        elif pn == "C":
            out.append(FFNPhase(ctx, cur, nxt, "ffc", load_stash=True))
        elif pn == "WM":
            out.append(MoEWeightPrep(ctx))
            continue
        elif pn == "D":
            out.append(MoEPhase(ctx, cur, nxt, "moed"))
        elif pn == "F":
            out.append(MoEPhase(ctx, cur, nxt, "moef"))
        elif pn == "E":
            out.append(MambaPhase(ctx, cur, nxt, "mam"))
        else:
            raise ValueError(pn)
        cur = nxt
    return out


def build_nc(phase_names):
    nc = bass.Bass("TRN2", target_bir_lowering=False)
    es = ExitStack()
    c = Ctx(nc, es)

    c.dram_t("x_in", [S, DM], F32, "ExternalInput")
    c.dram_t("x_state", [S, DM], F32, "ExternalOutput")
    for nm, shp in [("w_gate", [DFF, DM]), ("w_up", [DFF, DM]),
                    ("w_down", [DM, DFF]),
                    ("d_ident", [128, 128]), ("d_cst", [128, 8])]:
        c.dram_t(nm, shp, F32, "ExternalInput")
    for nm in ("WqT", "WkT", "WvT", "WfcT"):
        c.dram_t(nm, [DM, DM], F32, "ExternalInput")
    c.dram_t("keysR", [128, NEXP * 1024], F32, "ExternalInput")
    c.dram_t("valuesR", [128, NEXP * 1024], F32, "ExternalInput")
    c.dram_t("selT", [DM, 64], F32, "ExternalInput")
    c.dram_t("values_st", [128, NEXP * 1024], BF16, "Internal")
    c.dram_t("keys_st", [128, NEXP * 1024], BF16, "Internal")
    c.dram_t("WinR", [128, 16384], F32, "ExternalInput")
    c.dram_t("WoutR", [128, 16384], F32, "ExternalInput")
    c.dram_t("WxpR", [128, 768], F32, "ExternalInput")
    c.dram_t("WdtR", [64, 1024], F32, "ExternalInput")
    c.dram_t("KxR", [128, 32], F32, "ExternalInput")
    c.dram_t("KzR", [128, 32], F32, "ExternalInput")
    c.dram_t("bxR", [128, 8], F32, "ExternalInput")
    c.dram_t("bzR", [128, 8], F32, "ExternalInput")
    c.dram_t("AlogR", [128, 128], F32, "ExternalInput")
    c.dram_t("DR", [128, 8], F32, "ExternalInput")
    c.dram_t("bdtR", [128, 8], F32, "ExternalInput")
    c.dram_t("xraw_st", [128, 16 * S], F32, "Internal")
    c.dram_t("u_st", [128, 8 * S], BF16, "Internal")
    c.dram_t("z_st", [128, 8 * S], BF16, "Internal")
    c.dram_t("bc_st", [16, SEGT], F32, "Internal")
    c.dram_t("cc_st", [16, SEGT], F32, "Internal")
    c.dram_t("WgqT_st", [128, 8 * DFF], BF16, "Internal")
    c.dram_t("WuqT_st", [128, 8 * DFF], BF16, "Internal")
    c.dram_t("WdqT_st", [128, 22 * DM], BF16, "Internal")

    c.sbuf("cst", [128, 8], F32)
    c.sbuf("identbf", [128, 128], BF16)
    c.sbuf("onesbf", [128, 128], BF16)
    c.sbuf("xt", [128, 2 * DM], F32)
    c.sbuf("xnew", [128, 2 * DM], F32)
    c.sbuf("msc", [128, 4], F32)
    c.sbuf("sqscr", [128, DM], F32)
    c.sbuf("tmpA", [128, NT], F32)
    c.sbuf("tmpB", [128, NT], F32)
    c.sbuf("mk2", [128, 16], F32)
    for t in ("0", "1"):
        c.sbuf("stats_m" + t, [128, NT], F32)
        c.sbuf("stats_sq" + t, [128, NT], F32)
        c.sbuf("stats_r" + t, [128, NT], F32)
    arena = c.sbuf("arena", [128, 186 * 1024], mybir.dt.uint8)
    arv = arena[:]

    def carve(name, off, shape, dtype):
        n = int(np.prod(shape[1:]))
        ap = arv[:, off:off + n * mybir.dt.size(dtype)].bitcast(dtype)
        if len(shape) == 3:
            ap = ap.rearrange("p (a b) -> p a b", b=shape[2])
        elif len(shape) == 4:
            ap = ap.rearrange("p (a b cd) -> p a b cd", b=shape[2], cd=shape[3])
        c.sb[name] = ap
        return ap

    carve("Wg", 0, [128, 8 * DFF], BF16)
    carve("Wu", 45056, [128, 8 * DFF], BF16)
    carve("Wd", 90112, [128, 22 * DM], BF16)
    carve("wtile", 135168, [128, DFF], F32)
    carve("xqbf", 146432, [128, DM], BF16)
    carve("xqT", 148480, [128, DM], BF16)
    carve("hqbf", 150528, [128, DFF], BF16)
    carve("hqT", 156160, [128, DFF], BF16)
    carve("sgate", 161792, [128, 512], F32)
    carve("hpf", 163840, [128, 512], F32)
    carve("xc", 165888, [128, DM], F32)
    carve("q_a", 169984, [128, DM], F32)
    carve("q_xn", 174080, [128, DM], F32)
    carve("q_g2", 178176, [128, DM], F32)
    carve("q_sg", 182272, [128, DM], F32)
    carve("q_bm", 186368, [128, 64], F32)
    carve("q_scl", 186624, [128, 64], F32)
    carve("q_rec", 186880, [128, 64], F32)
    carve("q_sclF", 187136, [128, 64], F32)
    carve("Wat", 0, [128, 4 * 8 * 1024], BF16)
    carve("xnbf", 65536, [128, DM], BF16)
    carve("xnT", 67584, [128, DM], BF16)
    carve("qbf", 69632, [128, DM], BF16)
    carve("kbf", 71680, [128, DM], BF16)
    carve("vbf", 73728, [128, DM], BF16)
    carve("abf", 75776, [128, DM], BF16)
    carve("aT", 77824, [128, DM], BF16)
    carve("att_m4", 79872, [128, 4096], BF16)
    carve("att_sc", 88064, [128, 256], F32)
    carve("att_E", 89088, [128, 256], F32)
    carve("att_A", 90112, [128, 256], BF16)
    carve("att_mx", 90624, [128, 16], F32)
    carve("att_sm", 90688, [128, 16], F32)
    carve("att_rc", 90752, [128, 16], F32)
    carve("aof", 91136, [128, DM], F32)

    carve("keysb", 0, [128, 8, 8192], BF16)
    carve("m_xt", 131072, [128, 2, DM], F32)
    carve("xnTs", 139264, [128, 8, 768], BF16)
    carve("outac", 151552, [128, 6, DM], BF16)
    carve("vbuf", 163840, [128, 4, 2048], BF16)
    carve("scb", 180224, [128, 2, 512], BF16)
    carve("scT", 182272, [128, 2, 512], BF16)
    carve("gates_bf", 184320, [128, 2, 384], BF16)
    carve("selsb", 185856, [128, 2, 64], F32)
    carve("selw", 186368, [128, 64], F32)
    carve("selm", 186624, [128, 64], F32)
    carve("selTb", 186880, [128, 8, 64], BF16)
    carve("m_xnbf", 187904, [128, DM], BF16)
    carve("wmA", 0, [128, 2, 8192], F32)
    carve("wmB", 65536, [128, 2, 8192], BF16)

    carve("Win_b", 0, [128, 8, 2048], BF16)
    carve("Wout_b", 32768, [128, 16, 1024], BF16)
    carve("Wxp_b", 65536, [128, 8, 96], BF16)
    carve("Wdt_b", 67072, [128, 8, 128], BF16)
    carve("kx", 69120, [128, 8, 4], F32)
    carve("kz", 69248, [128, 8, 4], F32)
    carve("bx", 69376, [128, 8], F32)
    carve("bz", 69440, [128, 8], F32)
    carve("alog", 69504, [128, 8, 16], F32)
    carve("asb", 70016, [128, 8, 16], F32)
    carve("dsb", 70528, [128, 8], F32)
    carve("bdt", 70560, [128, 8], F32)
    carve("carry", 70592, [128, 128], F32)
    carve("bbc", 71104, [128, SEGT], F32)
    carve("cbc", 73152, [128, SEGT], F32)
    carve("dtlow", 75200, [128, SEGT], BF16)
    carve("Bsb", 76224, [128, SEGT], F32)
    carve("Csb", 78272, [128, SEGT], F32)
    carve("spa", 80320, [128, SEGT], F32)
    carve("spr", 82368, [128, SEGT], F32)
    carve("dA", 84416, [128, SEGT], F32)
    carve("dBu", 86464, [128, SEGT], F32)
    carve("hbuf", 88512, [128, SEGT], F32)
    carve("hcb", 90560, [128, SEGT], F32)
    carve("ubf", 92608, [128, 8, SEGT], BF16)
    carve("zbf", 100800, [128, 8, SEGT], BF16)
    carve("ybf", 108992, [128, 8, SEGT], BF16)
    carve("delta", 117184, [128, 8, SEGT], F32)
    carve("du", 133568, [128, 8, SEGT], F32)
    carve("y", 149952, [128, 8, SEGT], F32)
    carve("e_xt", 166336, [128, 4, DM], F32)
    carve("xnTseg", 92608, [128, 8, SEGT], BF16)
    carve("xrstage", 100800, [128, 2 * SEGT], F32)
    carve("ct1", 80320, [128, S], F32)
    carve("ct2", 96704, [128, S], F32)
    carve("cbuf", 113088, [128, 4112], F32)
    carve("cgout", 129536, [128, S], BF16)

    pt = c.psum("PT", [128, 2048], F32)
    c.pt_bf = pt[:].bitcast(BF16)
    c.ptf = pt[:]
    pgu = c.psum("PGU", [128, 2048], F32)
    c.pgu = pgu[:]
    c.pgu_bf = pgu[:].bitcast(BF16)

    cstv = [0.0, 1.0, 0.5, -0.5, -1.0, 0.125, EPS, 1.75]
    for k, v in enumerate(cstv):
        nc.const_aps.register(F32, v, c.sb["cst"][:, k:k + 1])

    c.EW = {}
    for n in ("sync", "scalar", "vector", "tensor", "gpsimd"):
        dsems = [c.sem(f"d_{n}_{s}") for s in range(NSLOT)] \
            if n in ("sync", "gpsimd") else []
        c.EW[n] = W(n, c.sem(f"c_{n}"), dsems,
                    serial=(n in ("vector", "scalar", "gpsimd")))
    phases = make_phases(c, phase_names)
    PB = c.sem("phase_barrier")
    cst_sem = c.sem("cst_load")
    id_sem = c.sem("ident_load")

    counts = {}
    for ph in phases:
        ph.emit_sync()
        ph.emit_scalar()
        ph.emit_vector()
        ph.emit_tensor()
        ph.emit_gpsimd()
        for n in ph.W:
            counts[(ph.tag, n)] = (ph.W[n].n, ph.W[n].dn)
    for w in c.EW.values():
        w.reset()
    marksnap = [dict(ph.mk) for ph in phases]

    def run_phases(eng, name, emit_name):
        wrap = c.EW[name].bind(eng)
        for k, ph in enumerate(phases):
            eng.wait_ge(PB, 5 * k)
            getattr(ph, emit_name)()
            assert (wrap.n, wrap.dn) == counts[(ph.tag, name)], (
                ph.tag, name, wrap.n, wrap.dn, counts[(ph.tag, name)])
            wrap.wait_self_all()
            eng.sem_inc(PB, 1)
        wrap.unbind()

    with nc.Block() as block:
        @block.sync
        def _(eng):
            eng.dma_start(out=c.sb["cst"][:], in_=c.dram["d_cst"]).then_inc(cst_sem, 16)
            eng.wait_ge(cst_sem, 16)
            eng.wait_ge(id_sem, 16)
            run_phases(eng, "sync", "emit_sync")

        @block.scalar
        def _(eng):
            eng.wait_ge(cst_sem, 16)
            eng.wait_ge(id_sem, 16)
            run_phases(eng, "scalar", "emit_scalar")

        @block.vector
        def _(eng):
            eng.wait_ge(cst_sem, 16)
            eng.wait_ge(id_sem, 16)
            run_phases(eng, "vector", "emit_vector")

        @block.tensor
        def _(eng):
            eng.wait_ge(cst_sem, 16)
            eng.wait_ge(id_sem, 16)
            run_phases(eng, "tensor", "emit_tensor")

        @block.gpsimd
        def _(eng):
            eng.dma_start(out=c.sb["identbf"][:], in_=c.dram["d_ident"]).then_inc(id_sem, 16)
            eng.wait_ge(cst_sem, 16)
            run_phases(eng, "gpsimd", "emit_gpsimd")

    for ph, snap in zip(phases, marksnap):
        if ph.mk != snap:
            diffs = [(k, snap.get(k), ph.mk.get(k))
                     for k in set(snap) | set(ph.mk)
                     if snap.get(k) != ph.mk.get(k)]
            raise AssertionError(f"mark drift in {ph.tag}: {diffs[:8]}")
    mybir.codegen_inst_isa_subclasses(nc)
    return nc, es


def prep_host_ffn(inputs):
    f32 = np.float32
    x = np.ascontiguousarray(inputs["L0_attn_x"], dtype=f32)
    shared = {
        "w_gate": np.ascontiguousarray(inputs["L0_ffn_w_gate"], f32),
        "w_up": np.ascontiguousarray(inputs["L0_ffn_w_up"], f32),
        "w_down": np.ascontiguousarray(inputs["L0_ffn_w_down"], f32),
        "WqT": np.ascontiguousarray(inputs["L0_attn_Wq"].T, f32),
        "WkT": np.ascontiguousarray(inputs["L0_attn_Wk"].T, f32),
        "WvT": np.ascontiguousarray(inputs["L0_attn_Wv"].T, f32),
        "WfcT": np.ascontiguousarray(inputs["L0_attn_Wfc"].T, f32),
        "keysR": np.ascontiguousarray(
            inputs["L1_ffn_keys"].reshape(NEXP, 8, 128, 128).transpose(2, 1, 0, 3)
            .reshape(128, NEXP * 1024), f32),
        "valuesR": np.ascontiguousarray(
            inputs["L1_ffn_values"].transpose(1, 0, 2).reshape(128, NEXP * 1024), f32),
        "selT": np.ascontiguousarray(inputs["L1_ffn_expert_sel"].T, f32),
        "WinR": np.ascontiguousarray(
            inputs["L1_attn_W_in"].reshape(8, 128, 2048).transpose(1, 0, 2)
            .reshape(128, 16384), f32),
        "WoutR": np.ascontiguousarray(
            inputs["L1_attn_W_out"].reshape(16, 128, 1024).transpose(1, 0, 2)
            .reshape(128, 16384), f32),
        "WxpR": np.ascontiguousarray(
            inputs["L1_attn_W_xproj"].reshape(8, 128, 96).transpose(1, 0, 2)
            .reshape(128, 768), f32),
        "WdtR": np.ascontiguousarray(inputs["L1_attn_W_dt"], f32),
        "KxR": np.ascontiguousarray(
            inputs["L1_attn_K_x"].reshape(8, 128, 4).transpose(1, 0, 2)
            .reshape(128, 32), f32),
        "KzR": np.ascontiguousarray(
            inputs["L1_attn_K_z"].reshape(8, 128, 4).transpose(1, 0, 2)
            .reshape(128, 32), f32),
        "bxR": np.ascontiguousarray(
            inputs["L1_attn_b_x"].reshape(8, 128).T, f32),
        "bzR": np.ascontiguousarray(
            inputs["L1_attn_b_z"].reshape(8, 128).T, f32),
        "AlogR": np.ascontiguousarray(
            inputs["L1_attn_A_log"].reshape(8, 128, 16).transpose(1, 0, 2)
            .reshape(128, 128), f32),
        "DR": np.ascontiguousarray(
            inputs["L1_attn_D"].reshape(8, 128).T, f32),
        "bdtR": np.ascontiguousarray(
            inputs["L1_attn_b_dt"].reshape(8, 128).T, f32),
        "d_ident": np.eye(128, dtype=f32),
        "d_cst": np.tile(np.array([0.0, 1.0, 0.5, -0.5, -1.0, 0.125, EPS, 1.75],
                                  f32), (128, 1)),
    }
    in_maps = []
    for cix in range(x.shape[0]):
        m = dict(shared)
        m["x_in"] = np.ascontiguousarray(x[cix])
        in_maps.append(m)
    return in_maps


def run_ffn_test(inputs, ncores=8):
    in_maps = prep_host_ffn(inputs)[:ncores]
    nc, es = build_nc(["S0", "W0", "A"])
    try:
        res = run_bass_kernel_spmd(nc, in_maps, core_ids=list(range(ncores)))
    finally:
        es.close()
    return np.stack([r["x_state"] for r in res.results])


def run_attn_test(inputs, x_prev, ncores=8):
    in_maps = prep_host_ffn(inputs)[:ncores]
    for cix in range(ncores):
        in_maps[cix]["x_in"] = np.ascontiguousarray(x_prev[cix].astype(np.float32))
    nc, es = build_nc(["S0", "B"])
    try:
        res = run_bass_kernel_spmd(nc, in_maps, core_ids=list(range(ncores)))
    finally:
        es.close()
    return np.stack([r["x_state"] for r in res.results])


FULL_CHAIN = ["S0", "W0", "A", "WM", "B", "C", "D", "E", "F"]


def kernel(**inputs):
    in_maps = prep_host_ffn(inputs)
    nc, es = build_nc(FULL_CHAIN)
    try:
        res = run_bass_kernel_spmd(nc, in_maps, core_ids=list(range(8)))
    finally:
        es.close()
    return np.stack([r["x_state"] for r in res.results]).astype(np.float32)


# version 05
# speedup vs baseline: 1.5359x; 1.0327x over previous
import numpy as np
import ml_dtypes
from contextlib import ExitStack

import concourse.bass as bass
import concourse.mybir as mybir
from concourse.bass_utils import run_bass_kernel_spmd
from concourse.dve_op import DveOp
from concourse.dve_spec import Spec, Src0, Src1, C0, C1, C2, C3, lower as dve_lower
from concourse.dve_table_gen import dve_ver_for
import concourse.dve_ops as dve_ops_mod
from concourse.dve_ops import AFFINE_THEN_ADD

F32 = mybir.dt.float32
BF16 = mybir.dt.bfloat16
I32 = mybir.dt.int32
AF = mybir.ActivationFunctionType
ALU = mybir.AluOpType
AX = mybir.AxisListType

_VER = dve_ver_for("TRN2")
DEBUG_ATTN = False


def _reg_op(name, spec, reference):
    for op in dve_ops_mod.OPS:
        if op.name == name:
            return op
    sha = dve_lower(spec, ver=_VER, perf_en=False, name=name).sha(_VER)
    op = DveOp(name, spec, False, {_VER: sha}, reference=reference)
    dve_ops_mod.OPS.append(op)
    dve_ops_mod._SUB_OPCODE_FOR_NAME[name] = (
        dve_ops_mod._CUSTOM_DVE_ROW_BASE + len(dve_ops_mod.OPS) - 1)
    dve_ops_mod.CUSTOM_DVE_REFS[name] = op.ref
    return op


SNAP1 = _reg_op(
    "ANT_SNAP1",
    Spec(body=((Src0 > C0) + (Src0 > C1)) + ((Src0 > C2) + (Src0 > C3))),
    lambda in0, in1, s0, s1, imm2: (
        (in0 > s0).astype(np.float64) + (in0 > s1) + (in0 > imm2) + (in0 > in1)),
)


def _snap2_spec():
    c5 = Src0 > C0
    c6 = Src0 > C1
    n7 = Src0 > C2
    t3 = (c5 + c6) + (n7 + n7)
    return Spec(body=Src1 + (t3 + t3))


SNAP2 = _reg_op(
    "ANT_SNAP2",
    _snap2_spec(),
    lambda in0, in1, s0, s1, imm2: in1 + 2.0 * (
        (in0 > s0).astype(np.float64) + (in0 > s1) + 2.0 * (in0 > imm2)),
)

S = 4096
DM = 1024
NT = S // 128
DFF = 2816
EPS = 1e-5
OCS = [(0, 512), (512, 512), (1024, 512), (1536, 512), (2048, 512), (2560, 256)]
NOC = len(OCS)


def bc16(small_ap, nb, k=16):
    return small_ap.unsqueeze(2).broadcast_to([128, nb, k])


def r3(ap, k=16):
    return ap.rearrange("p (nb k) -> p nb k", k=k)


_DMA_METHODS = frozenset({"dma_start", "dma_start_transpose", "indirect_dma_start"})


NSLOT = 6


class W:

    def __init__(self, name, sem, dsems, serial):
        self.name = name
        self.sem = sem
        self.dsems = dsems
        self.serial = serial
        self.eng = None
        self.n = 0
        self.dn = 0

    def bind(self, eng):
        self.eng = eng
        return self

    def reset(self):
        self.eng = None
        self.n = 0
        self.dn = 0

    def unbind(self):
        self.eng = None

    def op(self, meth, *a, **k):
        if meth in _DMA_METHODS:
            slot = self.dn % NSLOT
            if self.eng is not None:
                prev = self.dn // NSLOT
                if prev > 0:
                    self.eng.wait_ge(self.dsems[slot], 16 * prev)
                inst = getattr(self.eng, meth)(*a, **k)
                inst.then_inc(self.dsems[slot], 16)
            self.dn += 1
            return ("D", self.dn)
        if self.eng is not None:
            if self.serial and self.n > 0:
                self.eng.wait_ge(self.sem, self.n)
            inst = getattr(self.eng, meth)(*a, **k)
            inst.then_inc(self.sem, 1)
        self.n += 1
        return self.n

    def __getattr__(self, meth):
        if meth.startswith("_ant"):
            raise AttributeError(meth)
        return lambda *a, **k: self.op(meth, *a, **k)

    def wait(self, other, val):
        if self.eng is None:
            return
        if isinstance(val, tuple):
            m = val[1]
            if m <= 0:
                return
            self.eng.wait_ge(other.dsems[(m - 1) % NSLOT],
                             16 * ((m - 1) // NSLOT + 1))
            return
        if val > 0:
            if other is self and self.serial:
                return
            self.eng.wait_ge(other.sem, val)

    def wait_self_compute(self):
        if self.eng is not None and self.n > 0:
            self.eng.wait_ge(self.sem, self.n)

    def wait_dma_upto(self, other, dn):
        if self.eng is None or not dn:
            return
        for s in range(NSLOT):
            cnt = (dn - s + NSLOT - 1) // NSLOT
            if cnt > 0:
                self.eng.wait_ge(other.dsems[s], 16 * cnt)

    def wait_self_all(self):
        if self.eng is None:
            return
        if self.n > 0:
            self.eng.wait_ge(self.sem, self.n)
        for s in range(NSLOT):
            cnt = (self.dn - s + NSLOT - 1) // NSLOT
            if cnt > 0:
                self.eng.wait_ge(self.dsems[s], 16 * cnt)

    def mark(self):
        return self.n


class Ctx:
    def __init__(self, nc, es):
        self.nc = nc
        self.es = es
        self.sb = {}
        self.dram = {}

    def sbuf(self, name, shape, dtype):
        t = self.es.enter_context(self.nc.sbuf_tensor(name, shape, dtype))
        self.sb[name] = t
        return t

    def psum(self, name, shape, dtype=F32):
        return self.es.enter_context(self.nc.psum_tensor(name, shape, dtype))

    def sem(self, name):
        return self.es.enter_context(self.nc.semaphore(name))

    def dram_t(self, name, shape, dtype, kind):
        ap = self.nc.dram_tensor(name, shape, dtype, kind=kind).ap()
        self.dram[name] = ap
        return ap


class QuantHelper:

    def __init__(self, ctx):
        c = ctx
        self.a = c.sb["q_a"]
        self.xn = c.sb["q_xn"]
        self.g2 = c.sb["q_g2"]
        self.sg = c.sb["q_sg"]
        self.bm = c.sb["q_bm"]
        self.scl = c.sb["q_scl"]
        self.rec = c.sb["q_rec"]
        self.sclF = c.sb["q_sclF"]
        self.c175 = c.sb["cst"][:, 7:8]

    def v_part1(self, v, src, N, rstd_ap=None):
        nb = N // 16
        v.tensor_reduce(self.bm[:, :nb], r3(src), AX.X, ALU.max,
                        apply_absolute_value=True)
        v.tensor_scalar(self.scl[:, :nb], self.bm[:, :nb], 1.0 / 6.0, 1e-12,
                        ALU.mult, ALU.max)
        v.reciprocal(self.rec[:, :nb], self.scl[:, :nb])
        if rstd_ap is not None:
            v.tensor_scalar(self.sclF[:, :nb], self.scl[:, :nb], rstd_ap, 0.5,
                            ALU.mult, ALU.mult)
        else:
            v.tensor_scalar(self.sclF[:, :nb], self.scl[:, :nb], 0.5, None,
                            ALU.mult)

    def s_part(self, sc, src, N):
        sc.activation(self.a[:, :N], src, AF.Abs)
        sc.activation(self.sg[:, :N], src, AF.Sign)

    def v_part2(self, v, dst, N):
        nb = N // 16
        v.tensor_tensor(r3(self.xn[:, :N]), r3(self.a[:, :N]),
                        bc16(self.rec[:, :nb], nb), ALU.mult)
        v.op("_custom_dve", SNAP1, out=self.a[:, :N], in0=self.xn[:, :N],
             in1=self.c175, s0=0.25, s1=0.75, imm2=1.25)
        v.op("_custom_dve", SNAP2, out=self.g2[:, :N], in0=self.xn[:, :N],
             in1=self.a[:, :N], s0=2.5, s1=3.5, imm2=5.0)
        v.tensor_tensor(r3(self.a[:, :N]), r3(self.g2[:, :N]),
                        bc16(self.sclF[:, :nb], nb), ALU.mult)
        v.tensor_tensor(dst, self.a[:, :N], self.sg[:, :N], ALU.mult)


class Phase:
    def __init__(self, ctx, tag):
        self.ctx = ctx
        self.tag = tag
        self.mk = {}
        self.W = ctx.EW

    def emit_sync(self): pass
    def emit_scalar(self): pass
    def emit_vector(self): pass
    def emit_tensor(self): pass
    def emit_gpsimd(self): pass


class StatsPrepass(Phase):
    def __init__(self, ctx):
        super().__init__(ctx, "s0")

    def emit_sync(self):
        c, me = self.ctx, self.W["sync"]
        me.dma_start(out=c.dram["x_state"], in_=c.dram["x_in"])
        xi = c.dram["x_in"].rearrange("(t p) d -> t p d", p=128)
        for i in range(NT):
            if i >= 2:
                me.wait(self.W["vector"], self.mk.get(f"v{i-2}", 0))
                me.wait(self.W["scalar"], self.mk.get(f"s{i-2}", 0))
            self.mk[f"ld{i}"] = me.dma_start(
                out=c.sb["xt"][:, (i % 2) * DM:(i % 2) * DM + DM], in_=xi[i])
        me.wait_self_all()

    def emit_vector(self):
        c, me = self.ctx, self.W["vector"]
        for i in range(NT):
            me.wait(self.W["sync"], self.mk.get(f"ld{i}", 0))
            xt = c.sb["xt"][:, (i % 2) * DM:(i % 2) * DM + DM]
            me.reduce_sum(c.sb["msc"][:, 0:1], xt, AX.X)
            me.tensor_scalar(c.sb["stats_m0"][:, i:i + 1], c.sb["msc"][:, 0:1],
                             1.0 / DM, None, ALU.mult)
            self.mk[f"v{i}"] = me.mark()
        me.wait(self.W["scalar"], self.mk.get(f"s{NT-1}", 0))
        me.tensor_scalar(c.sb["tmpA"][:, 0:NT], c.sb["stats_sq0"][:, 0:NT],
                         1.0 / DM, None, ALU.mult)
        me.tensor_tensor(c.sb["tmpB"][:, 0:NT], c.sb["stats_m0"][:, 0:NT],
                         c.sb["stats_m0"][:, 0:NT], ALU.mult)
        me.tensor_tensor(c.sb["tmpA"][:, 0:NT], c.sb["tmpA"][:, 0:NT],
                         c.sb["tmpB"][:, 0:NT], ALU.subtract)
        self.mk["var"] = me.mark()

    def emit_scalar(self):
        c, me = self.ctx, self.W["scalar"]
        for i in range(NT):
            me.wait(self.W["sync"], self.mk.get(f"ld{i}", 0))
            xt = c.sb["xt"][:, (i % 2) * DM:(i % 2) * DM + DM]
            me.activation(c.sb["sqscr"][:, 0:DM], xt, AF.Square,
                          accum_out=c.sb["stats_sq0"][:, i:i + 1])
            self.mk[f"s{i}"] = me.mark()
        me.wait(self.W["vector"], self.mk.get("var", 0))
        me.activation(c.sb["tmpB"][:, 0:NT], c.sb["tmpA"][:, 0:NT], AF.Ln,
                      bias=EPS)
        me.activation(c.sb["stats_r0"][:, 0:NT], c.sb["tmpB"][:, 0:NT],
                      AF.Exp, scale=-0.5)


class FFNWeightPrep(Phase):
    def __init__(self, ctx):
        super().__init__(ctx, "w0")
        self.specs = [
            ("w_gate", DM, "Wg"),
            ("w_up", DM, "Wu"),
            ("w_down", DFF, "Wd"),
        ]
        self.tiles = []
        for k, (nm, cols, dst) in enumerate(self.specs):
            rows = DFF if cols == DM else DM
            for i in range(rows // 128):
                self.tiles.append((k, i))
        self.q = QuantHelper(ctx)

    @staticmethod
    def chunks(cols):
        out, o = [], 0
        while o < cols:
            wd = min(1024, cols - o)
            out.append((o, wd))
            o += wd
        return out

    def emit_sync(self):
        c, me = self.ctx, self.W["sync"]
        for n, (k, i) in enumerate(self.tiles):
            nm, cols, dst = self.specs[k]
            src = c.dram[nm].rearrange("(t p) d -> t p d", p=128)
            if n >= 1:
                me.wait(self.W["vector"], self.mk.get(f"vq{n-1}", 0))
                me.wait(self.W["scalar"], self.mk.get(f"sa{n-1}", 0))
            self.mk[f"ld{n}"] = me.dma_start(out=c.sb["wtile"][:, 0:cols],
                                             in_=src[i])
        me.wait(self.W["vector"], self.mk.get("copydone", 0))
        for dst, hbm in (("Wg", "WgqT_st"), ("Wu", "WuqT_st"), ("Wd", "WdqT_st")):
            me.dma_start(out=c.dram[hbm], in_=c.sb[dst][:])
        me.wait_self_all()

    def emit_scalar(self):
        c, me = self.ctx, self.W["scalar"]
        for n, (k, i) in enumerate(self.tiles):
            nm, cols, dst = self.specs[k]
            me.wait(self.W["sync"], self.mk.get(f"ld{n}", 0))
            for ci, (o0, wd) in enumerate(self.chunks(cols)):
                src = c.sb["wtile"][:, o0:o0 + wd]
                me.wait(self.W["vector"], self.mk.get(f"vq{n}_{ci-1}", 0))
                self.q.s_part(me, src, wd)
                self.mk[f"sa{n}_{ci}"] = me.mark()
            self.mk[f"sa{n}"] = me.mark()

    def emit_vector(self):
        c, me = self.ctx, self.W["vector"]
        for n, (k, i) in enumerate(self.tiles):
            nm, cols, dst = self.specs[k]
            me.wait(self.W["sync"], self.mk.get(f"ld{n}", 0))
            for ci, (o0, wd) in enumerate(self.chunks(cols)):
                src = c.sb["wtile"][:, o0:o0 + wd]
                self.q.v_part1(me, src, wd)
                me.wait(self.W["scalar"], self.mk.get(f"sa{n}_{ci}", 0))
                self.q.v_part2(me, c.sb["hqbf"][:, o0:o0 + wd], wd)
                self.mk[f"vq{n}_{ci}"] = me.mark()
            self.mk[f"vq{n}"] = me.mark()
            me.wait(self.W["tensor"], self.mk.get(f"tr{n}", 0))
            ncc = cols // 128
            dstv = c.sb[dst][:].rearrange("p (cchunk m) -> p cchunk m", cchunk=ncc)
            me.tensor_copy(dstv[:, :, i * 128:(i + 1) * 128],
                           c.pt_bf[:, 0:ncc * 128].rearrange(
                               "p (cc m) -> p cc m", cc=ncc))
            self.mk[f"cp{n}"] = me.mark()
        self.mk["copydone"] = me.mark()

    def emit_tensor(self):
        c, me = self.ctx, self.W["tensor"]
        for n, (k, i) in enumerate(self.tiles):
            nm, cols, dst = self.specs[k]
            me.wait(self.W["vector"], self.mk.get(f"vq{n}", 0))
            if n > 0:
                me.wait(self.W["vector"], self.mk.get(f"cp{n-1}", 0))
            for cc in range(cols // 128):
                me.transpose(c.pt_bf[:, cc * 128:(cc + 1) * 128],
                             c.sb["hqbf"][:, cc * 128:(cc + 1) * 128],
                             c.sb["identbf"][:])
            self.mk[f"tr{n}"] = me.mark()


class FFNPhase(Phase):
    def __init__(self, ctx, si, so, tag, load_stash=False):
        super().__init__(ctx, tag)
        self.si = si
        self.so = so
        self.load_stash = load_stash
        self.q = QuantHelper(ctx)

    def xt(self, i):
        return self.ctx.sb["xt"][:, (i % 2) * DM:(i % 2) * DM + DM]

    def xnew(self, i):
        return self.ctx.sb["xnew"][:, (i % 2) * DM:(i % 2) * DM + DM]

    def emit_sync(self):
        c, me = self.ctx, self.W["sync"]
        if self.load_stash:
            for dst, hbm in (("Wg", "WgqT_st"), ("Wu", "WuqT_st"), ("Wd", "WdqT_st")):
                self.mk["wl_" + dst] = me.dma_start(out=c.sb[dst][:],
                                                    in_=c.dram[hbm])
        xs = c.dram["x_state"].rearrange("(t p) d -> t p d", p=128)
        for i in range(2):
            self.mk[f"ld{i}"] = me.dma_start(out=self.xt(i), in_=xs[i])
        for i in range(NT):
            me.wait(self.W["vector"], self.mk.get(f"xn{i}", 0))
            self.mk[f"st{i}"] = me.dma_start(out=xs[i], in_=self.xnew(i))
            if i + 2 < NT:
                self.mk[f"ld{i+2}"] = me.dma_start(out=self.xt(i + 2),
                                                   in_=xs[i + 2])
        me.wait_self_all()

    def emit_scalar(self):
        c, me = self.ctx, self.W["scalar"]
        if self.load_stash:
            for dst in ("Wg", "Wu", "Wd"):
                me.wait(self.W["sync"], self.mk.get("wl_" + dst, 0))
        for i in range(NT):
            me.wait(self.W["vector"], self.mk.get(f"xc{i}", 0))
            self.q.s_part(me, c.sb["xc"][:], DM)
            self.mk[f"abs{i}"] = me.mark()
            for oc, (o0, ow) in enumerate(OCS):
                me.wait(self.W["tensor"], self.mk.get(f"mm{i}_{oc}", 0))
                hg = c.pgu[:, (oc % 2) * 1024:(oc % 2) * 1024 + ow]
                me.activation(c.sb["sgate"][:, 0:ow], hg, AF.Silu)
                self.mk[f"sil{i}_{oc}"] = me.mark()
                me.wait(self.W["vector"], self.mk.get(f"hp{i}_{oc}", 0))
                self.q.s_part(me, c.sb["hpf"][:, 0:ow], ow)
                self.mk[f"habs{i}_{oc}"] = me.mark()
            me.wait(self.W["vector"], self.mk.get(f"xn{i}", 0))
            me.activation(c.sb["sqscr"][:, 0:DM], self.xnew(i), AF.Square,
                          accum_out=c.sb["stats_sq" + self.so][:, i:i + 1])
            self.mk[f"sq{i}"] = me.mark()
        me.wait(self.W["vector"], self.mk.get("var", 0))
        me.activation(c.sb["tmpB"][:, 0:NT], c.sb["tmpA"][:, 0:NT], AF.Ln,
                      bias=EPS)
        me.activation(c.sb["stats_r" + self.so][:, 0:NT], c.sb["tmpB"][:, 0:NT],
                      AF.Exp, scale=-0.5)

    def emit_vector(self):
        c, me = self.ctx, self.W["vector"]
        if self.load_stash:
            for dst in ("Wg", "Wu", "Wd"):
                me.wait(self.W["sync"], self.mk.get("wl_" + dst, 0))
        for i in range(NT):
            me.wait(self.W["sync"], self.mk.get(f"ld{i}", 0))
            me.tensor_scalar(c.sb["xc"][:], self.xt(i),
                             c.sb["stats_m" + self.si][:, i:i + 1], None,
                             ALU.subtract)
            self.mk[f"xc{i}"] = me.mark()
            self.q.v_part1(me, c.sb["xc"][:], DM,
                           rstd_ap=c.sb["stats_r" + self.si][:, i:i + 1])
            me.wait(self.W["scalar"], self.mk.get(f"abs{i}", 0))
            self.q.v_part2(me, c.sb["xqbf"][:], DM)
            self.mk[f"xq{i}"] = me.mark()
            me.wait(self.W["tensor"], self.mk.get(f"tr{i}", 0))
            me.tensor_copy(c.sb["xqT"][:], c.pt_bf[:, 0:1024])
            self.mk[f"xqt{i}"] = me.mark()
            for oc, (o0, ow) in enumerate(OCS):
                me.wait(self.W["scalar"], self.mk.get(f"sil{i}_{oc}", 0))
                hu = c.pgu[:, (oc % 2) * 1024 + 512:(oc % 2) * 1024 + 512 + ow]
                me.tensor_tensor(c.sb["hpf"][:, 0:ow], c.sb["sgate"][:, 0:ow],
                                 hu, ALU.mult)
                self.mk[f"hp{i}_{oc}"] = me.mark()
                self.q.v_part1(me, c.sb["hpf"][:, 0:ow], ow)
                me.wait(self.W["scalar"], self.mk.get(f"habs{i}_{oc}", 0))
                self.q.v_part2(me, c.sb["hqbf"][:, o0:o0 + ow], ow)
                self.mk[f"hq{i}_{oc}"] = me.mark()
            for oc, (o0, ow) in enumerate(OCS):
                me.wait(self.W["tensor"], self.mk.get(f"tr2{i}_{oc}", 0))
                me.tensor_copy(c.sb["hqT"][:, o0:o0 + ow],
                               c.pt_bf[:, 1024:1024 + ow])
                self.mk[f"hqt{i}_{oc}"] = me.mark()
            me.wait(self.W["tensor"], self.mk.get(f"down{i}", 0))
            if i >= 2:
                me.wait(self.W["sync"], self.mk.get(f"st{i-2}", 0))
            me.op("_custom_dve", AFFINE_THEN_ADD, out=self.xnew(i),
                  in0=c.pgu[:, 0:1024], in1=self.xt(i), s0=0.5, s1=0.0)
            self.mk[f"xn{i}"] = me.mark()
            me.reduce_sum(c.sb["msc"][:, 0:1], self.xnew(i), AX.X)
            me.tensor_scalar(c.sb["stats_m" + self.so][:, i:i + 1],
                             c.sb["msc"][:, 0:1], 1.0 / DM, None, ALU.mult)
        me.wait(self.W["scalar"], self.mk.get(f"sq{NT-1}", 0))
        me.tensor_scalar(c.sb["tmpA"][:, 0:NT],
                         c.sb["stats_sq" + self.so][:, 0:NT],
                         1.0 / DM, None, ALU.mult)
        me.tensor_tensor(c.sb["tmpB"][:, 0:NT],
                         c.sb["stats_m" + self.so][:, 0:NT],
                         c.sb["stats_m" + self.so][:, 0:NT], ALU.mult)
        me.tensor_tensor(c.sb["tmpA"][:, 0:NT], c.sb["tmpA"][:, 0:NT],
                         c.sb["tmpB"][:, 0:NT], ALU.subtract)
        self.mk["var"] = me.mark()

    def emit_tensor(self):
        c, me = self.ctx, self.W["tensor"]
        if self.load_stash:
            for dst in ("Wg", "Wu", "Wd"):
                me.wait(self.W["sync"], self.mk.get("wl_" + dst, 0))
        for i in range(NT):
            me.wait(self.W["vector"], self.mk.get(f"xq{i}", 0))
            if i > 0:
                me.wait(self.W["vector"], self.mk.get(f"xqt{i-1}", 0))
            for cc in range(8):
                me.transpose(c.pt_bf[:, cc * 128:(cc + 1) * 128],
                             c.sb["xqbf"][:, cc * 128:(cc + 1) * 128],
                             c.sb["identbf"][:])
            self.mk[f"tr{i}"] = me.mark()
            me.wait(self.W["vector"], self.mk.get(f"xqt{i}", 0))
            for oc, (o0, ow) in enumerate(OCS):
                if oc < 2:
                    me.wait(self.W["vector"], self.mk.get(f"xn{i-1}", 0))
                    if i > 0:
                        me.wait(self.W["vector"], self.mk.get(f"hp{i-1}_5", 0))
                else:
                    me.wait(self.W["vector"], self.mk.get(f"hp{i}_{oc-2}", 0))
                pair = (oc % 2) * 1024
                hg = c.pgu[:, pair:pair + ow]
                hu = c.pgu[:, pair + 512:pair + 512 + ow]
                for cc in range(8):
                    me.matmul(hg, c.sb["xqT"][:, cc * 128:(cc + 1) * 128],
                              c.sb["Wg"][:, cc * DFF + o0:cc * DFF + o0 + ow],
                              start=(cc == 0), stop=(cc == 7))
                for cc in range(8):
                    me.matmul(hu, c.sb["xqT"][:, cc * 128:(cc + 1) * 128],
                              c.sb["Wu"][:, cc * DFF + o0:cc * DFF + o0 + ow],
                              start=(cc == 0), stop=(cc == 7))
                self.mk[f"mm{i}_{oc}"] = me.mark()
            for oc, (o0, ow) in enumerate(OCS):
                me.wait(self.W["vector"], self.mk.get(f"hq{i}_{oc}", 0))
                if not (i == 0 and oc == 0):
                    pi, po = (i, oc - 1) if oc > 0 else (i - 1, NOC - 1)
                    me.wait(self.W["vector"], self.mk.get(f"hqt{pi}_{po}", 0))
                for k in range(ow // 128):
                    me.transpose(c.pt_bf[:, 1024 + k * 128:1024 + (k + 1) * 128],
                                 c.sb["hqbf"][:, o0 + k * 128:o0 + (k + 1) * 128],
                                 c.sb["identbf"][:])
                self.mk[f"tr2{i}_{oc}"] = me.mark()
            me.wait(self.W["vector"], self.mk.get(f"hp{i}_4", 0))
            last_wait = -1
            for cc in range(22):
                oc_of = (cc * 128) // 512
                if oc_of != last_wait:
                    me.wait(self.W["vector"], self.mk.get(f"hqt{i}_{oc_of}", 0))
                    last_wait = oc_of
                me.matmul(c.pgu[:, 0:512], c.sb["hqT"][:, cc * 128:(cc + 1) * 128],
                          c.sb["Wd"][:, cc * DM:cc * DM + 512],
                          start=(cc == 0), stop=(cc == 21))
                me.matmul(c.pgu[:, 512:1024],
                          c.sb["hqT"][:, cc * 128:(cc + 1) * 128],
                          c.sb["Wd"][:, cc * DM + 512:cc * DM + 1024],
                          start=(cc == 0), stop=(cc == 21))
            self.mk[f"down{i}"] = me.mark()




NH = 16
HD = 64


class AttnPhase(Phase):

    def __init__(self, ctx, si, so, tag):
        super().__init__(ctx, tag)
        self.si = si
        self.so = so

    def xt(self, i):
        return self.ctx.sb["xt"][:, (i % 2) * DM:(i % 2) * DM + DM]

    def xnew(self, i):
        return self.ctx.sb["xnew"][:, (i % 2) * DM:(i % 2) * DM + DM]

    def emit_gpsimd(self):
        c, me = self.ctx, self.W["gpsimd"]
        for k, nm in enumerate(("WqT", "WkT", "WvT", "WfcT")):
            self.mk["w_" + nm] = me.dma_start(
                out=c.sb["Wat"][:, k * 8192:(k + 1) * 8192].rearrange(
                    "p (a b) -> p a b", b=1024),
                in_=c.dram[nm].rearrange("(a p) b -> p a b", p=128))

    def emit_sync(self, E=None):
        c, me = self.ctx, self.W["sync"]
        xs = c.dram["x_state"].rearrange("(t p) d -> t p d", p=128)
        for i in range(2):
            self.mk[f"ld{i}"] = me.dma_start(out=self.xt(i), in_=xs[i])
        for i in range(NT):
            me.wait(self.W["vector"], self.mk.get(f"xn{i}", 0))
            self.mk[f"st{i}"] = me.dma_start(out=xs[i], in_=self.xnew(i))
            if i + 2 < NT:
                self.mk[f"ld{i+2}"] = me.dma_start(out=self.xt(i + 2), in_=xs[i + 2])
        me.wait_self_all()

    def emit_scalar(self, E=None):
        c, me = self.ctx, self.W["scalar"]
        for i in range(NT):
            me.wait(self.W["vector"], self.mk.get(f"shift{i}", 0))
            me.activation(c.sb["att_E"][:, 0:256], c.sb["att_sc"][:, 0:256], AF.Exp)
            self.mk[f"exp{i}"] = me.mark()
            me.wait(self.W["vector"], self.mk.get(f"xn{i}", 0))
            me.activation(c.sb["sqscr"][:, 0:DM], self.xnew(i), AF.Square,
                          accum_out=c.sb["stats_sq" + self.so][:, i:i + 1])
            self.mk[f"sq{i}"] = me.mark()
        me.wait(self.W["vector"], self.mk.get("var", 0))
        me.activation(c.sb["tmpB"][:, 0:NT], c.sb["tmpA"][:, 0:NT], AF.Ln,
                      bias=EPS)
        me.activation(c.sb["stats_r" + self.so][:, 0:NT], c.sb["tmpB"][:, 0:NT],
                      AF.Exp, scale=-0.5)

    def emit_vector(self, E=None):
        c, me = self.ctx, self.W["vector"]
        for i in range(NT):
            me.wait(self.W["sync"], self.mk.get(f"ld{i}", 0))
            me.tensor_scalar(c.sb["xnbf"][:], self.xt(i),
                             c.sb["stats_m" + self.si][:, i:i + 1],
                             c.sb["stats_r" + self.si][:, i:i + 1],
                             ALU.subtract, ALU.mult)
            self.mk[f"xn_r{i}"] = me.mark()
            me.wait(self.W["tensor"], self.mk.get(f"tr{i}", 0))
            me.tensor_copy(c.sb["xnT"][:], c.pt_bf[:, 2048:3072])
            self.mk[f"xnt{i}"] = me.mark()
            me.wait(self.W["tensor"], self.mk.get(f"qkv{i}", 0))
            me.tensor_copy(c.sb["qbf"][:], c.pgu[:, 0:1024])
            me.tensor_copy(c.sb["kbf"][:], c.pgu[:, 1024:2048])
            me.tensor_copy(c.sb["vbf"][:], c.ptf[:, 0:1024])
            self.mk[f"ev{i}"] = me.mark()
            q4 = c.sb["qbf"][:].rearrange("p (h d) -> p h d", d=HD)
            k4 = c.sb["kbf"][:].rearrange("p (t d) -> p t d", d=HD)
            sc3 = c.sb["att_sc"][:, 0:256].rearrange("p (h t) -> p h t", t=NH)
            for hg in range(4):
                qv = q4[:, hg * 4:(hg + 1) * 4, :].unsqueeze(2).broadcast_to(
                    [128, 4, NH, HD])
                kv = k4.unsqueeze(1).broadcast_to([128, 4, NH, HD])
                m4 = c.sb["att_m4"][:, 0:4096].rearrange(
                    "p (h t d) -> p h t d", t=NH, d=HD)
                me.tensor_tensor(m4, qv, kv, ALU.mult)
                me.tensor_reduce(sc3[:, hg * 4:(hg + 1) * 4, :], m4, AX.X,
                                 ALU.add)
            me.tensor_reduce(c.sb["att_mx"][:, 0:NH], sc3, AX.X, ALU.max)
            mxb = c.sb["att_mx"][:, 0:NH].unsqueeze(2).broadcast_to([128, NH, NH])
            me.tensor_tensor(sc3, sc3, mxb, ALU.subtract)
            me.tensor_scalar(c.sb["att_sc"][:, 0:256], c.sb["att_sc"][:, 0:256],
                             0.125, None, ALU.mult)
            self.mk[f"shift{i}"] = me.mark()
            me.wait(self.W["scalar"], self.mk.get(f"exp{i}", 0))
            E3 = c.sb["att_E"][:, 0:256].rearrange("p (h t) -> p h t", t=NH)
            me.tensor_reduce(c.sb["att_sm"][:, 0:NH], E3, AX.X, ALU.add)
            me.reciprocal(c.sb["att_rc"][:, 0:NH], c.sb["att_sm"][:, 0:NH])
            rcb = c.sb["att_rc"][:, 0:NH].unsqueeze(2).broadcast_to([128, NH, NH])
            me.tensor_tensor(c.sb["att_A"][:, 0:256].rearrange(
                "p (h t) -> p h t", t=NH), E3, rcb, ALU.mult)
            A3 = c.sb["att_A"][:, 0:256].rearrange("p (h t) -> p h t", t=NH)
            v3 = c.sb["vbf"][:].rearrange("p (t d) -> p t d", d=HD)
            vperm = v3.transpose([0, 2, 1])
            ao3 = c.sb["aof"][:].rearrange("p (h d) -> p h d", d=HD)
            for hg in range(4):
                av = A3[:, hg * 4:(hg + 1) * 4, :].unsqueeze(2).broadcast_to(
                    [128, 4, HD, NH])
                vv = vperm.unsqueeze(1).broadcast_to([128, 4, HD, NH])
                m4 = c.sb["att_m4"][:, 0:4096].rearrange(
                    "p (h d t) -> p h d t", d=HD, t=NH)
                me.tensor_tensor(m4, av, vv, ALU.mult)
                me.tensor_reduce(ao3[:, hg * 4:(hg + 1) * 4, :], m4, AX.X,
                                 ALU.add)
            me.tensor_copy(c.sb["abf"][:], c.sb["aof"][:])
            self.mk[f"ao{i}"] = me.mark()
            me.wait(self.W["tensor"], self.mk.get(f"atr{i}", 0))
            me.tensor_copy(c.sb["aT"][:], c.pt_bf[:, 2048:3072])
            self.mk[f"aT{i}"] = me.mark()
            me.wait(self.W["tensor"], self.mk.get(f"fc{i}", 0))
            if i >= 2:
                me.wait(self.W["sync"], self.mk.get(f"st{i-2}", 0))
                me.wait(self.W["scalar"], self.mk.get(f"sq{i-2}", 0))
            me.op("_custom_dve", AFFINE_THEN_ADD, out=self.xnew(i),
                  in0=c.pgu[:, 0:1024], in1=self.xt(i), s0=1.0, s1=0.0)
            self.mk[f"xn{i}"] = me.mark()
            me.reduce_sum(c.sb["msc"][:, 0:1], self.xnew(i), AX.X)
            me.tensor_scalar(c.sb["stats_m" + self.so][:, i:i + 1],
                             c.sb["msc"][:, 0:1], 1.0 / DM, None, ALU.mult)
        me.wait(self.W["scalar"], self.mk.get(f"sq{NT-1}", 0))
        me.tensor_scalar(c.sb["tmpA"][:, 0:NT],
                         c.sb["stats_sq" + self.so][:, 0:NT],
                         1.0 / DM, None, ALU.mult)
        me.tensor_tensor(c.sb["tmpB"][:, 0:NT], c.sb["stats_m" + self.so][:, 0:NT],
                         c.sb["stats_m" + self.so][:, 0:NT], ALU.mult)
        me.tensor_tensor(c.sb["tmpA"][:, 0:NT], c.sb["tmpA"][:, 0:NT],
                         c.sb["tmpB"][:, 0:NT], ALU.subtract)
        self.mk["var"] = me.mark()

    def emit_tensor(self, E=None):
        c, me = self.ctx, self.W["tensor"]
        for k in ("WqT", "WkT", "WvT", "WfcT"):
            me.wait(self.W["gpsimd"], self.mk.get("w_" + k, 0))
        for i in range(NT):
            me.wait(self.W["vector"], self.mk.get(f"xn_r{i}", 0))
            if i > 0:
                me.wait(self.W["vector"], self.mk.get(f"xnt{i-1}", 0))
            for cc in range(8):
                me.transpose(c.pt_bf[:, 2048 + cc * 128:2048 + (cc + 1) * 128],
                             c.sb["xnbf"][:, cc * 128:(cc + 1) * 128],
                             c.sb["identbf"][:])
            self.mk[f"tr{i}"] = me.mark()
            me.wait(self.W["vector"], self.mk.get(f"xnt{i}", 0))
            if i > 0:
                me.wait(self.W["vector"], self.mk.get(f"ev{i-1}", 0))
                me.wait(self.W["vector"], self.mk.get(f"xn{i-1}", 0))
            for w, out in ((0, c.pgu[:, 0:1024]), (1, c.pgu[:, 1024:2048]),
                           (2, c.ptf[:, 0:1024])):
                for half in range(2):
                    for cc in range(8):
                        me.matmul(out[:, half * 512:half * 512 + 512],
                                  c.sb["xnT"][:, cc * 128:(cc + 1) * 128],
                                  c.sb["Wat"][:, w * 8192 + cc * 1024 + half * 512:
                                              w * 8192 + cc * 1024 + half * 512 + 512],
                                  start=(cc == 0), stop=(cc == 7))
            self.mk[f"qkv{i}"] = me.mark()
            me.wait(self.W["vector"], self.mk.get(f"ao{i}", 0))
            for cc in range(8):
                me.transpose(c.pt_bf[:, 2048 + cc * 128:2048 + (cc + 1) * 128],
                             c.sb["abf"][:, cc * 128:(cc + 1) * 128],
                             c.sb["identbf"][:])
            self.mk[f"atr{i}"] = me.mark()
            me.wait(self.W["vector"], self.mk.get(f"aT{i}", 0))
            for half in range(2):
                for cc in range(8):
                    me.matmul(c.pgu[:, half * 512:half * 512 + 512],
                              c.sb["aT"][:, cc * 128:(cc + 1) * 128],
                              c.sb["Wat"][:, 3 * 8192 + cc * 1024 + half * 512:
                                          3 * 8192 + cc * 1024 + half * 512 + 512],
                              start=(cc == 0), stop=(cc == 7))
            self.mk[f"fc{i}"] = me.mark()




NEXP = 64
MOE_SUPERS = [list(range(s, min(s + 6, NT))) for s in range(0, NT, 6)]


class MoEWeightPrep(Phase):

    def __init__(self, ctx):
        super().__init__(ctx, "wm")
        self.chunks = ([("keysR", "keys_st", s) for s in range(8)] +
                       [("valuesR", "values_st", s) for s in range(8)])

    def emit_sync(self):
        c, me = self.ctx, self.W["sync"]
        for ci, (src, dst, s) in enumerate(self.chunks):
            me.wait(self.W["vector"], self.mk.get(("ca", ci - 2), 0))
            self.mk[("rd", ci)] = me.dma_start(
                out=c.sb["wmA"][:, ci % 2, :],
                in_=c.dram[src][:, s * 8192:(s + 1) * 8192])
        me.wait_self_all()

    def emit_vector(self):
        c, me = self.ctx, self.W["vector"]
        for ci in range(len(self.chunks)):
            me.wait(self.W["sync"], self.mk.get(("rd", ci), 0))
            me.wait(self.W["gpsimd"], self.mk.get(("wr", ci - 2), 0))
            me.tensor_copy(c.sb["wmB"][:, ci % 2, :], c.sb["wmA"][:, ci % 2, :])
            self.mk[("ca", ci)] = me.mark()

    def emit_gpsimd(self):
        c, me = self.ctx, self.W["gpsimd"]
        for ci, (src, dst, s) in enumerate(self.chunks):
            me.wait(self.W["vector"], self.mk.get(("ca", ci), 0))
            self.mk[("wr", ci)] = me.dma_start(
                out=c.dram[dst][:, s * 8192:(s + 1) * 8192],
                in_=c.sb["wmB"][:, ci % 2, :])
        me.wait_self_all()


class MoEPhase(Phase):

    def __init__(self, ctx, si, so, tag):
        super().__init__(ctx, tag)
        self.si = si
        self.so = so
        sup = MOE_SUPERS
        self.sup = sup
        self.chk = []
        self.base = []
        for sx, Ts in enumerate(sup):
            self.base.append(len(self.chk))
            for g4 in range(16):
                for ti, t in enumerate(Ts):
                    self.chk.append((sx, g4, ti, t))
        self.TOT = len(self.chk)
        self.sqpos = {}
        for sx in range(len(sup) - 1):
            Xe = self.base[sx] + 16 * len(sup[sx])
            for ti, t in enumerate(sup[sx]):
                self.sqpos.setdefault(Xe - len(sup[sx]) + ti + 5, []).append(t)
        self.ptile = {}
        for sx, Ts in enumerate(sup):
            for tipos, t in enumerate(Ts):
                step = 0 if sx == 0 else self.base[sx] - len(sup[sx - 1]) + tipos
                self.ptile.setdefault(step, []).append((sx, tipos, t))
        self.items = []
        self.prep_items = {}
        self.sc_item = {}
        self.xn_item2 = {}
        self.xn_prev_h = {}
        for sx, Ts in enumerate(sup):
            if sx == 0:
                continue
            pTs = sup[sx - 1]
            for tipos, t in enumerate(Ts):
                if tipos < len(pTs):
                    self.xn_prev_h[t] = self.base[sx - 1] + 15 * len(pTs) + tipos
        for X in range(self.TOT + 4):
            if 0 <= X - 2 < self.TOT:
                self.sc_item[X - 2] = len(self.items)
                self.items.append(("sc", X - 2))
            if X in self.ptile:
                ids = []
                for sx, tipos, t in self.ptile[X]:
                    for half in range(2):
                        ids.append(len(self.items))
                        self.items.append(("xn", t, tipos, half))
                        if half == 1:
                            self.xn_item2[t] = ids[-1]
                    ids.append(len(self.items))
                    self.items.append(("lg", t, tipos))
                self.prep_items[X] = ids

    def xt2(self, i):
        return self.ctx.sb["xt"][:, (i % 2) * DM:(i % 2) * DM + DM]

    def xnew(self, i):
        return self.ctx.sb["xnew"][:, (i % 2) * DM:(i % 2) * DM + DM]


    def _t_item(self, me, iid):
        c = self.ctx
        it = self.items[iid]
        me.wait(self.W["scalar"], self.mk.get(("cpT", iid - 2), 0))
        if it[0] == "lg":
            _, t, tipos = it
            me.wait(self.W["gpsimd"], self.mk.get("selT", 0))
            me.wait(self.W["scalar"], self.mk.get(("xtc", t), 0))
            lb = 1024 + (iid % 2) * 512 + 256
            for cc in range(8):
                me.matmul(c.ptf[:, lb:lb + 64],
                          c.sb["xnTs"][:, cc, tipos * 128:(tipos + 1) * 128],
                          c.sb["selTb"][:, cc, :],
                          start=(cc == 0), stop=(cc == 7))
            self.mk[("lgi", t)] = me.mark()
            return
        tb = 2048 + (iid % 2) * 1024
        if it[0] == "sc":
            X = it[1]
            me.wait(self.W["vector"], self.mk.get(("gat", X), 0))
            for k in range(4):
                me.transpose(c.pt_bf[:, tb + k * 128:tb + (k + 1) * 128],
                             c.sb["scb"][:, X % 2, k * 128:(k + 1) * 128],
                             c.sb["identbf"][:])
        else:
            _, t, tipos, half = it
            me.wait(self.W["vector"], self.mk.get(("xn", t), 0))
            for k in range(4):
                cc = half * 4 + k
                me.transpose(c.pt_bf[:, tb + k * 128:tb + (k + 1) * 128],
                             c.sb["m_xnbf"][:, cc * 128:(cc + 1) * 128],
                             c.sb["identbf"][:])
        self.mk[("tri", iid)] = me.mark()

    def _t_h(self, me, X):
        c = self.ctx
        sx, g4, ti, t = self.chk[X]
        if X == 0:
            for s8 in range(6):
                me.wait(self.W["sync"], self.mk.get(("k", s8), 0))
            for s8 in range(6, 8):
                me.wait(self.W["gpsimd"], self.mk.get(("k", s8), 0))
        me.wait(self.W["scalar"], self.mk.get(("xtc", t), 0))
        me.wait(self.W["scalar"], self.mk.get(("hev", X - 2), 0))
        b = (X % 2) * 512
        for cc in range(8):
            me.matmul(c.ptf[:, b:b + 512],
                      c.sb["xnTs"][:, cc, ti * 128:(ti + 1) * 128],
                      c.sb["keysb"][:, cc, g4 * 512:(g4 + 1) * 512],
                      start=(cc == 0), stop=(cc == 7))
        self.mk[("h", X)] = me.mark()

    def _t_vmm(self, me, X):
        c = self.ctx
        sx, g4, ti, t = self.chk[X]
        g4g = sx * 16 + g4
        if ti == 0:
            me.wait(self.W["sync"], self.mk.get(("vd", 2 * g4g), 0))
            me.wait(self.W["gpsimd"], self.mk.get(("vd", 2 * g4g + 1), 0))
        me.wait(self.W["scalar"], self.mk.get(("cpT", self.sc_item[X]), 0))
        me.wait(self.W["vector"], self.mk.get(("ev", X - 2), 0))
        pb = (X % 2) * 1024
        for half in range(2):
            for e4 in range(4):
                pglob = 2 * g4g + e4 // 2
                vs = (e4 % 2) * 1024 + half * 512
                me.matmul(c.pgu[:, pb + half * 512:pb + half * 512 + 512],
                          c.sb["scT"][:, X % 2, e4 * 128:(e4 + 1) * 128],
                          c.sb["vbuf"][:, pglob % 4, vs:vs + 512],
                          start=(e4 == 0), stop=(e4 == 3))
        self.mk[("vmm", X)] = me.mark()
        if ti == len(self.sup[sx]) - 1:
            self.mk[("vmG", g4g)] = me.mark()

    def emit_tensor(self, E=None):
        me = self.W["tensor"]
        for X in range(self.TOT + 4):
            if X == 0:
                for iid in self.prep_items.get(0, []):
                    self._t_item(me, iid)
            if X < self.TOT:
                self._t_h(me, X)
            if 0 <= X - 2 < self.TOT:
                self._t_item(me, self.sc_item[X - 2])
            if 0 <= X - 4 < self.TOT:
                self._t_vmm(me, X - 4)
            if X > 0:
                for iid in self.prep_items.get(X, []):
                    self._t_item(me, iid)


    def _s_item(self, me, iid):
        c = self.ctx
        it = self.items[iid]
        if it[0] == "lg":
            _, t, tipos = it
            me.wait(self.W["tensor"], self.mk.get(("lgi", t), 0))
            me.wait(self.W["vector"], self.mk.get(("gt", t - 2), 0))
            lb = 1024 + (iid % 2) * 512 + 256
            me.activation(c.sb["selsb"][:, t % 2, :], c.ptf[:, lb:lb + 64],
                          AF.Sigmoid)
            self.mk[("sig", t)] = me.mark()
            self.mk[("cpT", iid)] = me.mark()
            return
        me.wait(self.W["tensor"], self.mk.get(("tri", iid), 0))
        tb = 2048 + (iid % 2) * 1024
        src = c.pt_bf[:, tb:tb + 512]
        if it[0] == "sc":
            X = it[1]
            me.wait(self.W["tensor"], self.mk.get(("vmm", X - 2), 0))
            me.activation(c.sb["scT"][:, X % 2, :], src, AF.Relu)
        else:
            _, t, tipos, half = it
            ph = self.xn_prev_h.get(t)
            if ph is not None:
                me.wait(self.W["tensor"], self.mk.get(("h", ph), 0))
            me.activation(
                c.sb["xnTs"][:, half * 4:half * 4 + 4,
                             tipos * 128:(tipos + 1) * 128],
                src.rearrange("p (a b) -> p a b", b=128), AF.Copy)
            if half == 1:
                self.mk[("xtc", t)] = me.mark()
        self.mk[("cpT", iid)] = me.mark()

    def emit_scalar(self, E=None):
        c, me = self.ctx, self.W["scalar"]
        for X in range(self.TOT + 4):
            for t in self.sqpos.get(X, []):
                me.wait(self.W["vector"], self.mk.get(("af", t), 0))
                me.activation(c.sb["sqscr"][:, 0:DM], self.xnew(t), AF.Square,
                              accum_out=c.sb["stats_sq" + self.so][:, t:t + 1])
                self.mk[("sq", t)] = me.mark()
            if X == 0:
                for iid in self.prep_items.get(0, []):
                    self._s_item(me, iid)
            if X < self.TOT:
                me.wait(self.W["tensor"], self.mk.get(("h", X), 0))
                me.wait(self.W["tensor"],
                        self.mk.get(("tri", self.sc_item.get(X - 2, -9)), 0))
                me.activation(c.sb["scb"][:, X % 2, :],
                              c.ptf[:, (X % 2) * 512:(X % 2) * 512 + 512],
                              AF.Relu)
                self.mk[("hev", X)] = me.mark()
            if 0 <= X - 2 < self.TOT:
                self._s_item(me, self.sc_item[X - 2])
            if X > 0:
                for iid in self.prep_items.get(X, []):
                    self._s_item(me, iid)
        for t in self.sup[-1]:
            me.wait(self.W["vector"], self.mk.get(("af", t), 0))
            me.activation(c.sb["sqscr"][:, 0:DM], self.xnew(t), AF.Square,
                          accum_out=c.sb["stats_sq" + self.so][:, t:t + 1])
            self.mk[("sq", t)] = me.mark()
        me.wait(self.W["vector"], self.mk.get("var", 0))
        me.activation(c.sb["tmpB"][:, 0:NT], c.sb["tmpA"][:, 0:NT], AF.Ln,
                      bias=EPS)
        me.activation(c.sb["stats_r" + self.so][:, 0:NT], c.sb["tmpB"][:, 0:NT],
                      AF.Exp, scale=-0.5)


    def _v_gating(self, me, sx, tipos, t):
        c = self.ctx
        me.wait(self.W["scalar"], self.mk.get(("sig", t), 0))
        sl = c.sb["selsb"][:, t % 2, :]
        me.tensor_copy(c.sb["selw"][:], sl)
        for r in range(4):
            me.reduce_max(c.sb["msc"][:, 1:2], c.sb["selw"][:], AX.X)
            if r < 3:
                me.tensor_scalar(c.sb["selm"][:], c.sb["selw"][:],
                                 c.sb["msc"][:, 1:2], -1e30, ALU.is_ge,
                                 ALU.mult)
                me.tensor_tensor(c.sb["selw"][:], c.sb["selw"][:],
                                 c.sb["selm"][:], ALU.add)
        me.tensor_scalar(c.sb["selm"][:], sl, c.sb["msc"][:, 1:2], None,
                         ALU.is_ge)
        me.tensor_tensor(
            c.sb["gates_bf"][:, sx % 2, tipos * 64:tipos * 64 + 64],
            c.sb["selm"][:], sl, ALU.mult)
        self.mk[("gt", t)] = me.mark()

    def emit_vector(self, E=None):
        c, me = self.ctx, self.W["vector"]
        sup, chk = self.sup, self.chk
        def vprep(X):
            for sx, tipos, t in self.ptile.get(X, []):
                me.wait(self.W["sync"], self.mk.get(("ld", t), 0))
                me.wait(self.W["tensor"],
                        self.mk.get(("tri", self.xn_item2.get(t - 1, -9)), 0))
                me.tensor_scalar(c.sb["m_xnbf"][:],
                                 c.sb["m_xt"][:, t % 2, :],
                                 c.sb["stats_m" + self.si][:, t:t + 1],
                                 c.sb["stats_r" + self.si][:, t:t + 1],
                                 ALU.subtract, ALU.mult)
                self.mk[("xn", t)] = me.mark()
                self._v_gating(me, sx, tipos, t)

        for X in range(self.TOT + 4):
            if X == 0:
                vprep(0)
            if X < self.TOT:
                sx, g4, ti, t = chk[X]
                me.wait(self.W["scalar"], self.mk.get(("hev", X), 0))
                g = c.sb["gates_bf"][:, sx % 2,
                                     ti * 64 + 4 * g4:ti * 64 + 4 * g4 + 4]
                sc3 = c.sb["scb"][:, X % 2, :].rearrange("p (a b) -> p a b",
                                                         b=128)
                me.tensor_tensor(sc3, sc3,
                                 g.unsqueeze(2).broadcast_to([128, 4, 128]),
                                 ALU.mult)
                self.mk[("gat", X)] = me.mark()
            if X > 0:
                vprep(X)
            if 0 <= X - 4 < self.TOT:
                Xe = X - 4
                sx, g4, ti, t = chk[Xe]
                me.wait(self.W["tensor"], self.mk.get(("vmm", Xe), 0))
                oa = c.sb["outac"][:, ti, :]
                pg = c.pgu[:, (Xe % 2) * 1024:(Xe % 2) * 1024 + 1024]
                if g4 == 0:
                    me.tensor_copy(oa, pg)
                else:
                    me.tensor_tensor(oa, oa, pg, ALU.add)
                if g4 == 15:
                    me.wait(self.W["gpsimd"], self.mk.get(("ld2", t), 0))
                    me.wait(self.W["gpsimd"], self.mk.get(("st", t - 2), 0))
                    me.wait(self.W["scalar"], self.mk.get(("sq", t - 2), 0))
                    me.op("_custom_dve", AFFINE_THEN_ADD, out=self.xnew(t),
                          in0=oa, in1=self.xt2(t),
                          s0=0.5, s1=0.0)
                    self.mk[("af", t)] = me.mark()
                    me.reduce_sum(c.sb["msc"][:, 0:1], self.xnew(t), AX.X)
                    me.tensor_scalar(c.sb["stats_m" + self.so][:, t:t + 1],
                                     c.sb["msc"][:, 0:1], 1.0 / DM, None,
                                     ALU.mult)
                self.mk[("ev", Xe)] = me.mark()
        me.wait(self.W["scalar"], self.mk.get(("sq", NT - 1), 0))
        me.tensor_scalar(c.sb["tmpA"][:, 0:NT],
                         c.sb["stats_sq" + self.so][:, 0:NT],
                         1.0 / DM, None, ALU.mult)
        me.tensor_tensor(c.sb["tmpB"][:, 0:NT],
                         c.sb["stats_m" + self.so][:, 0:NT],
                         c.sb["stats_m" + self.so][:, 0:NT], ALU.mult)
        me.tensor_tensor(c.sb["tmpA"][:, 0:NT], c.sb["tmpA"][:, 0:NT],
                         c.sb["tmpB"][:, 0:NT], ALU.subtract)
        self.mk["var"] = me.mark()


    def emit_sync(self, E=None):
        c, me = self.ctx, self.W["sync"]
        sup, chk = self.sup, self.chk
        xs = c.dram["x_state"].rearrange("(t p) d -> t p d", p=128)
        ldpos = {}
        for step, entries in self.ptile.items():
            for sx, tipos, t in entries:
                ldpos.setdefault(max(0, step - 2), []).append(t)
        for X in range(self.TOT + 4):
            if X == 0:
                for s8 in range(6):
                    self.mk[("k", s8)] = me.dma_start(
                        out=c.sb["keysb"][:, s8, :],
                        in_=c.dram["keys_st"][:, s8 * 8192:(s8 + 1) * 8192])
            if X in ldpos:
                for t in ldpos[X]:
                    me.wait(self.W["vector"], self.mk.get(("xn", t - 2), 0))
                    self.mk[("ld", t)] = me.dma_start(
                        out=c.sb["m_xt"][:, t % 2, :], in_=xs[t])
            if X < self.TOT:
                sx, g4, ti, t = chk[X]
                if ti == 0:
                    g4g = sx * 16 + g4
                    pglob = 2 * g4g
                    me.wait(self.W["tensor"],
                            self.mk.get(("vmG", (pglob - 4) // 2), 0))
                    e0 = 4 * g4
                    self.mk[("vd", pglob)] = me.dma_start(
                        out=c.sb["vbuf"][:, pglob % 4, :],
                        in_=c.dram["values_st"][:, e0 * 1024:(e0 + 2) * 1024])
        me.wait_self_all()

    def emit_gpsimd(self, E=None):
        c, me = self.ctx, self.W["gpsimd"]
        sup, chk = self.sup, self.chk
        xs = c.dram["x_state"].rearrange("(t p) d -> t p d", p=128)
        self.mk["selT"] = me.dma_start(
            out=c.sb["selTb"][:],
            in_=c.dram["selT"].rearrange("(a p) b -> p a b", p=128))
        for s8 in range(6, 8):
            self.mk[("k", s8)] = me.dma_start(
                out=c.sb["keysb"][:, s8, :],
                in_=c.dram["keys_st"][:, s8 * 8192:(s8 + 1) * 8192])
        stpos = {}
        for sx in range(len(sup)):
            Xe = self.base[sx] + 16 * len(sup[sx])
            stpos[Xe - len(sup[sx])] = sx
        for X in range(self.TOT + 4):
            if X < self.TOT:
                sx, g4, ti, t = chk[X]
                if ti == 0:
                    g4g = sx * 16 + g4
                    pglob = 2 * g4g + 1
                    me.wait(self.W["tensor"],
                            self.mk.get(("vmG", (pglob - 4) // 2), 0))
                    e0 = 4 * g4 + 2
                    self.mk[("vd", pglob)] = me.dma_start(
                        out=c.sb["vbuf"][:, pglob % 4, :],
                        in_=c.dram["values_st"][:, e0 * 1024:(e0 + 2) * 1024])
            if X in stpos:
                Ts = sup[stpos[X]]
                seq = []
                for k2, t in enumerate(Ts):
                    seq.append(("ld2", t))
                    if k2 >= 2:
                        seq.append(("st", Ts[k2 - 2]))
                seq.append(("st", Ts[-2]))
                seq.append(("st", Ts[-1]))
                for kind, t in seq:
                    if kind == "ld2":
                        me.wait(self.W["vector"], self.mk.get(("af", t - 2), 0))
                        self.mk[("ld2", t)] = me.dma_start(
                            out=self.xt2(t), in_=xs[t])
                    else:
                        me.wait(self.W["vector"], self.mk.get(("af", t), 0))
                        self.mk[("st", t)] = me.dma_start(
                            out=xs[t], in_=self.xnew(t))
        me.wait_self_all()




MSTAGES = ("E0", "E1", "E2", "E2X", "E2SCAN", "E2OUT")
NSEG = 8
SEGT = 512
SEGTILES = SEGT // 128


class MambaPhase(Phase):

    def __init__(self, ctx, si, so, tag):
        super().__init__(ctx, tag)
        self.si = si
        self.so = so

    def xt(self, i):
        return self.ctx.sb["e_xt"][:, i % 4, :]

    def xnew(self, i):
        return self.ctx.sb["xnew"][:, (i % 2) * DM:(i % 2) * DM + DM]

    @staticmethod
    def spred(sg, n, dc, lag):
        k = (sg * 16 + n) * 8 + dc - lag
        if k < 0:
            return None
        return (k // 128, (k // 8) % 16, k % 8)

    def emit_gpsimd(self):
        c, me = self.ctx, self.W["gpsimd"]
        self.mk["win"] = me.dma_start(
            out=c.sb["Win_b"][:].rearrange("p a b -> p (a b)"),
            in_=c.dram["WinR"])
        self.mk["wout"] = me.dma_start(
            out=c.sb["Wout_b"][:].rearrange("p a b -> p (a b)"),
            in_=c.dram["WoutR"])
        self.mk["wxp"] = me.dma_start(
            out=c.sb["Wxp_b"][:].rearrange("p a b -> p (a b)"),
            in_=c.dram["WxpR"])
        self.mk["wdt"] = me.dma_start(
            out=c.sb["Wdt_b"][0:64, :, :].rearrange("p a b -> p (a b)"),
            in_=c.dram["WdtR"])
        xr = c.dram["xraw_st"].rearrange("p (a b) -> p a b", b=S)
        for sg in range(NSEG if "E0" in MSTAGES else 0):
            for dc in range(16):
                me.wait(self.W["vector"], self.mk.get(("xrev", sg, dc), 0))
                self.mk[("xrw", sg, dc)] = me.dma_start(
                    out=xr[:, dc, sg * SEGT:(sg + 1) * SEGT],
                    in_=c.sb["xrstage"][:, (dc % 2) * SEGT:(dc % 2) * SEGT + SEGT])
        self.mk["e0_dn"] = me.dn
        for dcz in range(16 if "E1" in MSTAGES else 0):
            me.wait(self.W["vector"], self.mk.get(("cvz", dcz), 0) if dcz > 0
                    else self.mk.get("cbz", 0))
            self.mk[("cvl", dcz)] = me.dma_start(
                out=c.sb["cbuf"][:, 1:S + 1], in_=xr[:, dcz, :])
            me.wait(self.W["scalar"], self.mk.get(("gel", dcz), 0))
            dst = c.dram["u_st"] if dcz < 8 else c.dram["z_st"]
            self.mk[("cgw", dcz)] = me.dma_start(
                out=dst.rearrange("p (a b) -> p a b", b=S)[:, dcz % 8, :],
                in_=c.sb["cgout"][:])
        us = c.dram["u_st"].rearrange("p (a b) -> p a b", b=S)
        zs = c.dram["z_st"].rearrange("p (a b) -> p a b", b=S)
        for sg in range(NSEG if "E2" in MSTAGES else 0):
            if sg == 0:
                for dcz in range(16):
                    me.wait(me, self.mk.get(("cgw", dcz), 0))
            me.wait(self.W["vector"], self.mk.get(("useg_done", sg - 1), 0))
            self.mk[("ubf", sg)] = me.dma_start(
                out=c.sb["ubf"][:], in_=us[:, :, sg * SEGT:(sg + 1) * SEGT])
            self.mk[("zbf", sg)] = me.dma_start(
                out=c.sb["zbf"][:], in_=zs[:, :, sg * SEGT:(sg + 1) * SEGT])
            if "E2X" not in MSTAGES:
                continue
            me.wait(self.W["vector"], self.mk.get(("bc_ev", sg), 0))
            if sg > 0:
                me.wait(me, self.mk.get(("bbc", sg - 1, 15), 0))
                me.wait(me, self.mk.get(("cbc", sg - 1, 15), 0))
            self.mk[("bst", sg)] = me.dma_start(out=c.dram["bc_st"],
                                                in_=c.sb["Bsb"][0:16, :])
            self.mk[("cst", sg)] = me.dma_start(out=c.dram["cc_st"],
                                                in_=c.sb["Csb"][0:16, :])
            me.wait(me, self.mk[("bst", sg)])
            me.wait(me, self.mk[("cst", sg)])
            def cbuf(n):
                return c.sb["cbc" if n % 2 == 0 else "spr"][:]
            if "E2SCAN" in MSTAGES:
                self.mk[("bbc", sg, 0)] = me.dma_start(
                    out=c.sb["bbc"][:],
                    in_=c.dram["bc_st"][0:1, :].broadcast_to([128, SEGT]))
                self.mk[("cbc", sg, 0)] = me.dma_start(
                    out=cbuf(0), in_=c.dram["cc_st"][0:1, :]
                    .broadcast_to([128, SEGT]))
            for n in range(16 if "E2SCAN" in MSTAGES else 0):
                for dc in range(8):
                    k = n * 8 + dc
                    if dc == 0:
                        me.wait(me, self.mk[("bbc", sg, n)])
                        if n == 0:
                            me.wait(self.W["vector"], self.mk.get(("du", sg), 0))
                    p2 = self.spred(sg, n, dc, 2)
                    if p2:
                        me.wait(self.W["vector"],
                                self.mk.get(("scn",) + p2, 0))
                    me.tensor_tensor(c.sb["dBubf"][:, k % 2, :],
                                     c.sb["du"][:, dc, :], c.sb["bbc"][:],
                                     ALU.mult)
                    self.mk[("dBu", sg, n, dc)] = me.mark()
                    if dc == 3 and n < 15:
                        me.wait_self_compute()
                        self.mk[("cbc", sg, n + 1)] = me.dma_start(
                            out=cbuf(n + 1), in_=c.dram["cc_st"][n + 1:n + 2, :]
                            .broadcast_to([128, SEGT]))
                    if dc == 7 and n < 15:
                        me.wait_self_compute()
                        self.mk[("bbc", sg, n + 1)] = me.dma_start(
                            out=c.sb["bbc"][:],
                            in_=c.dram["bc_st"][n + 1:n + 2, :]
                            .broadcast_to([128, SEGT]))
        me.wait_self_all()

    def emit_sync(self, E=None):
        c, me = self.ctx, self.W["sync"]
        xs = c.dram["x_state"].rearrange("(t p) d -> t p d", p=128)
        for nm, dst in (("KxR", "kx"), ("KzR", "kz"), ("bxR", "bx"),
                        ("bzR", "bz"), ("AlogR", "alog"), ("DR", "dsb"),
                        ("bdtR", "bdt")):
            self.mk["p_" + dst] = me.dma_start(out=c.sb[dst][:], in_=c.dram[nm])
        for i in range(NT if "E0" in MSTAGES else 0):
            if i >= 4:
                me.wait(self.W["vector"], self.mk.get(f"e0xn{i-4}", 0))
            self.mk[f"ld{i}"] = me.dma_start(out=self.xt(i), in_=xs[i])
        for sg in range(NSEG if "E2" in MSTAGES else 0):
            for tl in range(SEGTILES):
                i = sg * SEGTILES + tl
                if i < 4:
                    me.wait(self.W["vector"], self.mk.get(f"e0xn{NT-1}", 0))
                else:
                    me.wait(self.W["vector"], self.mk.get(f"xn{i-4}", 0))
                self.mk[f"ld2_{i}"] = me.dma_start(out=self.xt(i), in_=xs[i])
            for tl in range(SEGTILES):
                i = sg * SEGTILES + tl
                me.wait(self.W["vector"], self.mk.get(f"xn{i}", 0))
                self.mk[f"st{i}"] = me.dma_start(out=xs[i], in_=self.xnew(i))
        me.wait_self_all()

    def emit_scalar(self, E=None):
        c, me = self.ctx, self.W["scalar"]
        me.wait(self.W["sync"], self.mk.get("p_alog", 0))
        me.activation(c.sb["asb"][:].rearrange("p a b -> p (a b)"),
                      c.sb["alog"][:].rearrange("p a b -> p (a b)"), AF.Exp)
        self.mk["aexp"] = me.mark()
        for dcz in range(16 if "E1" in MSTAGES else 0):
            me.wait(self.W["vector"], self.mk.get(("cv4", dcz), 0))
            if dcz > 0:
                me.wait(self.W["gpsimd"], self.mk.get(("cgw", dcz - 1), 0))
            b = (c.sb["bx"] if dcz < 8 else c.sb["bz"])[:, dcz % 8:dcz % 8 + 1]
            me.activation(c.sb["cgout"][:], c.sb["ct2"][:], AF.Gelu, bias=b)
            self.mk[("gel", dcz)] = me.mark()
        for sg in range(NSEG if ("E2" in MSTAGES and "E2X" in MSTAGES) else 0):
            for dc in range(8):
                me.wait(self.W["tensor"], self.mk.get(("dtp", sg, dc), 0))
                bcol = c.sb["bdt"][:, dc:dc + 1]
                me.activation(c.sb["spa"][:], c.ptf[:, 0:SEGT], AF.Abs, bias=bcol)
                me.activation(c.sb["spa"][:], c.sb["spa"][:], AF.Exp, scale=-1.0)
                me.activation(c.sb["spa"][:], c.sb["spa"][:], AF.Ln, bias=1.0)
                self.mk[("spl", sg, dc)] = me.mark()
                me.wait(self.W["vector"], self.mk.get(("spd", sg, dc), 0))
            for n in range(16 if "E2SCAN" in MSTAGES else 0):
                for dc in range(8):
                    k = n * 8 + dc
                    p2 = self.spred(sg, n, dc, 2)
                    if p2:
                        me.wait(self.W["vector"],
                                self.mk.get(("scn",) + p2, 0))
                    me.activation(c.sb["dApp"][:, k % 2, :],
                                  c.sb["delta"][:, dc, :], AF.Exp,
                                  scale=c.sb["asb"][:, dc, n:n + 1])
                    self.mk[("dA", sg, n, dc)] = me.mark()
            for tl in range(SEGTILES if "E2OUT" in MSTAGES else 0):
                i = sg * SEGTILES + tl
                me.wait(self.W["vector"], self.mk.get(f"xn{i}", 0))
                me.activation(c.sb["sqscr"][:, 0:DM], self.xnew(i), AF.Square,
                              accum_out=c.sb["stats_sq" + self.so][:, i:i + 1])
                self.mk[f"sq{i}"] = me.mark()
        if "E2" in MSTAGES:
            me.wait(self.W["vector"], self.mk.get("var", 0))
            me.activation(c.sb["tmpB"][:, 0:NT], c.sb["tmpA"][:, 0:NT], AF.Ln,
                          bias=EPS)
            me.activation(c.sb["stats_r" + self.so][:, 0:NT],
                          c.sb["tmpB"][:, 0:NT], AF.Exp, scale=-0.5)

    def emit_vector(self, E=None):
        c, me = self.ctx, self.W["vector"]
        me.memset(c.sb["carry"][:], 0.0)
        me.wait(self.W["scalar"], self.mk.get("aexp", 0))
        me.tensor_scalar(c.sb["asb"][:].rearrange("p a b -> p (a b)"),
                         c.sb["asb"][:].rearrange("p a b -> p (a b)"),
                         -1.0, None, ALU.mult)
        self.mk["aneg"] = me.mark()
        for sg in range(NSEG if "E0" in MSTAGES else 0):
            for tl in range(SEGTILES):
                i = sg * SEGTILES + tl
                me.wait(self.W["sync"], self.mk.get(f"ld{i}", 0))
                me.tensor_scalar(c.sb["m_xnbf"][:], self.xt(i),
                                 c.sb["stats_m" + self.si][:, i:i + 1],
                                 c.sb["stats_r" + self.si][:, i:i + 1],
                                 ALU.subtract, ALU.mult)
                self.mk[f"e0xn{i}"] = me.mark()
                me.wait(self.W["tensor"], self.mk.get(f"e0tr{i}", 0))
                if tl == 0 and sg > 0:
                    me.wait(self.W["tensor"], self.mk.get(("xzmm", sg - 1, 15), 0))
                me.tensor_copy(c.sb["xnTseg"][:, :, tl * 128:(tl + 1) * 128],
                               c.pgu_bf[:, 0:1024].rearrange(
                                   "p (a b) -> p a b", b=128))
                self.mk[f"e0t{i}"] = me.mark()
            for dc in range(16):
                me.wait(self.W["tensor"], self.mk.get(("xzmm", sg, dc), 0))
                if dc >= 2:
                    me.wait(self.W["gpsimd"], self.mk.get(("xrw", sg, dc - 2), 0))
                elif sg > 0:
                    me.wait(self.W["gpsimd"], self.mk.get(("xrw", sg - 1, 14 + dc), 0))
                me.tensor_copy(
                    c.sb["xrstage"][:, (dc % 2) * SEGT:(dc % 2) * SEGT + SEGT],
                    c.pgu[:, (dc % 2) * 512:(dc % 2) * 512 + 512])
                self.mk[("xrev", sg, dc)] = me.mark()
        me.wait_dma_upto(self.W["gpsimd"], self.mk.get("e0_dn", 0))
        me.memset(c.sb["cbuf"][:], 0.0)
        self.mk["cbz"] = me.mark()
        self.mk[("cvz", 0)] = self.mk["cbz"]
        for dcz in range(16 if "E1" in MSTAGES else 0):
            me.wait(self.W["gpsimd"], self.mk.get(("cvl", dcz), 0))
            kk = (c.sb["kx"] if dcz < 8 else c.sb["kz"])
            dc8 = dcz % 8
            me.tensor_single_scalar(c.sb["ct1"][:], c.sb["cbuf"][:, 0:S],
                                    kk[:, dc8, 0:1], ALU.mult)
            me.scalar_tensor_tensor(c.sb["ct2"][:], c.sb["cbuf"][:, 1:S + 1],
                                    kk[:, dc8, 1:2], c.sb["ct1"][:],
                                    ALU.mult, ALU.add)
            me.scalar_tensor_tensor(c.sb["ct1"][:], c.sb["cbuf"][:, 2:S + 2],
                                    kk[:, dc8, 2:3], c.sb["ct2"][:],
                                    ALU.mult, ALU.add)
            me.scalar_tensor_tensor(c.sb["ct2"][:], c.sb["cbuf"][:, 3:S + 3],
                                    kk[:, dc8, 3:4], c.sb["ct1"][:],
                                    ALU.mult, ALU.add)
            self.mk[("cv4", dcz)] = me.mark()
            me.wait(self.W["scalar"], self.mk.get(("gel", dcz), 0))
            self.mk[("cvz", dcz + 1)] = me.mark()
        for sg in range(NSEG if "E2" in MSTAGES else 0):
            if "E2X" not in MSTAGES:
                continue
            me.wait(self.W["tensor"], self.mk.get(("xdbl", sg), 0))
            me.tensor_copy(c.sb["dtlow"][0:64, :], c.pgu[0:64, 0:512])
            me.tensor_copy(c.sb["Bsb"][0:16, :], c.pgu[0:16, 512:1024])
            me.tensor_copy(c.sb["Csb"][0:16, :], c.pgu[0:16, 1536:2048])
            self.mk[("bc_ev", sg)] = me.mark()
            for dc in range(8):
                me.wait(self.W["scalar"], self.mk.get(("spl", sg, dc), 0))
                me.tensor_scalar(c.sb["spr"][:], c.ptf[:, 0:SEGT],
                                 c.sb["bdt"][:, dc:dc + 1], 0.0,
                                 ALU.add, ALU.max)
                me.tensor_tensor(c.sb["delta"][:, dc, :], c.sb["spa"][:],
                                 c.sb["spr"][:], ALU.add)
                self.mk[("spd", sg, dc)] = me.mark()
            me.wait(self.W["gpsimd"], self.mk.get(("ubf", sg), 0))
            for dc in range(8):
                me.tensor_tensor(c.sb["du"][:, dc, :], c.sb["delta"][:, dc, :],
                                 c.sb["ubf"][:, dc, :], ALU.mult)
            self.mk[("du", sg)] = me.mark()
            cr2 = c.sb["carry"][:].rearrange("p (a b) -> p a b", b=16)

            def yadd(n1, dc1):
                k1 = n1 * 8 + dc1
                if n1 == 0:
                    me.tensor_copy(c.sb["y"][:, dc1, :],
                                   c.sb["hcbbf"][:, k1 % 2, :])
                else:
                    me.tensor_tensor(c.sb["y"][:, dc1, :], c.sb["y"][:, dc1, :],
                                     c.sb["hcbbf"][:, k1 % 2, :], ALU.add)
                self.mk[("ya", sg, n1, dc1)] = me.mark()

            nloop = 16 if "E2SCAN" in MSTAGES else 0
            for n in range(nloop):
                me.wait(self.W["gpsimd"], self.mk.get(("cbc", sg, n), 0))
                for dc in range(8):
                    k = n * 8 + dc
                    me.wait(self.W["scalar"], self.mk.get(("dA", sg, n, dc), 0))
                    me.wait(self.W["gpsimd"],
                            self.mk.get(("dBu", sg, n, dc), 0))
                    me.op("tensor_tensor_scan", c.sb["hbpp"][:, k % 2, :],
                          c.sb["dApp"][:, k % 2, :], c.sb["dBubf"][:, k % 2, :],
                          c.sb["carry"][:, dc * 16 + n:dc * 16 + n + 1],
                          ALU.mult, ALU.add)
                    self.mk[("scn", sg, n, dc)] = me.mark()
                    if dc % 2 == 1:
                        me.tensor_copy(cr2[:, dc - 1:dc + 1, n:n + 1],
                                       c.sb["hbpp"][:, :, SEGT - 1:SEGT])
                    me.tensor_tensor(c.sb["hcbbf"][:, k % 2, :],
                                     c.sb["hbpp"][:, k % 2, :],
                                     c.sb["cbc" if n % 2 == 0 else
                                           "spr"][:], ALU.mult)
                    yadd(n, dc)
            for dc in range(8):
                me.scalar_tensor_tensor(c.sb["y"][:, dc, :], c.sb["ubf"][:, dc, :],
                                        c.sb["dsb"][:, dc:dc + 1],
                                        c.sb["y"][:, dc, :], ALU.mult, ALU.add)
                me.tensor_copy(c.sb["ybf"][:, dc, :], c.sb["y"][:, dc, :])
            me.wait(self.W["gpsimd"], self.mk.get(("zbf", sg), 0))
            self.mk[("ydone", sg)] = me.mark()
            for tl in range(SEGTILES if "E2OUT" in MSTAGES else 0):
                i = sg * SEGTILES + tl
                me.wait(self.W["sync"], self.mk.get(f"ld2_{i}", 0))
                me.wait(self.W["tensor"], self.mk.get(("omm", sg, tl), 0))
                if i >= 2:
                    me.wait(self.W["sync"], self.mk.get(f"st{i-2}", 0))
                    me.wait(self.W["scalar"], self.mk.get(f"sq{i-2}", 0))
                me.op("_custom_dve", AFFINE_THEN_ADD, out=self.xnew(i),
                      in0=c.ptf[:, 512:1536], in1=self.xt(i), s0=1.0, s1=0.0)
                self.mk[f"xn{i}"] = me.mark()
                me.reduce_sum(c.sb["msc"][:, 0:1], self.xnew(i), AX.X)
                me.tensor_scalar(c.sb["stats_m" + self.so][:, i:i + 1],
                                 c.sb["msc"][:, 0:1], 1.0 / DM, None, ALU.mult)
            self.mk[("useg_done", sg)] = me.mark()
        if "E2" in MSTAGES:
            me.wait(self.W["scalar"], self.mk.get(f"sq{NT-1}", 0))
            me.tensor_scalar(c.sb["tmpA"][:, 0:NT],
                             c.sb["stats_sq" + self.so][:, 0:NT],
                             1.0 / DM, None, ALU.mult)
            me.tensor_tensor(c.sb["tmpB"][:, 0:NT],
                             c.sb["stats_m" + self.so][:, 0:NT],
                             c.sb["stats_m" + self.so][:, 0:NT], ALU.mult)
            me.tensor_tensor(c.sb["tmpA"][:, 0:NT], c.sb["tmpA"][:, 0:NT],
                             c.sb["tmpB"][:, 0:NT], ALU.subtract)
            self.mk["var"] = me.mark()

    def emit_tensor(self, E=None):
        c, me = self.ctx, self.W["tensor"]
        me.wait(self.W["gpsimd"], self.mk.get("win", 0))
        for sg in range(NSEG if "E0" in MSTAGES else 0):
            for tl in range(SEGTILES):
                i = sg * SEGTILES + tl
                me.wait(self.W["vector"], self.mk.get(f"e0xn{i}", 0))
                if i > 0:
                    me.wait(self.W["vector"], self.mk.get(f"e0t{i-1}", 0))
                for cc in range(8):
                    me.transpose(c.pgu_bf[:, cc * 128:(cc + 1) * 128],
                                 c.sb["m_xnbf"][:, cc * 128:(cc + 1) * 128],
                                 c.sb["identbf"][:])
                self.mk[f"e0tr{i}"] = me.mark()
            me.wait(self.W["vector"], self.mk.get(f"e0t{sg * SEGTILES + SEGTILES - 1}", 0))
            for dc in range(16):
                if dc >= 2:
                    me.wait(self.W["vector"], self.mk.get(("xrev", sg, dc - 2), 0))
                elif sg > 0:
                    me.wait(self.W["vector"], self.mk.get(("xrev", sg - 1, 14 + dc), 0))
                for cc in range(8):
                    me.matmul(c.pgu[:, (dc % 2) * 512:(dc % 2) * 512 + 512],
                              c.sb["Win_b"][:, cc, dc * 128:(dc + 1) * 128],
                              c.sb["xnTseg"][:, cc, :],
                              start=(cc == 0), stop=(cc == 7))
                self.mk[("xzmm", sg, dc)] = me.mark()
        me.wait(self.W["gpsimd"], self.mk.get("wxp", 0))
        me.wait(self.W["gpsimd"], self.mk.get("wdt", 0))
        me.wait(self.W["gpsimd"], self.mk.get("wout", 0))
        for sg in range(NSEG if ("E2" in MSTAGES and "E2X" in MSTAGES) else 0):
            me.wait(self.W["gpsimd"], self.mk.get(("ubf", sg), 0))
            if sg > 0:
                me.wait(self.W["vector"], self.mk.get(("useg_done", sg - 1), 0))
            for (o0, w, bank) in ((0, 64, 0), (64, 16, 512), (80, 16, 1536)):
                for cc in range(8):
                    me.matmul(c.pgu[0:w, bank:bank + SEGT],
                              c.sb["Wxp_b"][:, cc, o0:o0 + w],
                              c.sb["ubf"][:, cc, :],
                              start=(cc == 0), stop=(cc == 7))
            self.mk[("xdbl", sg)] = me.mark()
            me.wait(self.W["vector"], self.mk.get(("bc_ev", sg), 0))
            for dc in range(8):
                if dc > 0:
                    me.wait(self.W["vector"], self.mk.get(("spd", sg, dc - 1), 0))
                    me.wait(self.W["scalar"], self.mk.get(("spl", sg, dc - 1), 0))
                me.matmul(c.ptf[:, 0:SEGT], c.sb["Wdt_b"][0:64, dc, :],
                          c.sb["dtlow"][0:64, :], start=True, stop=True)
                self.mk[("dtp", sg, dc)] = me.mark()
            me.wait(self.W["vector"], self.mk.get(("ydone", sg), 0))
            me.wait(self.W["scalar"], self.mk.get(("spl", sg, 7), 0))
            me.wait(self.W["vector"], self.mk.get(("spd", sg, 7), 0))
            for tl in range(SEGTILES if "E2OUT" in MSTAGES else 0):
                if not (sg == 0 and tl == 0):
                    pi = sg * SEGTILES + tl - 1
                    me.wait(self.W["vector"], self.mk.get(f"xn{pi}", 0))
                for dcz in range(16):
                    lhs = (c.sb["ybf"] if dcz < 8 else c.sb["zbf"])[
                        :, dcz % 8, tl * 128:(tl + 1) * 128]
                    for half in range(2):
                        me.matmul(c.ptf[:, 512 + half * 512:512 + half * 512 + 512],
                                  lhs, c.sb["Wout_b"][:, dcz,
                                                      half * 512:half * 512 + 512],
                                  start=(dcz == 0), stop=(dcz == 15))
                self.mk[("omm", sg, tl)] = me.mark()




def make_phases(ctx, names):
    out = []
    cur = "0"
    for pn in names:
        nxt = "1" if cur == "0" else "0"
        if pn == "S0":
            out.append(StatsPrepass(ctx))
            continue
        elif pn == "W0":
            out.append(FFNWeightPrep(ctx))
            continue
        elif pn == "A":
            out.append(FFNPhase(ctx, cur, nxt, "ffa"))
        elif pn == "B":
            out.append(AttnPhase(ctx, cur, nxt, "atb"))
        elif pn == "C":
            out.append(FFNPhase(ctx, cur, nxt, "ffc", load_stash=True))
        elif pn == "WM":
            out.append(MoEWeightPrep(ctx))
            continue
        elif pn == "D":
            out.append(MoEPhase(ctx, cur, nxt, "moed"))
        elif pn == "F":
            out.append(MoEPhase(ctx, cur, nxt, "moef"))
        elif pn == "E":
            out.append(MambaPhase(ctx, cur, nxt, "mam"))
        else:
            raise ValueError(pn)
        cur = nxt
    return out


def build_nc(phase_names):
    nc = bass.Bass("TRN2", target_bir_lowering=False)
    es = ExitStack()
    c = Ctx(nc, es)

    c.dram_t("x_in", [S, DM], F32, "ExternalInput")
    c.dram_t("x_state", [S, DM], F32, "ExternalOutput")
    for nm, shp in [("w_gate", [DFF, DM]), ("w_up", [DFF, DM]),
                    ("w_down", [DM, DFF]),
                    ("d_ident", [128, 128]), ("d_cst", [128, 8])]:
        c.dram_t(nm, shp, F32, "ExternalInput")
    for nm in ("WqT", "WkT", "WvT", "WfcT"):
        c.dram_t(nm, [DM, DM], F32, "ExternalInput")
    c.dram_t("keysR", [128, NEXP * 1024], F32, "ExternalInput")
    c.dram_t("valuesR", [128, NEXP * 1024], F32, "ExternalInput")
    c.dram_t("selT", [DM, 64], F32, "ExternalInput")
    c.dram_t("values_st", [128, NEXP * 1024], BF16, "Internal")
    c.dram_t("keys_st", [128, NEXP * 1024], BF16, "Internal")
    c.dram_t("WinR", [128, 16384], F32, "ExternalInput")
    c.dram_t("WoutR", [128, 16384], F32, "ExternalInput")
    c.dram_t("WxpR", [128, 768], F32, "ExternalInput")
    c.dram_t("WdtR", [64, 1024], F32, "ExternalInput")
    c.dram_t("KxR", [128, 32], F32, "ExternalInput")
    c.dram_t("KzR", [128, 32], F32, "ExternalInput")
    c.dram_t("bxR", [128, 8], F32, "ExternalInput")
    c.dram_t("bzR", [128, 8], F32, "ExternalInput")
    c.dram_t("AlogR", [128, 128], F32, "ExternalInput")
    c.dram_t("DR", [128, 8], F32, "ExternalInput")
    c.dram_t("bdtR", [128, 8], F32, "ExternalInput")
    c.dram_t("xraw_st", [128, 16 * S], F32, "Internal")
    c.dram_t("u_st", [128, 8 * S], BF16, "Internal")
    c.dram_t("z_st", [128, 8 * S], BF16, "Internal")
    c.dram_t("bc_st", [16, SEGT], F32, "Internal")
    c.dram_t("cc_st", [16, SEGT], F32, "Internal")
    c.dram_t("WgqT_st", [128, 8 * DFF], BF16, "Internal")
    c.dram_t("WuqT_st", [128, 8 * DFF], BF16, "Internal")
    c.dram_t("WdqT_st", [128, 22 * DM], BF16, "Internal")

    c.sbuf("cst", [128, 8], F32)
    c.sbuf("identbf", [128, 128], BF16)
    c.sbuf("onesbf", [128, 128], BF16)
    c.sbuf("xt", [128, 2 * DM], F32)
    c.sbuf("xnew", [128, 2 * DM], F32)
    c.sbuf("msc", [128, 4], F32)
    c.sbuf("sqscr", [128, DM], F32)
    c.sbuf("tmpA", [128, NT], F32)
    c.sbuf("tmpB", [128, NT], F32)
    c.sbuf("mk2", [128, 16], F32)
    for t in ("0", "1"):
        c.sbuf("stats_m" + t, [128, NT], F32)
        c.sbuf("stats_sq" + t, [128, NT], F32)
        c.sbuf("stats_r" + t, [128, NT], F32)
    arena = c.sbuf("arena", [128, 186 * 1024], mybir.dt.uint8)
    arv = arena[:]

    def carve(name, off, shape, dtype):
        n = int(np.prod(shape[1:]))
        ap = arv[:, off:off + n * mybir.dt.size(dtype)].bitcast(dtype)
        if len(shape) == 3:
            ap = ap.rearrange("p (a b) -> p a b", b=shape[2])
        elif len(shape) == 4:
            ap = ap.rearrange("p (a b cd) -> p a b cd", b=shape[2], cd=shape[3])
        c.sb[name] = ap
        return ap

    carve("Wg", 0, [128, 8 * DFF], BF16)
    carve("Wu", 45056, [128, 8 * DFF], BF16)
    carve("Wd", 90112, [128, 22 * DM], BF16)
    carve("wtile", 135168, [128, DFF], F32)
    carve("xqbf", 146432, [128, DM], BF16)
    carve("xqT", 148480, [128, DM], BF16)
    carve("hqbf", 150528, [128, DFF], BF16)
    carve("hqT", 156160, [128, DFF], BF16)
    carve("sgate", 161792, [128, 512], F32)
    carve("hpf", 163840, [128, 512], F32)
    carve("xc", 165888, [128, DM], F32)
    carve("q_a", 169984, [128, DM], F32)
    carve("q_xn", 174080, [128, DM], F32)
    carve("q_g2", 178176, [128, DM], F32)
    carve("q_sg", 182272, [128, DM], F32)
    carve("q_bm", 186368, [128, 64], F32)
    carve("q_scl", 186624, [128, 64], F32)
    carve("q_rec", 186880, [128, 64], F32)
    carve("q_sclF", 187136, [128, 64], F32)
    carve("Wat", 0, [128, 4 * 8 * 1024], BF16)
    carve("xnbf", 65536, [128, DM], BF16)
    carve("xnT", 67584, [128, DM], BF16)
    carve("qbf", 69632, [128, DM], BF16)
    carve("kbf", 71680, [128, DM], BF16)
    carve("vbf", 73728, [128, DM], BF16)
    carve("abf", 75776, [128, DM], BF16)
    carve("aT", 77824, [128, DM], BF16)
    carve("att_m4", 79872, [128, 4096], BF16)
    carve("att_sc", 88064, [128, 256], F32)
    carve("att_E", 89088, [128, 256], F32)
    carve("att_A", 90112, [128, 256], BF16)
    carve("att_mx", 90624, [128, 16], F32)
    carve("att_sm", 90688, [128, 16], F32)
    carve("att_rc", 90752, [128, 16], F32)
    carve("aof", 91136, [128, DM], F32)

    carve("keysb", 0, [128, 8, 8192], BF16)
    carve("m_xt", 131072, [128, 2, DM], F32)
    carve("xnTs", 139264, [128, 8, 768], BF16)
    carve("outac", 151552, [128, 6, DM], BF16)
    carve("vbuf", 163840, [128, 4, 2048], BF16)
    carve("scb", 180224, [128, 2, 512], BF16)
    carve("scT", 182272, [128, 2, 512], BF16)
    carve("gates_bf", 184320, [128, 2, 384], BF16)
    carve("selsb", 185856, [128, 2, 64], F32)
    carve("selw", 186368, [128, 64], F32)
    carve("selm", 186624, [128, 64], F32)
    carve("selTb", 186880, [128, 8, 64], BF16)
    carve("m_xnbf", 187904, [128, DM], BF16)
    carve("wmA", 0, [128, 2, 8192], F32)
    carve("wmB", 65536, [128, 2, 8192], BF16)

    carve("Win_b", 0, [128, 8, 2048], BF16)
    carve("Wout_b", 32768, [128, 16, 1024], BF16)
    carve("Wxp_b", 65536, [128, 8, 96], BF16)
    carve("Wdt_b", 67072, [128, 8, 128], BF16)
    carve("kx", 69120, [128, 8, 4], F32)
    carve("kz", 69248, [128, 8, 4], F32)
    carve("bx", 69376, [128, 8], F32)
    carve("bz", 69440, [128, 8], F32)
    carve("alog", 69504, [128, 8, 16], F32)
    carve("asb", 70016, [128, 8, 16], F32)
    carve("dsb", 70528, [128, 8], F32)
    carve("bdt", 70560, [128, 8], F32)
    carve("carry", 70592, [128, 128], F32)
    carve("bbc", 71104, [128, SEGT], F32)
    carve("cbc", 73152, [128, SEGT], F32)
    carve("dtlow", 75200, [128, SEGT], BF16)
    carve("Bsb", 76224, [128, SEGT], F32)
    carve("Csb", 78272, [128, SEGT], F32)
    carve("spa", 80320, [128, SEGT], F32)
    carve("spr", 82368, [128, SEGT], F32)
    carve("dA", 84416, [128, SEGT], F32)
    carve("dBubf", 86464, [128, 2, SEGT], BF16)
    carve("hbpp", 88512, [128, 2, SEGT], F32)
    carve("hcbbf", 182720, [128, 2, SEGT], BF16)
    carve("dApp", 184768, [128, 2, SEGT], F32)
    carve("ubf", 92608, [128, 8, SEGT], BF16)
    carve("zbf", 100800, [128, 8, SEGT], BF16)
    carve("ybf", 108992, [128, 8, SEGT], BF16)
    carve("delta", 117184, [128, 8, SEGT], F32)
    carve("du", 133568, [128, 8, SEGT], F32)
    carve("y", 149952, [128, 8, SEGT], F32)
    carve("e_xt", 166336, [128, 4, DM], F32)
    carve("xnTseg", 92608, [128, 8, SEGT], BF16)
    carve("xrstage", 100800, [128, 2 * SEGT], F32)
    carve("ct1", 80320, [128, S], F32)
    carve("ct2", 96704, [128, S], F32)
    carve("cbuf", 113088, [128, 4112], F32)
    carve("cgout", 129536, [128, S], BF16)

    pt = c.psum("PT", [128, 2048], F32)
    c.pt_bf = pt[:].bitcast(BF16)
    c.ptf = pt[:]
    pgu = c.psum("PGU", [128, 2048], F32)
    c.pgu = pgu[:]
    c.pgu_bf = pgu[:].bitcast(BF16)

    cstv = [0.0, 1.0, 0.5, -0.5, -1.0, 0.125, EPS, 1.75]
    for k, v in enumerate(cstv):
        nc.const_aps.register(F32, v, c.sb["cst"][:, k:k + 1])

    c.EW = {}
    for n in ("sync", "scalar", "vector", "tensor", "gpsimd"):
        dsems = [c.sem(f"d_{n}_{s}") for s in range(NSLOT)] \
            if n in ("sync", "gpsimd") else []
        c.EW[n] = W(n, c.sem(f"c_{n}"), dsems,
                    serial=(n in ("vector", "scalar", "gpsimd")))
    phases = make_phases(c, phase_names)
    PB = c.sem("phase_barrier")
    cst_sem = c.sem("cst_load")
    id_sem = c.sem("ident_load")

    counts = {}
    for ph in phases:
        ph.emit_sync()
        ph.emit_scalar()
        ph.emit_vector()
        ph.emit_tensor()
        ph.emit_gpsimd()
        for n in ph.W:
            counts[(ph.tag, n)] = (ph.W[n].n, ph.W[n].dn)
    for w in c.EW.values():
        w.reset()
    marksnap = [dict(ph.mk) for ph in phases]

    def run_phases(eng, name, emit_name):
        wrap = c.EW[name].bind(eng)
        for k, ph in enumerate(phases):
            eng.wait_ge(PB, 5 * k)
            getattr(ph, emit_name)()
            assert (wrap.n, wrap.dn) == counts[(ph.tag, name)], (
                ph.tag, name, wrap.n, wrap.dn, counts[(ph.tag, name)])
            wrap.wait_self_all()
            eng.sem_inc(PB, 1)
        wrap.unbind()

    with nc.Block() as block:
        @block.sync
        def _(eng):
            eng.dma_start(out=c.sb["cst"][:], in_=c.dram["d_cst"]).then_inc(cst_sem, 16)
            eng.wait_ge(cst_sem, 16)
            eng.wait_ge(id_sem, 16)
            run_phases(eng, "sync", "emit_sync")

        @block.scalar
        def _(eng):
            eng.wait_ge(cst_sem, 16)
            eng.wait_ge(id_sem, 16)
            run_phases(eng, "scalar", "emit_scalar")

        @block.vector
        def _(eng):
            eng.wait_ge(cst_sem, 16)
            eng.wait_ge(id_sem, 16)
            run_phases(eng, "vector", "emit_vector")

        @block.tensor
        def _(eng):
            eng.wait_ge(cst_sem, 16)
            eng.wait_ge(id_sem, 16)
            run_phases(eng, "tensor", "emit_tensor")

        @block.gpsimd
        def _(eng):
            eng.dma_start(out=c.sb["identbf"][:], in_=c.dram["d_ident"]).then_inc(id_sem, 16)
            eng.wait_ge(cst_sem, 16)
            run_phases(eng, "gpsimd", "emit_gpsimd")

    for ph, snap in zip(phases, marksnap):
        if ph.mk != snap:
            diffs = [(k, snap.get(k), ph.mk.get(k))
                     for k in set(snap) | set(ph.mk)
                     if snap.get(k) != ph.mk.get(k)]
            raise AssertionError(f"mark drift in {ph.tag}: {diffs[:8]}")
    mybir.codegen_inst_isa_subclasses(nc)
    return nc, es


def prep_host_ffn(inputs):
    f32 = np.float32
    x = np.ascontiguousarray(inputs["L0_attn_x"], dtype=f32)
    shared = {
        "w_gate": np.ascontiguousarray(inputs["L0_ffn_w_gate"], f32),
        "w_up": np.ascontiguousarray(inputs["L0_ffn_w_up"], f32),
        "w_down": np.ascontiguousarray(inputs["L0_ffn_w_down"], f32),
        "WqT": np.ascontiguousarray(inputs["L0_attn_Wq"].T, f32),
        "WkT": np.ascontiguousarray(inputs["L0_attn_Wk"].T, f32),
        "WvT": np.ascontiguousarray(inputs["L0_attn_Wv"].T, f32),
        "WfcT": np.ascontiguousarray(inputs["L0_attn_Wfc"].T, f32),
        "keysR": np.ascontiguousarray(
            inputs["L1_ffn_keys"].reshape(NEXP, 8, 128, 128).transpose(2, 1, 0, 3)
            .reshape(128, NEXP * 1024), f32),
        "valuesR": np.ascontiguousarray(
            inputs["L1_ffn_values"].transpose(1, 0, 2).reshape(128, NEXP * 1024), f32),
        "selT": np.ascontiguousarray(inputs["L1_ffn_expert_sel"].T, f32),
        "WinR": np.ascontiguousarray(
            inputs["L1_attn_W_in"].reshape(8, 128, 2048).transpose(1, 0, 2)
            .reshape(128, 16384), f32),
        "WoutR": np.ascontiguousarray(
            inputs["L1_attn_W_out"].reshape(16, 128, 1024).transpose(1, 0, 2)
            .reshape(128, 16384), f32),
        "WxpR": np.ascontiguousarray(
            inputs["L1_attn_W_xproj"].reshape(8, 128, 96).transpose(1, 0, 2)
            .reshape(128, 768), f32),
        "WdtR": np.ascontiguousarray(inputs["L1_attn_W_dt"], f32),
        "KxR": np.ascontiguousarray(
            inputs["L1_attn_K_x"].reshape(8, 128, 4).transpose(1, 0, 2)
            .reshape(128, 32), f32),
        "KzR": np.ascontiguousarray(
            inputs["L1_attn_K_z"].reshape(8, 128, 4).transpose(1, 0, 2)
            .reshape(128, 32), f32),
        "bxR": np.ascontiguousarray(
            inputs["L1_attn_b_x"].reshape(8, 128).T, f32),
        "bzR": np.ascontiguousarray(
            inputs["L1_attn_b_z"].reshape(8, 128).T, f32),
        "AlogR": np.ascontiguousarray(
            inputs["L1_attn_A_log"].reshape(8, 128, 16).transpose(1, 0, 2)
            .reshape(128, 128), f32),
        "DR": np.ascontiguousarray(
            inputs["L1_attn_D"].reshape(8, 128).T, f32),
        "bdtR": np.ascontiguousarray(
            inputs["L1_attn_b_dt"].reshape(8, 128).T, f32),
        "d_ident": np.eye(128, dtype=f32),
        "d_cst": np.tile(np.array([0.0, 1.0, 0.5, -0.5, -1.0, 0.125, EPS, 1.75],
                                  f32), (128, 1)),
    }
    in_maps = []
    for cix in range(x.shape[0]):
        m = dict(shared)
        m["x_in"] = np.ascontiguousarray(x[cix])
        in_maps.append(m)
    return in_maps


def run_ffn_test(inputs, ncores=8):
    in_maps = prep_host_ffn(inputs)[:ncores]
    nc, es = build_nc(["S0", "W0", "A"])
    try:
        res = run_bass_kernel_spmd(nc, in_maps, core_ids=list(range(ncores)))
    finally:
        es.close()
    return np.stack([r["x_state"] for r in res.results])


def run_attn_test(inputs, x_prev, ncores=8):
    in_maps = prep_host_ffn(inputs)[:ncores]
    for cix in range(ncores):
        in_maps[cix]["x_in"] = np.ascontiguousarray(x_prev[cix].astype(np.float32))
    nc, es = build_nc(["S0", "B"])
    try:
        res = run_bass_kernel_spmd(nc, in_maps, core_ids=list(range(ncores)))
    finally:
        es.close()
    return np.stack([r["x_state"] for r in res.results])


FULL_CHAIN = ["S0", "W0", "A", "WM", "B", "C", "D", "E", "F"]


def kernel(**inputs):
    in_maps = prep_host_ffn(inputs)
    nc, es = build_nc(FULL_CHAIN)
    try:
        res = run_bass_kernel_spmd(nc, in_maps, core_ids=list(range(8)))
    finally:
        es.close()
    return np.stack([r["x_state"] for r in res.results]).astype(np.float32)


# version 06 (final)
# speedup vs baseline: 1.6147x; 1.0513x over previous
import numpy as np
import ml_dtypes
from contextlib import ExitStack

import concourse.bass as bass
import concourse.mybir as mybir
from concourse.bass_utils import run_bass_kernel_spmd
from concourse.dve_op import DveOp
from concourse.dve_spec import Spec, Src0, Src1, C0, C1, C2, C3, lower as dve_lower
from concourse.dve_table_gen import dve_ver_for
import concourse.dve_ops as dve_ops_mod
from concourse.dve_ops import AFFINE_THEN_ADD

F32 = mybir.dt.float32
BF16 = mybir.dt.bfloat16
I32 = mybir.dt.int32
AF = mybir.ActivationFunctionType
ALU = mybir.AluOpType
AX = mybir.AxisListType

_VER = dve_ver_for("TRN2")
DEBUG_ATTN = False


def _reg_op(name, spec, reference):
    for op in dve_ops_mod.OPS:
        if op.name == name:
            return op
    sha = dve_lower(spec, ver=_VER, perf_en=False, name=name).sha(_VER)
    op = DveOp(name, spec, False, {_VER: sha}, reference=reference)
    dve_ops_mod.OPS.append(op)
    dve_ops_mod._SUB_OPCODE_FOR_NAME[name] = (
        dve_ops_mod._CUSTOM_DVE_ROW_BASE + len(dve_ops_mod.OPS) - 1)
    dve_ops_mod.CUSTOM_DVE_REFS[name] = op.ref
    return op


SNAP1 = _reg_op(
    "ANT_SNAP1",
    Spec(body=((Src0 > C0) + (Src0 > C1)) + ((Src0 > C2) + (Src0 > C3))),
    lambda in0, in1, s0, s1, imm2: (
        (in0 > s0).astype(np.float64) + (in0 > s1) + (in0 > imm2) + (in0 > in1)),
)


def _snap2_spec():
    c5 = Src0 > C0
    c6 = Src0 > C1
    n7 = Src0 > C2
    t3 = (c5 + c6) + (n7 + n7)
    return Spec(body=Src1 + (t3 + t3))


SNAP2 = _reg_op(
    "ANT_SNAP2",
    _snap2_spec(),
    lambda in0, in1, s0, s1, imm2: in1 + 2.0 * (
        (in0 > s0).astype(np.float64) + (in0 > s1) + 2.0 * (in0 > imm2)),
)

S = 4096
DM = 1024
NT = S // 128
DFF = 2816
EPS = 1e-5
OCS = [(0, 512), (512, 512), (1024, 512), (1536, 512), (2048, 512), (2560, 256)]
NOC = len(OCS)


def bc16(small_ap, nb, k=16):
    return small_ap.unsqueeze(2).broadcast_to([128, nb, k])


def r3(ap, k=16):
    return ap.rearrange("p (nb k) -> p nb k", k=k)


_DMA_METHODS = frozenset({"dma_start", "dma_start_transpose", "indirect_dma_start"})


NSLOT = 6


class W:

    def __init__(self, name, sem, dsems, serial):
        self.name = name
        self.sem = sem
        self.dsems = dsems
        self.serial = serial
        self.eng = None
        self.n = 0
        self.dn = 0

    def bind(self, eng):
        self.eng = eng
        return self

    def reset(self):
        self.eng = None
        self.n = 0
        self.dn = 0

    def unbind(self):
        self.eng = None

    def op(self, meth, *a, **k):
        if meth in _DMA_METHODS:
            slot = self.dn % NSLOT
            if self.eng is not None:
                prev = self.dn // NSLOT
                if prev > 0:
                    self.eng.wait_ge(self.dsems[slot], 16 * prev)
                inst = getattr(self.eng, meth)(*a, **k)
                inst.then_inc(self.dsems[slot], 16)
            self.dn += 1
            return ("D", self.dn)
        if self.eng is not None:
            if self.serial and self.n > 0:
                self.eng.wait_ge(self.sem, self.n)
            inst = getattr(self.eng, meth)(*a, **k)
            inst.then_inc(self.sem, 1)
        self.n += 1
        return self.n

    def __getattr__(self, meth):
        if meth.startswith("_ant"):
            raise AttributeError(meth)
        return lambda *a, **k: self.op(meth, *a, **k)

    def wait(self, other, val):
        if self.eng is None:
            return
        if isinstance(val, tuple):
            m = val[1]
            if m <= 0:
                return
            self.eng.wait_ge(other.dsems[(m - 1) % NSLOT],
                             16 * ((m - 1) // NSLOT + 1))
            return
        if val > 0:
            if other is self and self.serial:
                return
            self.eng.wait_ge(other.sem, val)

    def wait_self_compute(self):
        if self.eng is not None and self.n > 0:
            self.eng.wait_ge(self.sem, self.n)

    def wait_dma_upto(self, other, dn):
        if self.eng is None or not dn:
            return
        for s in range(NSLOT):
            cnt = (dn - s + NSLOT - 1) // NSLOT
            if cnt > 0:
                self.eng.wait_ge(other.dsems[s], 16 * cnt)

    def wait_self_all(self):
        if self.eng is None:
            return
        if self.n > 0:
            self.eng.wait_ge(self.sem, self.n)
        for s in range(NSLOT):
            cnt = (self.dn - s + NSLOT - 1) // NSLOT
            if cnt > 0:
                self.eng.wait_ge(self.dsems[s], 16 * cnt)

    def mark(self):
        return self.n


class Ctx:
    def __init__(self, nc, es):
        self.nc = nc
        self.es = es
        self.sb = {}
        self.dram = {}

    def sbuf(self, name, shape, dtype):
        t = self.es.enter_context(self.nc.sbuf_tensor(name, shape, dtype))
        self.sb[name] = t
        return t

    def psum(self, name, shape, dtype=F32):
        return self.es.enter_context(self.nc.psum_tensor(name, shape, dtype))

    def sem(self, name):
        return self.es.enter_context(self.nc.semaphore(name))

    def dram_t(self, name, shape, dtype, kind):
        ap = self.nc.dram_tensor(name, shape, dtype, kind=kind).ap()
        self.dram[name] = ap
        return ap


class QuantHelper:

    def __init__(self, ctx):
        c = ctx
        self.a = c.sb["q_a"]
        self.xn = c.sb["q_xn"]
        self.g2 = c.sb["q_g2"]
        self.sg = c.sb["q_sg"]
        self.bm = c.sb["q_bm"]
        self.scl = c.sb["q_scl"]
        self.rec = c.sb["q_rec"]
        self.sclF = c.sb["q_sclF"]
        self.c175 = c.sb["cst"][:, 7:8]

    def v_part1(self, v, src, N, rstd_ap=None):
        nb = N // 16
        v.tensor_reduce(self.bm[:, :nb], r3(src), AX.X, ALU.max,
                        apply_absolute_value=True)
        v.tensor_scalar(self.scl[:, :nb], self.bm[:, :nb], 1.0 / 6.0, 1e-12,
                        ALU.mult, ALU.max)
        v.reciprocal(self.rec[:, :nb], self.scl[:, :nb])
        if rstd_ap is not None:
            v.tensor_scalar(self.sclF[:, :nb], self.scl[:, :nb], rstd_ap, 0.5,
                            ALU.mult, ALU.mult)
        else:
            v.tensor_scalar(self.sclF[:, :nb], self.scl[:, :nb], 0.5, None,
                            ALU.mult)

    def s_part(self, sc, src, N):
        sc.activation(self.a[:, :N], src, AF.Abs)
        sc.activation(self.sg[:, :N], src, AF.Sign)

    def v_part2(self, v, dst, N):
        nb = N // 16
        v.tensor_tensor(r3(self.xn[:, :N]), r3(self.a[:, :N]),
                        bc16(self.rec[:, :nb], nb), ALU.mult)
        v.op("_custom_dve", SNAP1, out=self.a[:, :N], in0=self.xn[:, :N],
             in1=self.c175, s0=0.25, s1=0.75, imm2=1.25)
        v.op("_custom_dve", SNAP2, out=self.g2[:, :N], in0=self.xn[:, :N],
             in1=self.a[:, :N], s0=2.5, s1=3.5, imm2=5.0)
        v.tensor_tensor(r3(self.a[:, :N]), r3(self.g2[:, :N]),
                        bc16(self.sclF[:, :nb], nb), ALU.mult)
        v.tensor_tensor(dst, self.a[:, :N], self.sg[:, :N], ALU.mult)


class Phase:
    def __init__(self, ctx, tag):
        self.ctx = ctx
        self.tag = tag
        self.mk = {}
        self.W = ctx.EW

    def emit_sync(self): pass
    def emit_scalar(self): pass
    def emit_vector(self): pass
    def emit_tensor(self): pass
    def emit_gpsimd(self): pass


class StatsPrepass(Phase):
    def __init__(self, ctx):
        super().__init__(ctx, "s0")

    def emit_sync(self):
        c, me = self.ctx, self.W["sync"]
        me.dma_start(out=c.dram["x_state"], in_=c.dram["x_in"])
        xi = c.dram["x_in"].rearrange("(t p) d -> t p d", p=128)
        for i in range(NT):
            if i >= 2:
                me.wait(self.W["vector"], self.mk.get(f"v{i-2}", 0))
                me.wait(self.W["scalar"], self.mk.get(f"s{i-2}", 0))
            self.mk[f"ld{i}"] = me.dma_start(
                out=c.sb["xt"][:, (i % 2) * DM:(i % 2) * DM + DM], in_=xi[i])
        me.wait_self_all()

    def emit_vector(self):
        c, me = self.ctx, self.W["vector"]
        for i in range(NT):
            me.wait(self.W["sync"], self.mk.get(f"ld{i}", 0))
            xt = c.sb["xt"][:, (i % 2) * DM:(i % 2) * DM + DM]
            me.reduce_sum(c.sb["msc"][:, 0:1], xt, AX.X)
            me.tensor_scalar(c.sb["stats_m0"][:, i:i + 1], c.sb["msc"][:, 0:1],
                             1.0 / DM, None, ALU.mult)
            self.mk[f"v{i}"] = me.mark()
        me.wait(self.W["scalar"], self.mk.get(f"s{NT-1}", 0))
        me.tensor_scalar(c.sb["tmpA"][:, 0:NT], c.sb["stats_sq0"][:, 0:NT],
                         1.0 / DM, None, ALU.mult)
        me.tensor_tensor(c.sb["tmpB"][:, 0:NT], c.sb["stats_m0"][:, 0:NT],
                         c.sb["stats_m0"][:, 0:NT], ALU.mult)
        me.tensor_tensor(c.sb["tmpA"][:, 0:NT], c.sb["tmpA"][:, 0:NT],
                         c.sb["tmpB"][:, 0:NT], ALU.subtract)
        self.mk["var"] = me.mark()

    def emit_scalar(self):
        c, me = self.ctx, self.W["scalar"]
        for i in range(NT):
            me.wait(self.W["sync"], self.mk.get(f"ld{i}", 0))
            xt = c.sb["xt"][:, (i % 2) * DM:(i % 2) * DM + DM]
            me.activation(c.sb["sqscr"][:, 0:DM], xt, AF.Square,
                          accum_out=c.sb["stats_sq0"][:, i:i + 1])
            self.mk[f"s{i}"] = me.mark()
        me.wait(self.W["vector"], self.mk.get("var", 0))
        me.activation(c.sb["tmpB"][:, 0:NT], c.sb["tmpA"][:, 0:NT], AF.Ln,
                      bias=EPS)
        me.activation(c.sb["stats_r0"][:, 0:NT], c.sb["tmpB"][:, 0:NT],
                      AF.Exp, scale=-0.5)


class FFNWeightPrep(Phase):
    def __init__(self, ctx):
        super().__init__(ctx, "w0")
        self.specs = [
            ("w_gate", DM, "Wg"),
            ("w_up", DM, "Wu"),
            ("w_down", DFF, "Wd"),
        ]
        self.tiles = []
        for k, (nm, cols, dst) in enumerate(self.specs):
            rows = DFF if cols == DM else DM
            for i in range(rows // 128):
                self.tiles.append((k, i))
        self.q = QuantHelper(ctx)

    @staticmethod
    def chunks(cols):
        out, o = [], 0
        while o < cols:
            wd = min(1024, cols - o)
            out.append((o, wd))
            o += wd
        return out

    def emit_sync(self):
        c, me = self.ctx, self.W["sync"]
        for n, (k, i) in enumerate(self.tiles):
            nm, cols, dst = self.specs[k]
            src = c.dram[nm].rearrange("(t p) d -> t p d", p=128)
            if n >= 1:
                me.wait(self.W["vector"], self.mk.get(f"vq{n-1}", 0))
                me.wait(self.W["scalar"], self.mk.get(f"sa{n-1}", 0))
            self.mk[f"ld{n}"] = me.dma_start(out=c.sb["wtile"][:, 0:cols],
                                             in_=src[i])
        me.wait(self.W["vector"], self.mk.get("copydone", 0))
        for dst, hbm in (("Wg", "WgqT_st"), ("Wu", "WuqT_st"), ("Wd", "WdqT_st")):
            me.dma_start(out=c.dram[hbm], in_=c.sb[dst][:])
        me.wait_self_all()

    def emit_scalar(self):
        c, me = self.ctx, self.W["scalar"]
        for n, (k, i) in enumerate(self.tiles):
            nm, cols, dst = self.specs[k]
            me.wait(self.W["sync"], self.mk.get(f"ld{n}", 0))
            for ci, (o0, wd) in enumerate(self.chunks(cols)):
                src = c.sb["wtile"][:, o0:o0 + wd]
                me.wait(self.W["vector"], self.mk.get(f"vq{n}_{ci-1}", 0))
                self.q.s_part(me, src, wd)
                self.mk[f"sa{n}_{ci}"] = me.mark()
            self.mk[f"sa{n}"] = me.mark()

    def emit_vector(self):
        c, me = self.ctx, self.W["vector"]
        for n, (k, i) in enumerate(self.tiles):
            nm, cols, dst = self.specs[k]
            me.wait(self.W["sync"], self.mk.get(f"ld{n}", 0))
            for ci, (o0, wd) in enumerate(self.chunks(cols)):
                src = c.sb["wtile"][:, o0:o0 + wd]
                self.q.v_part1(me, src, wd)
                me.wait(self.W["scalar"], self.mk.get(f"sa{n}_{ci}", 0))
                self.q.v_part2(me, c.sb["hqbf"][:, o0:o0 + wd], wd)
                self.mk[f"vq{n}_{ci}"] = me.mark()
            self.mk[f"vq{n}"] = me.mark()
            me.wait(self.W["tensor"], self.mk.get(f"tr{n}", 0))
            ncc = cols // 128
            dstv = c.sb[dst][:].rearrange("p (cchunk m) -> p cchunk m", cchunk=ncc)
            me.tensor_copy(dstv[:, :, i * 128:(i + 1) * 128],
                           c.pt_bf[:, 0:ncc * 128].rearrange(
                               "p (cc m) -> p cc m", cc=ncc))
            self.mk[f"cp{n}"] = me.mark()
        self.mk["copydone"] = me.mark()

    def emit_tensor(self):
        c, me = self.ctx, self.W["tensor"]
        for n, (k, i) in enumerate(self.tiles):
            nm, cols, dst = self.specs[k]
            me.wait(self.W["vector"], self.mk.get(f"vq{n}", 0))
            if n > 0:
                me.wait(self.W["vector"], self.mk.get(f"cp{n-1}", 0))
            for cc in range(cols // 128):
                me.transpose(c.pt_bf[:, cc * 128:(cc + 1) * 128],
                             c.sb["hqbf"][:, cc * 128:(cc + 1) * 128],
                             c.sb["identbf"][:])
            self.mk[f"tr{n}"] = me.mark()


class FFNPhase(Phase):
    def __init__(self, ctx, si, so, tag, load_stash=False):
        super().__init__(ctx, tag)
        self.si = si
        self.so = so
        self.load_stash = load_stash
        self.q = QuantHelper(ctx)

    def xt(self, i):
        return self.ctx.sb["xt"][:, (i % 2) * DM:(i % 2) * DM + DM]

    def xnew(self, i):
        return self.ctx.sb["xnew"][:, (i % 2) * DM:(i % 2) * DM + DM]

    def emit_sync(self):
        c, me = self.ctx, self.W["sync"]
        if self.load_stash:
            for dst, hbm in (("Wg", "WgqT_st"), ("Wu", "WuqT_st"), ("Wd", "WdqT_st")):
                self.mk["wl_" + dst] = me.dma_start(out=c.sb[dst][:],
                                                    in_=c.dram[hbm])
        xs = c.dram["x_state"].rearrange("(t p) d -> t p d", p=128)
        for i in range(2):
            self.mk[f"ld{i}"] = me.dma_start(out=self.xt(i), in_=xs[i])
        for i in range(NT):
            me.wait(self.W["vector"], self.mk.get(f"xn{i}", 0))
            self.mk[f"st{i}"] = me.dma_start(out=xs[i], in_=self.xnew(i))
            if i + 2 < NT:
                self.mk[f"ld{i+2}"] = me.dma_start(out=self.xt(i + 2),
                                                   in_=xs[i + 2])
        me.wait_self_all()

    def emit_scalar(self):
        c, me = self.ctx, self.W["scalar"]
        if self.load_stash:
            for dst in ("Wg", "Wu", "Wd"):
                me.wait(self.W["sync"], self.mk.get("wl_" + dst, 0))
        for i in range(NT):
            me.wait(self.W["vector"], self.mk.get(f"xc{i}", 0))
            self.q.s_part(me, c.sb["xc"][:], DM)
            self.mk[f"abs{i}"] = me.mark()
            for oc, (o0, ow) in enumerate(OCS):
                me.wait(self.W["tensor"], self.mk.get(f"mm{i}_{oc}", 0))
                hg = c.pgu[:, (oc % 2) * 1024:(oc % 2) * 1024 + ow]
                me.activation(c.sb["sgate"][:, 0:ow], hg, AF.Silu)
                self.mk[f"sil{i}_{oc}"] = me.mark()
                me.wait(self.W["vector"], self.mk.get(f"hp{i}_{oc}", 0))
                self.q.s_part(me, c.sb["hpf"][:, 0:ow], ow)
                self.mk[f"habs{i}_{oc}"] = me.mark()
            me.wait(self.W["vector"], self.mk.get(f"xn{i}", 0))
            me.activation(c.sb["sqscr"][:, 0:DM], self.xnew(i), AF.Square,
                          accum_out=c.sb["stats_sq" + self.so][:, i:i + 1])
            self.mk[f"sq{i}"] = me.mark()
        me.wait(self.W["vector"], self.mk.get("var", 0))
        me.activation(c.sb["tmpB"][:, 0:NT], c.sb["tmpA"][:, 0:NT], AF.Ln,
                      bias=EPS)
        me.activation(c.sb["stats_r" + self.so][:, 0:NT], c.sb["tmpB"][:, 0:NT],
                      AF.Exp, scale=-0.5)

    def emit_vector(self):
        c, me = self.ctx, self.W["vector"]
        if self.load_stash:
            for dst in ("Wg", "Wu", "Wd"):
                me.wait(self.W["sync"], self.mk.get("wl_" + dst, 0))
        for i in range(NT):
            me.wait(self.W["sync"], self.mk.get(f"ld{i}", 0))
            me.tensor_scalar(c.sb["xc"][:], self.xt(i),
                             c.sb["stats_m" + self.si][:, i:i + 1], None,
                             ALU.subtract)
            self.mk[f"xc{i}"] = me.mark()
            self.q.v_part1(me, c.sb["xc"][:], DM,
                           rstd_ap=c.sb["stats_r" + self.si][:, i:i + 1])
            me.wait(self.W["scalar"], self.mk.get(f"abs{i}", 0))
            self.q.v_part2(me, c.sb["xqbf"][:], DM)
            self.mk[f"xq{i}"] = me.mark()
            me.wait(self.W["tensor"], self.mk.get(f"tr{i}", 0))
            me.tensor_copy(c.sb["xqT"][:], c.pt_bf[:, 0:1024])
            self.mk[f"xqt{i}"] = me.mark()
            for oc, (o0, ow) in enumerate(OCS):
                me.wait(self.W["scalar"], self.mk.get(f"sil{i}_{oc}", 0))
                hu = c.pgu[:, (oc % 2) * 1024 + 512:(oc % 2) * 1024 + 512 + ow]
                me.tensor_tensor(c.sb["hpf"][:, 0:ow], c.sb["sgate"][:, 0:ow],
                                 hu, ALU.mult)
                self.mk[f"hp{i}_{oc}"] = me.mark()
                self.q.v_part1(me, c.sb["hpf"][:, 0:ow], ow)
                me.wait(self.W["scalar"], self.mk.get(f"habs{i}_{oc}", 0))
                self.q.v_part2(me, c.sb["hqbf"][:, o0:o0 + ow], ow)
                self.mk[f"hq{i}_{oc}"] = me.mark()
            for oc, (o0, ow) in enumerate(OCS):
                me.wait(self.W["tensor"], self.mk.get(f"tr2{i}_{oc}", 0))
                me.tensor_copy(c.sb["hqT"][:, o0:o0 + ow],
                               c.pt_bf[:, 1024:1024 + ow])
                self.mk[f"hqt{i}_{oc}"] = me.mark()
            me.wait(self.W["tensor"], self.mk.get(f"down{i}", 0))
            if i >= 2:
                me.wait(self.W["sync"], self.mk.get(f"st{i-2}", 0))
            me.op("_custom_dve", AFFINE_THEN_ADD, out=self.xnew(i),
                  in0=c.pgu[:, 0:1024], in1=self.xt(i), s0=0.5, s1=0.0)
            self.mk[f"xn{i}"] = me.mark()
            me.reduce_sum(c.sb["msc"][:, 0:1], self.xnew(i), AX.X)
            me.tensor_scalar(c.sb["stats_m" + self.so][:, i:i + 1],
                             c.sb["msc"][:, 0:1], 1.0 / DM, None, ALU.mult)
        me.wait(self.W["scalar"], self.mk.get(f"sq{NT-1}", 0))
        me.tensor_scalar(c.sb["tmpA"][:, 0:NT],
                         c.sb["stats_sq" + self.so][:, 0:NT],
                         1.0 / DM, None, ALU.mult)
        me.tensor_tensor(c.sb["tmpB"][:, 0:NT],
                         c.sb["stats_m" + self.so][:, 0:NT],
                         c.sb["stats_m" + self.so][:, 0:NT], ALU.mult)
        me.tensor_tensor(c.sb["tmpA"][:, 0:NT], c.sb["tmpA"][:, 0:NT],
                         c.sb["tmpB"][:, 0:NT], ALU.subtract)
        self.mk["var"] = me.mark()

    def emit_tensor(self):
        c, me = self.ctx, self.W["tensor"]
        if self.load_stash:
            for dst in ("Wg", "Wu", "Wd"):
                me.wait(self.W["sync"], self.mk.get("wl_" + dst, 0))
        for i in range(NT):
            me.wait(self.W["vector"], self.mk.get(f"xq{i}", 0))
            if i > 0:
                me.wait(self.W["vector"], self.mk.get(f"xqt{i-1}", 0))
            for cc in range(8):
                me.transpose(c.pt_bf[:, cc * 128:(cc + 1) * 128],
                             c.sb["xqbf"][:, cc * 128:(cc + 1) * 128],
                             c.sb["identbf"][:])
            self.mk[f"tr{i}"] = me.mark()
            me.wait(self.W["vector"], self.mk.get(f"xqt{i}", 0))
            for oc, (o0, ow) in enumerate(OCS):
                if oc < 2:
                    me.wait(self.W["vector"], self.mk.get(f"xn{i-1}", 0))
                    if i > 0:
                        me.wait(self.W["vector"], self.mk.get(f"hp{i-1}_5", 0))
                else:
                    me.wait(self.W["vector"], self.mk.get(f"hp{i}_{oc-2}", 0))
                pair = (oc % 2) * 1024
                hg = c.pgu[:, pair:pair + ow]
                hu = c.pgu[:, pair + 512:pair + 512 + ow]
                for cc in range(8):
                    me.matmul(hg, c.sb["xqT"][:, cc * 128:(cc + 1) * 128],
                              c.sb["Wg"][:, cc * DFF + o0:cc * DFF + o0 + ow],
                              start=(cc == 0), stop=(cc == 7))
                for cc in range(8):
                    me.matmul(hu, c.sb["xqT"][:, cc * 128:(cc + 1) * 128],
                              c.sb["Wu"][:, cc * DFF + o0:cc * DFF + o0 + ow],
                              start=(cc == 0), stop=(cc == 7))
                self.mk[f"mm{i}_{oc}"] = me.mark()
            for oc, (o0, ow) in enumerate(OCS):
                me.wait(self.W["vector"], self.mk.get(f"hq{i}_{oc}", 0))
                if not (i == 0 and oc == 0):
                    pi, po = (i, oc - 1) if oc > 0 else (i - 1, NOC - 1)
                    me.wait(self.W["vector"], self.mk.get(f"hqt{pi}_{po}", 0))
                for k in range(ow // 128):
                    me.transpose(c.pt_bf[:, 1024 + k * 128:1024 + (k + 1) * 128],
                                 c.sb["hqbf"][:, o0 + k * 128:o0 + (k + 1) * 128],
                                 c.sb["identbf"][:])
                self.mk[f"tr2{i}_{oc}"] = me.mark()
            me.wait(self.W["vector"], self.mk.get(f"hp{i}_4", 0))
            last_wait = -1
            for cc in range(22):
                oc_of = (cc * 128) // 512
                if oc_of != last_wait:
                    me.wait(self.W["vector"], self.mk.get(f"hqt{i}_{oc_of}", 0))
                    last_wait = oc_of
                me.matmul(c.pgu[:, 0:512], c.sb["hqT"][:, cc * 128:(cc + 1) * 128],
                          c.sb["Wd"][:, cc * DM:cc * DM + 512],
                          start=(cc == 0), stop=(cc == 21))
                me.matmul(c.pgu[:, 512:1024],
                          c.sb["hqT"][:, cc * 128:(cc + 1) * 128],
                          c.sb["Wd"][:, cc * DM + 512:cc * DM + 1024],
                          start=(cc == 0), stop=(cc == 21))
            self.mk[f"down{i}"] = me.mark()




NH = 16
HD = 64


class AttnPhase(Phase):

    def __init__(self, ctx, si, so, tag):
        super().__init__(ctx, tag)
        self.si = si
        self.so = so

    def xt(self, i):
        return self.ctx.sb["xt"][:, (i % 2) * DM:(i % 2) * DM + DM]

    def xnew(self, i):
        return self.ctx.sb["xnew"][:, (i % 2) * DM:(i % 2) * DM + DM]

    def emit_gpsimd(self):
        c, me = self.ctx, self.W["gpsimd"]
        for k, nm in enumerate(("WqT", "WkT", "WvT", "WfcT")):
            self.mk["w_" + nm] = me.dma_start(
                out=c.sb["Wat"][:, k * 8192:(k + 1) * 8192].rearrange(
                    "p (a b) -> p a b", b=1024),
                in_=c.dram[nm].rearrange("(a p) b -> p a b", p=128))

    def emit_sync(self, E=None):
        c, me = self.ctx, self.W["sync"]
        xs = c.dram["x_state"].rearrange("(t p) d -> t p d", p=128)
        for i in range(2):
            self.mk[f"ld{i}"] = me.dma_start(out=self.xt(i), in_=xs[i])
        for i in range(NT):
            me.wait(self.W["vector"], self.mk.get(f"xn{i}", 0))
            self.mk[f"st{i}"] = me.dma_start(out=xs[i], in_=self.xnew(i))
            if i + 2 < NT:
                self.mk[f"ld{i+2}"] = me.dma_start(out=self.xt(i + 2), in_=xs[i + 2])
        me.wait_self_all()

    def _s_tile(self, me, i):
        c = self.ctx
        if i < NT:
            me.wait(self.W["tensor"], self.mk.get(f"tr{i}", 0))
            me.wait(self.W["tensor"], self.mk.get(f"qkv{i-1}", 0))
            me.activation(c.sb["xnT"][:], c.pt_bf[:, 2048:3072], AF.Copy)
            self.mk[f"xnTc{i}"] = me.mark()
            me.wait(self.W["tensor"], self.mk.get(f"qkv{i}", 0))
            me.wait(self.W["vector"], self.mk.get(f"smul{i-1}", 0))
            me.wait(self.W["vector"], self.mk.get(f"ao{i-2}", 0))
            me.activation(c.sb["qbf"][:], c.pgu[:, 0:1024], AF.Copy)
            me.activation(c.sb["kbf"][:], c.pgu[:, 1024:2048], AF.Copy)
            me.activation(c.sb["vbf2"][:, i % 2, :], c.ptf[:, 0:1024], AF.Copy)
            self.mk[f"ev{i}"] = me.mark()
        if i < NT:
            me.wait(self.W["vector"], self.mk.get(f"shift{i}", 0))
            me.activation(c.sb["attE"][:], c.sb["asc2"][:, i % 2, :], AF.Exp)
            self.mk[f"exp{i}"] = me.mark()
        j = i - 1
        if 0 <= j < NT:
            me.wait(self.W["tensor"], self.mk.get(f"atr{j}", 0))
            me.wait(self.W["tensor"], self.mk.get(f"fch1_{j-1}", 0))
            me.activation(c.sb["aT"][:], c.pt_bf[:, 3072:4096], AF.Copy)
            self.mk[f"aTc{j}"] = me.mark()
            me.wait(self.W["tensor"], self.mk.get(f"fch0_{j}", 0))
            me.activation(c.sb["fcb"][:, j % 2, 0:512], c.ptf[:, 1536:2048],
                          AF.Copy)
            self.mk[f"fce0_{j}"] = me.mark()
            me.wait(self.W["tensor"], self.mk.get(f"fch1_{j}", 0))
            me.activation(c.sb["fcb"][:, j % 2, 512:1024], c.ptf[:, 1536:2048],
                          AF.Copy)
            self.mk[f"fce1_{j}"] = me.mark()
        jj = i - 2
        if 0 <= jj < NT:
            me.wait(self.W["vector"], self.mk.get(f"xn{jj}", 0))
            me.activation(c.sb["sqscr"][:, 0:DM], self.xnew(jj), AF.Square,
                          accum_out=c.sb["stats_sq" + self.so][:, jj:jj + 1])
            self.mk[f"sq{jj}"] = me.mark()

    def emit_scalar(self, E=None):
        c, me = self.ctx, self.W["scalar"]
        for i in range(NT + 2):
            self._s_tile(me, i)
        me.wait(self.W["vector"], self.mk.get("var", 0))
        me.activation(c.sb["tmpB"][:, 0:NT], c.sb["tmpA"][:, 0:NT], AF.Ln,
                      bias=EPS)
        me.activation(c.sb["stats_r" + self.so][:, 0:NT], c.sb["tmpB"][:, 0:NT],
                      AF.Exp, scale=-0.5)

    def _v_tile(self, me, i):
        c = self.ctx
        if i < NT:
            me.wait(self.W["sync"], self.mk.get(f"ld{i}", 0))
            me.tensor_scalar(c.sb["xnbf"][:], self.xt(i),
                             c.sb["stats_m" + self.si][:, i:i + 1],
                             c.sb["stats_r" + self.si][:, i:i + 1],
                             ALU.subtract, ALU.mult)
            self.mk[f"xn_r{i}"] = me.mark()
            me.wait(self.W["scalar"], self.mk.get(f"ev{i}", 0))
            me.wait(self.W["gpsimd"], self.mk.get(f"srd{i-2}", 0))
            q4 = c.sb["qbf"][:].rearrange("p (h d) -> p h d", d=HD)
            k4 = c.sb["kbf"][:].rearrange("p (t d) -> p t d", d=HD)
            sc3i = c.sb["asc2"][:, i % 2, :].rearrange("p (h t) -> p h t",
                                                       t=NH)
            qv = q4.unsqueeze(2).broadcast_to([128, NH, NH, HD])
            kv = k4.unsqueeze(1).broadcast_to([128, NH, NH, HD])
            m4 = c.sb["m4s"][:, 0, :].rearrange(
                "p (h t d) -> p h t d", t=NH, d=HD)
            me.tensor_tensor(m4, qv, kv, ALU.mult)
            self.mk[f"smul{i}"] = me.mark()
            me.tensor_reduce(sc3i, m4, AX.X, ALU.add)
        j = i
        if 0 <= j < NT:
            sc3 = c.sb["asc2"][:, j % 2, :].rearrange("p (h t) -> p h t", t=NH)
            me.tensor_reduce(c.sb["att_mx"][:, 0:NH], sc3, AX.X, ALU.max)
            mxb = c.sb["att_mx"][:, 0:NH].unsqueeze(2).broadcast_to(
                [128, NH, NH])
            me.tensor_tensor(sc3, sc3, mxb, ALU.subtract)
            me.tensor_scalar(c.sb["asc2"][:, j % 2, :],
                             c.sb["asc2"][:, j % 2, :], 0.125, None, ALU.mult)
            self.mk[f"shift{j}"] = me.mark()
            me.wait(self.W["scalar"], self.mk.get(f"exp{j}", 0))
            E3 = c.sb["attE"][:].rearrange("p (h t) -> p h t", t=NH)
            me.tensor_reduce(c.sb["att_sm"][:, 0:NH], E3, AX.X, ALU.add)
            me.reciprocal(c.sb["att_rc"][:, 0:NH], c.sb["att_sm"][:, 0:NH])
            rcb = c.sb["att_rc"][:, 0:NH].unsqueeze(2).broadcast_to(
                [128, NH, NH])
            me.tensor_tensor(c.sb["att_A"][:, 0:256].rearrange(
                "p (h t) -> p h t", t=NH), E3, rcb, ALU.mult)
            A3 = c.sb["att_A"][:, 0:256].rearrange("p (h t) -> p h t", t=NH)
            v3 = c.sb["vbf2"][:, j % 2, :].rearrange("p (t d) -> p t d", d=HD)
            vT3 = c.sb["att_m4"][:, 0:1024].rearrange("p (d t) -> p d t", t=NH)
            me.tensor_copy(vT3, v3.transpose([0, 2, 1]))
            ao3 = c.sb["aof"][:].rearrange("p (h d) -> p h d", d=HD)
            av = A3.unsqueeze(2).broadcast_to([128, NH, HD, NH])
            vv = vT3.unsqueeze(1).broadcast_to([128, NH, HD, NH])
            m4b = c.sb["m4s"][:, 1, :].rearrange(
                "p (h d t) -> p h d t", d=HD, t=NH)
            me.tensor_tensor(m4b, av, vv, ALU.mult)
            me.tensor_reduce(ao3, m4b, AX.X, ALU.add)
            me.wait(self.W["tensor"], self.mk.get(f"atr{j-1}", 0))
            me.tensor_copy(c.sb["abf"][:], c.sb["aof"][:])
            self.mk[f"ao{j}"] = me.mark()
        jj = i - 1
        if 0 <= jj < NT:
            me.wait(self.W["scalar"], self.mk.get(f"fce1_{jj}", 0))
            if jj >= 2:
                me.wait(self.W["sync"], self.mk.get(f"st{jj-2}", 0))
                me.wait(self.W["scalar"], self.mk.get(f"sq{jj-2}", 0))
            me.op("_custom_dve", AFFINE_THEN_ADD, out=self.xnew(jj),
                  in0=c.sb["fcb"][:, jj % 2, :], in1=self.xt(jj), s0=1.0,
                  s1=0.0)
            self.mk[f"xn{jj}"] = me.mark()
            me.reduce_sum(c.sb["msc"][:, 0:1], self.xnew(jj), AX.X)
            me.tensor_scalar(c.sb["stats_m" + self.so][:, jj:jj + 1],
                             c.sb["msc"][:, 0:1], 1.0 / DM, None, ALU.mult)

    def emit_vector(self, E=None):
        c, me = self.ctx, self.W["vector"]
        for i in range(NT + 1):
            self._v_tile(me, i)
        me.wait(self.W["scalar"], self.mk.get(f"sq{NT-1}", 0))
        me.tensor_scalar(c.sb["tmpA"][:, 0:NT],
                         c.sb["stats_sq" + self.so][:, 0:NT],
                         1.0 / DM, None, ALU.mult)
        me.tensor_tensor(c.sb["tmpB"][:, 0:NT], c.sb["stats_m" + self.so][:, 0:NT],
                         c.sb["stats_m" + self.so][:, 0:NT], ALU.mult)
        me.tensor_tensor(c.sb["tmpA"][:, 0:NT], c.sb["tmpA"][:, 0:NT],
                         c.sb["tmpB"][:, 0:NT], ALU.subtract)
        self.mk["var"] = me.mark()

    def _t_tile(self, me, i):
        c = self.ctx
        if i < NT:
            me.wait(self.W["vector"], self.mk.get(f"xn_r{i}", 0))
            me.wait(self.W["scalar"], self.mk.get(f"xnTc{i-1}", 0))
            for cc in range(8):
                me.transpose(c.pt_bf[:, 2048 + cc * 128:2048 + (cc + 1) * 128],
                             c.sb["xnbf"][:, cc * 128:(cc + 1) * 128],
                             c.sb["identbf"][:])
            self.mk[f"tr{i}"] = me.mark()
            me.wait(self.W["scalar"], self.mk.get(f"xnTc{i}", 0))
            me.wait(self.W["scalar"], self.mk.get(f"ev{i-1}", 0))
            for w, out in ((0, c.pgu[:, 0:1024]), (1, c.pgu[:, 1024:2048]),
                           (2, c.ptf[:, 0:1024])):
                for half in range(2):
                    for cc in range(8):
                        me.matmul(out[:, half * 512:half * 512 + 512],
                                  c.sb["xnT"][:, cc * 128:(cc + 1) * 128],
                                  c.sb["Wat"][:, w * 8192 + cc * 1024 + half * 512:
                                              w * 8192 + cc * 1024 + half * 512 + 512],
                                  start=(cc == 0), stop=(cc == 7))
            self.mk[f"qkv{i}"] = me.mark()
        j = i - 1
        if 0 <= j < NT:
            me.wait(self.W["vector"], self.mk.get(f"ao{j}", 0))
            me.wait(self.W["scalar"], self.mk.get(f"fce1_{j-1}", 0))
            for cc in range(8):
                me.transpose(c.pt_bf[:, 3072 + cc * 128:3072 + (cc + 1) * 128],
                             c.sb["abf"][:, cc * 128:(cc + 1) * 128],
                             c.sb["identbf"][:])
            self.mk[f"atr{j}"] = me.mark()
            me.wait(self.W["scalar"], self.mk.get(f"aTc{j}", 0))
            for half in range(2):
                for cc in range(8):
                    me.matmul(c.ptf[:, 1536:2048],
                              c.sb["aT"][:, cc * 128:(cc + 1) * 128],
                              c.sb["Wat"][:, 3 * 8192 + cc * 1024 + half * 512:
                                          3 * 8192 + cc * 1024 + half * 512 + 512],
                              start=(cc == 0), stop=(cc == 7))
                self.mk[f"fch{half}_{j}"] = me.mark()
                if half == 0:
                    me.wait(self.W["scalar"], self.mk.get(f"fce0_{j}", 0))

    def emit_tensor(self, E=None):
        c, me = self.ctx, self.W["tensor"]
        for k in ("WqT", "WkT", "WvT", "WfcT"):
            me.wait(self.W["gpsimd"], self.mk.get("w_" + k, 0))
        for i in range(NT + 1):
            self._t_tile(me, i)


NEXP = 64
MOE_SUPERS = [list(range(s, min(s + 6, NT))) for s in range(0, NT, 6)]


class MoEWeightPrep(Phase):

    def __init__(self, ctx):
        super().__init__(ctx, "wm")
        self.chunks = ([("keysR", "keys_st", s) for s in range(8)] +
                       [("valuesR", "values_st", s) for s in range(8)])

    def emit_sync(self):
        c, me = self.ctx, self.W["sync"]
        for ci, (src, dst, s) in enumerate(self.chunks):
            me.wait(self.W["vector"], self.mk.get(("ca", ci - 2), 0))
            self.mk[("rd", ci)] = me.dma_start(
                out=c.sb["wmA"][:, ci % 2, :],
                in_=c.dram[src][:, s * 8192:(s + 1) * 8192])
        me.wait_self_all()

    def emit_vector(self):
        c, me = self.ctx, self.W["vector"]
        for ci in range(len(self.chunks)):
            me.wait(self.W["sync"], self.mk.get(("rd", ci), 0))
            me.wait(self.W["gpsimd"], self.mk.get(("wr", ci - 2), 0))
            me.tensor_copy(c.sb["wmB"][:, ci % 2, :], c.sb["wmA"][:, ci % 2, :])
            self.mk[("ca", ci)] = me.mark()

    def emit_gpsimd(self):
        c, me = self.ctx, self.W["gpsimd"]
        for ci, (src, dst, s) in enumerate(self.chunks):
            me.wait(self.W["vector"], self.mk.get(("ca", ci), 0))
            self.mk[("wr", ci)] = me.dma_start(
                out=c.dram[dst][:, s * 8192:(s + 1) * 8192],
                in_=c.sb["wmB"][:, ci % 2, :])
        me.wait_self_all()


class MoEPhase(Phase):

    def __init__(self, ctx, si, so, tag):
        super().__init__(ctx, tag)
        self.si = si
        self.so = so
        sup = MOE_SUPERS
        self.sup = sup
        self.chk = []
        self.base = []
        for sx, Ts in enumerate(sup):
            self.base.append(len(self.chk))
            for g4 in range(16):
                for ti, t in enumerate(Ts):
                    self.chk.append((sx, g4, ti, t))
        self.TOT = len(self.chk)
        self.sqpos = {}
        for sx in range(len(sup) - 1):
            Xe = self.base[sx] + 16 * len(sup[sx])
            for ti, t in enumerate(sup[sx]):
                self.sqpos.setdefault(Xe - len(sup[sx]) + ti + 5, []).append(t)
        self.ptile = {}
        for sx, Ts in enumerate(sup):
            for tipos, t in enumerate(Ts):
                step = 0 if sx == 0 else self.base[sx] - len(sup[sx - 1]) + tipos
                self.ptile.setdefault(step, []).append((sx, tipos, t))
        self.items = []
        self.prep_items = {}
        self.sc_item = {}
        self.xn_item2 = {}
        self.xn_prev_h = {}
        for sx, Ts in enumerate(sup):
            if sx == 0:
                continue
            pTs = sup[sx - 1]
            for tipos, t in enumerate(Ts):
                if tipos < len(pTs):
                    self.xn_prev_h[t] = self.base[sx - 1] + 15 * len(pTs) + tipos
        for X in range(self.TOT + 4):
            if 0 <= X - 2 < self.TOT:
                self.sc_item[X - 2] = len(self.items)
                self.items.append(("sc", X - 2))
            if X in self.ptile:
                ids = []
                for sx, tipos, t in self.ptile[X]:
                    for half in range(2):
                        ids.append(len(self.items))
                        self.items.append(("xn", t, tipos, half))
                        if half == 1:
                            self.xn_item2[t] = ids[-1]
                    ids.append(len(self.items))
                    self.items.append(("lg", t, tipos))
                self.prep_items[X] = ids

    def xt2(self, i):
        return self.ctx.sb["xt"][:, (i % 2) * DM:(i % 2) * DM + DM]

    def xnew(self, i):
        return self.ctx.sb["xnew"][:, (i % 2) * DM:(i % 2) * DM + DM]


    def _t_item(self, me, iid):
        c = self.ctx
        it = self.items[iid]
        me.wait(self.W["scalar"], self.mk.get(("cpT", iid - 2), 0))
        if it[0] == "lg":
            _, t, tipos = it
            me.wait(self.W["gpsimd"], self.mk.get("selT", 0))
            me.wait(self.W["scalar"], self.mk.get(("xtc", t), 0))
            lb = 1024 + (iid % 2) * 512 + 256
            for cc in range(8):
                me.matmul(c.ptf[:, lb:lb + 64],
                          c.sb["xnTs"][:, cc, tipos * 128:(tipos + 1) * 128],
                          c.sb["selTb"][:, cc, :],
                          start=(cc == 0), stop=(cc == 7))
            self.mk[("lgi", t)] = me.mark()
            return
        tb = 2048 + (iid % 2) * 1024
        if it[0] == "sc":
            X = it[1]
            me.wait(self.W["vector"], self.mk.get(("gat", X), 0))
            for k in range(4):
                me.transpose(c.pt_bf[:, tb + k * 128:tb + (k + 1) * 128],
                             c.sb["scb"][:, X % 2, k * 128:(k + 1) * 128],
                             c.sb["identbf"][:])
        else:
            _, t, tipos, half = it
            me.wait(self.W["vector"], self.mk.get(("xn", t), 0))
            for k in range(4):
                cc = half * 4 + k
                me.transpose(c.pt_bf[:, tb + k * 128:tb + (k + 1) * 128],
                             c.sb["m_xnbf"][:, cc * 128:(cc + 1) * 128],
                             c.sb["identbf"][:])
        self.mk[("tri", iid)] = me.mark()

    def _t_h(self, me, X):
        c = self.ctx
        sx, g4, ti, t = self.chk[X]
        if X == 0:
            for s8 in range(6):
                me.wait(self.W["sync"], self.mk.get(("k", s8), 0))
            for s8 in range(6, 8):
                me.wait(self.W["gpsimd"], self.mk.get(("k", s8), 0))
        me.wait(self.W["scalar"], self.mk.get(("xtc", t), 0))
        me.wait(self.W["scalar"], self.mk.get(("hev", X - 2), 0))
        b = (X % 2) * 512
        for cc in range(8):
            me.matmul(c.ptf[:, b:b + 512],
                      c.sb["xnTs"][:, cc, ti * 128:(ti + 1) * 128],
                      c.sb["keysb"][:, cc, g4 * 512:(g4 + 1) * 512],
                      start=(cc == 0), stop=(cc == 7))
        self.mk[("h", X)] = me.mark()

    def _t_vmm(self, me, X):
        c = self.ctx
        sx, g4, ti, t = self.chk[X]
        g4g = sx * 16 + g4
        if ti == 0:
            me.wait(self.W["sync"], self.mk.get(("vd", 2 * g4g), 0))
            me.wait(self.W["gpsimd"], self.mk.get(("vd", 2 * g4g + 1), 0))
        me.wait(self.W["scalar"], self.mk.get(("cpT", self.sc_item[X]), 0))
        me.wait(self.W["vector"], self.mk.get(("ev", X - 2), 0))
        pb = (X % 2) * 1024
        for half in range(2):
            for e4 in range(4):
                pglob = 2 * g4g + e4 // 2
                vs = (e4 % 2) * 1024 + half * 512
                me.matmul(c.pgu[:, pb + half * 512:pb + half * 512 + 512],
                          c.sb["scT"][:, X % 2, e4 * 128:(e4 + 1) * 128],
                          c.sb["vbuf"][:, pglob % 4, vs:vs + 512],
                          start=(e4 == 0), stop=(e4 == 3))
        self.mk[("vmm", X)] = me.mark()
        if ti == len(self.sup[sx]) - 1:
            self.mk[("vmG", g4g)] = me.mark()

    def emit_tensor(self, E=None):
        me = self.W["tensor"]
        for X in range(self.TOT + 4):
            if X == 0:
                for iid in self.prep_items.get(0, []):
                    self._t_item(me, iid)
            if X < self.TOT:
                self._t_h(me, X)
            if 0 <= X - 2 < self.TOT:
                self._t_item(me, self.sc_item[X - 2])
            if 0 <= X - 4 < self.TOT:
                self._t_vmm(me, X - 4)
            if X > 0:
                for iid in self.prep_items.get(X, []):
                    self._t_item(me, iid)


    def _s_item(self, me, iid):
        c = self.ctx
        it = self.items[iid]
        if it[0] == "lg":
            _, t, tipos = it
            me.wait(self.W["tensor"], self.mk.get(("lgi", t), 0))
            me.wait(self.W["vector"], self.mk.get(("gt", t - 2), 0))
            lb = 1024 + (iid % 2) * 512 + 256
            me.activation(c.sb["selsb"][:, t % 2, :], c.ptf[:, lb:lb + 64],
                          AF.Sigmoid)
            self.mk[("sig", t)] = me.mark()
            self.mk[("cpT", iid)] = me.mark()
            return
        me.wait(self.W["tensor"], self.mk.get(("tri", iid), 0))
        tb = 2048 + (iid % 2) * 1024
        src = c.pt_bf[:, tb:tb + 512]
        if it[0] == "sc":
            X = it[1]
            me.wait(self.W["tensor"], self.mk.get(("vmm", X - 2), 0))
            me.activation(c.sb["scT"][:, X % 2, :], src, AF.Relu)
        else:
            _, t, tipos, half = it
            ph = self.xn_prev_h.get(t)
            if ph is not None:
                me.wait(self.W["tensor"], self.mk.get(("h", ph), 0))
            me.activation(
                c.sb["xnTs"][:, half * 4:half * 4 + 4,
                             tipos * 128:(tipos + 1) * 128],
                src.rearrange("p (a b) -> p a b", b=128), AF.Copy)
            if half == 1:
                self.mk[("xtc", t)] = me.mark()
        self.mk[("cpT", iid)] = me.mark()

    def emit_scalar(self, E=None):
        c, me = self.ctx, self.W["scalar"]
        for X in range(self.TOT + 4):
            for t in self.sqpos.get(X, []):
                me.wait(self.W["vector"], self.mk.get(("af", t), 0))
                me.activation(c.sb["sqscr"][:, 0:DM], self.xnew(t), AF.Square,
                              accum_out=c.sb["stats_sq" + self.so][:, t:t + 1])
                self.mk[("sq", t)] = me.mark()
            if X == 0:
                for iid in self.prep_items.get(0, []):
                    self._s_item(me, iid)
            if X < self.TOT:
                me.wait(self.W["tensor"], self.mk.get(("h", X), 0))
                me.wait(self.W["tensor"],
                        self.mk.get(("tri", self.sc_item.get(X - 2, -9)), 0))
                me.activation(c.sb["scb"][:, X % 2, :],
                              c.ptf[:, (X % 2) * 512:(X % 2) * 512 + 512],
                              AF.Relu)
                self.mk[("hev", X)] = me.mark()
            if 0 <= X - 2 < self.TOT:
                self._s_item(me, self.sc_item[X - 2])
            if X > 0:
                for iid in self.prep_items.get(X, []):
                    self._s_item(me, iid)
        for t in self.sup[-1]:
            me.wait(self.W["vector"], self.mk.get(("af", t), 0))
            me.activation(c.sb["sqscr"][:, 0:DM], self.xnew(t), AF.Square,
                          accum_out=c.sb["stats_sq" + self.so][:, t:t + 1])
            self.mk[("sq", t)] = me.mark()
        me.wait(self.W["vector"], self.mk.get("var", 0))
        me.activation(c.sb["tmpB"][:, 0:NT], c.sb["tmpA"][:, 0:NT], AF.Ln,
                      bias=EPS)
        me.activation(c.sb["stats_r" + self.so][:, 0:NT], c.sb["tmpB"][:, 0:NT],
                      AF.Exp, scale=-0.5)


    def _v_gating(self, me, sx, tipos, t):
        c = self.ctx
        me.wait(self.W["scalar"], self.mk.get(("sig", t), 0))
        sl = c.sb["selsb"][:, t % 2, :]
        me.tensor_copy(c.sb["selw"][:], sl)
        for r in range(4):
            me.reduce_max(c.sb["msc"][:, 1:2], c.sb["selw"][:], AX.X)
            if r < 3:
                me.tensor_scalar(c.sb["selm"][:], c.sb["selw"][:],
                                 c.sb["msc"][:, 1:2], -1e30, ALU.is_ge,
                                 ALU.mult)
                me.tensor_tensor(c.sb["selw"][:], c.sb["selw"][:],
                                 c.sb["selm"][:], ALU.add)
        me.tensor_scalar(c.sb["selm"][:], sl, c.sb["msc"][:, 1:2], None,
                         ALU.is_ge)
        me.tensor_tensor(
            c.sb["gates_bf"][:, sx % 2, tipos * 64:tipos * 64 + 64],
            c.sb["selm"][:], sl, ALU.mult)
        self.mk[("gt", t)] = me.mark()

    def emit_vector(self, E=None):
        c, me = self.ctx, self.W["vector"]
        sup, chk = self.sup, self.chk
        def vprep(X):
            for sx, tipos, t in self.ptile.get(X, []):
                me.wait(self.W["sync"], self.mk.get(("ld", t), 0))
                me.wait(self.W["tensor"],
                        self.mk.get(("tri", self.xn_item2.get(t - 1, -9)), 0))
                me.tensor_scalar(c.sb["m_xnbf"][:],
                                 c.sb["m_xt"][:, t % 2, :],
                                 c.sb["stats_m" + self.si][:, t:t + 1],
                                 c.sb["stats_r" + self.si][:, t:t + 1],
                                 ALU.subtract, ALU.mult)
                self.mk[("xn", t)] = me.mark()
                self._v_gating(me, sx, tipos, t)

        for X in range(self.TOT + 4):
            if X == 0:
                vprep(0)
            if X < self.TOT:
                sx, g4, ti, t = chk[X]
                me.wait(self.W["scalar"], self.mk.get(("hev", X), 0))
                g = c.sb["gates_bf"][:, sx % 2,
                                     ti * 64 + 4 * g4:ti * 64 + 4 * g4 + 4]
                sc3 = c.sb["scb"][:, X % 2, :].rearrange("p (a b) -> p a b",
                                                         b=128)
                me.tensor_tensor(sc3, sc3,
                                 g.unsqueeze(2).broadcast_to([128, 4, 128]),
                                 ALU.mult)
                self.mk[("gat", X)] = me.mark()
            if X > 0:
                vprep(X)
            if 0 <= X - 4 < self.TOT:
                Xe = X - 4
                sx, g4, ti, t = chk[Xe]
                me.wait(self.W["tensor"], self.mk.get(("vmm", Xe), 0))
                oa = c.sb["outac"][:, ti, :]
                pg = c.pgu[:, (Xe % 2) * 1024:(Xe % 2) * 1024 + 1024]
                if g4 == 0:
                    me.tensor_copy(oa, pg)
                else:
                    me.tensor_tensor(oa, oa, pg, ALU.add)
                if g4 == 15:
                    me.wait(self.W["gpsimd"], self.mk.get(("ld2", t), 0))
                    me.wait(self.W["gpsimd"], self.mk.get(("st", t - 2), 0))
                    me.wait(self.W["scalar"], self.mk.get(("sq", t - 2), 0))
                    me.op("_custom_dve", AFFINE_THEN_ADD, out=self.xnew(t),
                          in0=oa, in1=self.xt2(t),
                          s0=0.5, s1=0.0)
                    self.mk[("af", t)] = me.mark()
                    me.reduce_sum(c.sb["msc"][:, 0:1], self.xnew(t), AX.X)
                    me.tensor_scalar(c.sb["stats_m" + self.so][:, t:t + 1],
                                     c.sb["msc"][:, 0:1], 1.0 / DM, None,
                                     ALU.mult)
                self.mk[("ev", Xe)] = me.mark()
        me.wait(self.W["scalar"], self.mk.get(("sq", NT - 1), 0))
        me.tensor_scalar(c.sb["tmpA"][:, 0:NT],
                         c.sb["stats_sq" + self.so][:, 0:NT],
                         1.0 / DM, None, ALU.mult)
        me.tensor_tensor(c.sb["tmpB"][:, 0:NT],
                         c.sb["stats_m" + self.so][:, 0:NT],
                         c.sb["stats_m" + self.so][:, 0:NT], ALU.mult)
        me.tensor_tensor(c.sb["tmpA"][:, 0:NT], c.sb["tmpA"][:, 0:NT],
                         c.sb["tmpB"][:, 0:NT], ALU.subtract)
        self.mk["var"] = me.mark()


    def emit_sync(self, E=None):
        c, me = self.ctx, self.W["sync"]
        sup, chk = self.sup, self.chk
        xs = c.dram["x_state"].rearrange("(t p) d -> t p d", p=128)
        ldpos = {}
        for step, entries in self.ptile.items():
            for sx, tipos, t in entries:
                ldpos.setdefault(max(0, step - 2), []).append(t)
        for X in range(self.TOT + 4):
            if X == 0:
                for s8 in range(6):
                    self.mk[("k", s8)] = me.dma_start(
                        out=c.sb["keysb"][:, s8, :],
                        in_=c.dram["keys_st"][:, s8 * 8192:(s8 + 1) * 8192])
            if X in ldpos:
                for t in ldpos[X]:
                    me.wait(self.W["vector"], self.mk.get(("xn", t - 2), 0))
                    self.mk[("ld", t)] = me.dma_start(
                        out=c.sb["m_xt"][:, t % 2, :], in_=xs[t])
            if X < self.TOT:
                sx, g4, ti, t = chk[X]
                if ti == 0:
                    g4g = sx * 16 + g4
                    pglob = 2 * g4g
                    me.wait(self.W["tensor"],
                            self.mk.get(("vmG", (pglob - 4) // 2), 0))
                    e0 = 4 * g4
                    self.mk[("vd", pglob)] = me.dma_start(
                        out=c.sb["vbuf"][:, pglob % 4, :],
                        in_=c.dram["values_st"][:, e0 * 1024:(e0 + 2) * 1024])
        me.wait_self_all()

    def emit_gpsimd(self, E=None):
        c, me = self.ctx, self.W["gpsimd"]
        sup, chk = self.sup, self.chk
        xs = c.dram["x_state"].rearrange("(t p) d -> t p d", p=128)
        self.mk["selT"] = me.dma_start(
            out=c.sb["selTb"][:],
            in_=c.dram["selT"].rearrange("(a p) b -> p a b", p=128))
        for s8 in range(6, 8):
            self.mk[("k", s8)] = me.dma_start(
                out=c.sb["keysb"][:, s8, :],
                in_=c.dram["keys_st"][:, s8 * 8192:(s8 + 1) * 8192])
        stpos = {}
        for sx in range(len(sup)):
            Xe = self.base[sx] + 16 * len(sup[sx])
            stpos[Xe - len(sup[sx])] = sx
        for X in range(self.TOT + 4):
            if X < self.TOT:
                sx, g4, ti, t = chk[X]
                if ti == 0:
                    g4g = sx * 16 + g4
                    pglob = 2 * g4g + 1
                    me.wait(self.W["tensor"],
                            self.mk.get(("vmG", (pglob - 4) // 2), 0))
                    e0 = 4 * g4 + 2
                    self.mk[("vd", pglob)] = me.dma_start(
                        out=c.sb["vbuf"][:, pglob % 4, :],
                        in_=c.dram["values_st"][:, e0 * 1024:(e0 + 2) * 1024])
            if X in stpos:
                Ts = sup[stpos[X]]
                seq = []
                for k2, t in enumerate(Ts):
                    seq.append(("ld2", t))
                    if k2 >= 2:
                        seq.append(("st", Ts[k2 - 2]))
                seq.append(("st", Ts[-2]))
                seq.append(("st", Ts[-1]))
                for kind, t in seq:
                    if kind == "ld2":
                        me.wait(self.W["vector"], self.mk.get(("af", t - 2), 0))
                        self.mk[("ld2", t)] = me.dma_start(
                            out=self.xt2(t), in_=xs[t])
                    else:
                        me.wait(self.W["vector"], self.mk.get(("af", t), 0))
                        self.mk[("st", t)] = me.dma_start(
                            out=xs[t], in_=self.xnew(t))
        me.wait_self_all()




MSTAGES = ("E0", "E1", "E2", "E2X", "E2SCAN", "E2OUT")
NSEG = 8
SEGT = 512
SEGTILES = SEGT // 128


class MambaPhase(Phase):

    def __init__(self, ctx, si, so, tag):
        super().__init__(ctx, tag)
        self.si = si
        self.so = so

    def xt(self, i):
        return self.ctx.sb["e_xt"][:, i % 4, :]

    def xnew(self, i):
        return self.ctx.sb["xnew"][:, (i % 2) * DM:(i % 2) * DM + DM]

    @staticmethod
    def spred(sg, n, dc, lag):
        k = (sg * 16 + n) * 8 + dc - lag
        if k < 0:
            return None
        return (k // 128, (k // 8) % 16, k % 8)

    def emit_gpsimd(self):
        c, me = self.ctx, self.W["gpsimd"]
        self.mk["win"] = me.dma_start(
            out=c.sb["Win_b"][:].rearrange("p a b -> p (a b)"),
            in_=c.dram["WinR"])
        self.mk["wout"] = me.dma_start(
            out=c.sb["Wout_b"][:].rearrange("p a b -> p (a b)"),
            in_=c.dram["WoutR"])
        self.mk["wxp"] = me.dma_start(
            out=c.sb["Wxp_b"][:].rearrange("p a b -> p (a b)"),
            in_=c.dram["WxpR"])
        self.mk["wdt"] = me.dma_start(
            out=c.sb["Wdt_b"][0:64, :, :].rearrange("p a b -> p (a b)"),
            in_=c.dram["WdtR"])
        xr = c.dram["xraw_st"].rearrange("p (a b) -> p a b", b=S)
        for sg in range(NSEG if "E0" in MSTAGES else 0):
            for dc in range(16):
                me.wait(self.W["vector"], self.mk.get(("xrev", sg, dc), 0))
                self.mk[("xrw", sg, dc)] = me.dma_start(
                    out=xr[:, dc, sg * SEGT:(sg + 1) * SEGT],
                    in_=c.sb["xrstage"][:, (dc % 2) * SEGT:(dc % 2) * SEGT + SEGT])
        self.mk["e0_dn"] = me.dn
        for dcz in range(16 if "E1" in MSTAGES else 0):
            me.wait(self.W["vector"], self.mk.get(("cvz", dcz), 0) if dcz > 0
                    else self.mk.get("cbz", 0))
            self.mk[("cvl", dcz)] = me.dma_start(
                out=c.sb["cbuf"][:, 1:S + 1], in_=xr[:, dcz, :])
            me.wait(self.W["scalar"], self.mk.get(("gel", dcz), 0))
            dst = c.dram["u_st"] if dcz < 8 else c.dram["z_st"]
            self.mk[("cgw", dcz)] = me.dma_start(
                out=dst.rearrange("p (a b) -> p a b", b=S)[:, dcz % 8, :],
                in_=c.sb["cgout"][:])
        us = c.dram["u_st"].rearrange("p (a b) -> p a b", b=S)
        zs = c.dram["z_st"].rearrange("p (a b) -> p a b", b=S)
        for sg in range(NSEG if "E2" in MSTAGES else 0):
            if sg == 0:
                for dcz in range(16):
                    me.wait(me, self.mk.get(("cgw", dcz), 0))
            me.wait(self.W["vector"], self.mk.get(("useg_done", sg - 1), 0))
            self.mk[("ubf", sg)] = me.dma_start(
                out=c.sb["ubf"][:], in_=us[:, :, sg * SEGT:(sg + 1) * SEGT])
            self.mk[("zbf", sg)] = me.dma_start(
                out=c.sb["zbf"][:], in_=zs[:, :, sg * SEGT:(sg + 1) * SEGT])
            if "E2X" not in MSTAGES:
                continue
            me.wait(self.W["vector"], self.mk.get(("bc_ev", sg), 0))
            if sg > 0:
                me.wait(me, self.mk.get(("bbc", sg - 1, 15), 0))
                me.wait(me, self.mk.get(("cbc", sg - 1, 15), 0))
            self.mk[("bst", sg)] = me.dma_start(out=c.dram["bc_st"],
                                                in_=c.sb["Bsb"][0:16, :])
            self.mk[("cst", sg)] = me.dma_start(out=c.dram["cc_st"],
                                                in_=c.sb["Csb"][0:16, :])
            me.wait(me, self.mk[("bst", sg)])
            me.wait(me, self.mk[("cst", sg)])
            def cbuf(n):
                return c.sb["cbc" if n % 2 == 0 else "spr"][:]
            if "E2SCAN" in MSTAGES:
                self.mk[("bbc", sg, 0)] = me.dma_start(
                    out=c.sb["bbc"][:],
                    in_=c.dram["bc_st"][0:1, :].broadcast_to([128, SEGT]))
                self.mk[("cbc", sg, 0)] = me.dma_start(
                    out=cbuf(0), in_=c.dram["cc_st"][0:1, :]
                    .broadcast_to([128, SEGT]))
            for n in range(16 if "E2SCAN" in MSTAGES else 0):
                for dc in range(8):
                    k = n * 8 + dc
                    if dc == 0:
                        me.wait(me, self.mk[("bbc", sg, n)])
                        if n == 0:
                            me.wait(self.W["vector"], self.mk.get(("du", sg), 0))
                    p2 = self.spred(sg, n, dc, 2)
                    if p2:
                        me.wait(self.W["vector"],
                                self.mk.get(("scn",) + p2, 0))
                    me.tensor_tensor(c.sb["dBubf"][:, k % 2, :],
                                     c.sb["du"][:, dc, :], c.sb["bbc"][:],
                                     ALU.mult)
                    self.mk[("dBu", sg, n, dc)] = me.mark()
                    if dc == 3 and n < 15:
                        me.wait_self_compute()
                        self.mk[("cbc", sg, n + 1)] = me.dma_start(
                            out=cbuf(n + 1), in_=c.dram["cc_st"][n + 1:n + 2, :]
                            .broadcast_to([128, SEGT]))
                    if dc == 7 and n < 15:
                        me.wait_self_compute()
                        self.mk[("bbc", sg, n + 1)] = me.dma_start(
                            out=c.sb["bbc"][:],
                            in_=c.dram["bc_st"][n + 1:n + 2, :]
                            .broadcast_to([128, SEGT]))
        me.wait_self_all()

    def emit_sync(self, E=None):
        c, me = self.ctx, self.W["sync"]
        xs = c.dram["x_state"].rearrange("(t p) d -> t p d", p=128)
        for nm, dst in (("KxR", "kx"), ("KzR", "kz"), ("bxR", "bx"),
                        ("bzR", "bz"), ("AlogR", "alog"), ("DR", "dsb"),
                        ("bdtR", "bdt")):
            self.mk["p_" + dst] = me.dma_start(out=c.sb[dst][:], in_=c.dram[nm])
        for i in range(NT if "E0" in MSTAGES else 0):
            if i >= 4:
                me.wait(self.W["vector"], self.mk.get(f"e0xn{i-4}", 0))
            self.mk[f"ld{i}"] = me.dma_start(out=self.xt(i), in_=xs[i])
        for sg in range(NSEG if "E2" in MSTAGES else 0):
            for tl in range(SEGTILES):
                i = sg * SEGTILES + tl
                if i < 4:
                    me.wait(self.W["vector"], self.mk.get(f"e0xn{NT-1}", 0))
                else:
                    me.wait(self.W["vector"], self.mk.get(f"xn{i-4}", 0))
                self.mk[f"ld2_{i}"] = me.dma_start(out=self.xt(i), in_=xs[i])
            for tl in range(SEGTILES):
                i = sg * SEGTILES + tl
                me.wait(self.W["vector"], self.mk.get(f"xn{i}", 0))
                self.mk[f"st{i}"] = me.dma_start(out=xs[i], in_=self.xnew(i))
        me.wait_self_all()

    def emit_scalar(self, E=None):
        c, me = self.ctx, self.W["scalar"]
        me.wait(self.W["sync"], self.mk.get("p_alog", 0))
        me.activation(c.sb["asb"][:].rearrange("p a b -> p (a b)"),
                      c.sb["alog"][:].rearrange("p a b -> p (a b)"), AF.Exp)
        self.mk["aexp"] = me.mark()
        for dcz in range(16 if "E1" in MSTAGES else 0):
            me.wait(self.W["vector"], self.mk.get(("cv4", dcz), 0))
            if dcz > 0:
                me.wait(self.W["gpsimd"], self.mk.get(("cgw", dcz - 1), 0))
            b = (c.sb["bx"] if dcz < 8 else c.sb["bz"])[:, dcz % 8:dcz % 8 + 1]
            me.activation(c.sb["cgout"][:], c.sb["ct2"][:], AF.Gelu, bias=b)
            self.mk[("gel", dcz)] = me.mark()
        for sg in range(NSEG if ("E2" in MSTAGES and "E2X" in MSTAGES) else 0):
            for dc in range(8):
                me.wait(self.W["tensor"], self.mk.get(("dtp", sg, dc), 0))
                bcol = c.sb["bdt"][:, dc:dc + 1]
                me.activation(c.sb["spa"][:], c.ptf[:, 0:SEGT], AF.Abs, bias=bcol)
                me.activation(c.sb["spa"][:], c.sb["spa"][:], AF.Exp, scale=-1.0)
                me.activation(c.sb["spa"][:], c.sb["spa"][:], AF.Ln, bias=1.0)
                self.mk[("spl", sg, dc)] = me.mark()
                me.wait(self.W["vector"], self.mk.get(("spd", sg, dc), 0))
            for n in range(16 if "E2SCAN" in MSTAGES else 0):
                for dc in range(8):
                    k = n * 8 + dc
                    p2 = self.spred(sg, n, dc, 2)
                    if p2:
                        me.wait(self.W["vector"],
                                self.mk.get(("scn",) + p2, 0))
                    me.activation(c.sb["dApp"][:, k % 2, :],
                                  c.sb["delta"][:, dc, :], AF.Exp,
                                  scale=c.sb["asb"][:, dc, n:n + 1])
                    self.mk[("dA", sg, n, dc)] = me.mark()
            for tl in range(SEGTILES if "E2OUT" in MSTAGES else 0):
                i = sg * SEGTILES + tl
                me.wait(self.W["vector"], self.mk.get(f"xn{i}", 0))
                me.activation(c.sb["sqscr"][:, 0:DM], self.xnew(i), AF.Square,
                              accum_out=c.sb["stats_sq" + self.so][:, i:i + 1])
                self.mk[f"sq{i}"] = me.mark()
        if "E2" in MSTAGES:
            me.wait(self.W["vector"], self.mk.get("var", 0))
            me.activation(c.sb["tmpB"][:, 0:NT], c.sb["tmpA"][:, 0:NT], AF.Ln,
                          bias=EPS)
            me.activation(c.sb["stats_r" + self.so][:, 0:NT],
                          c.sb["tmpB"][:, 0:NT], AF.Exp, scale=-0.5)

    def emit_vector(self, E=None):
        c, me = self.ctx, self.W["vector"]
        me.memset(c.sb["carry"][:], 0.0)
        me.wait(self.W["scalar"], self.mk.get("aexp", 0))
        me.tensor_scalar(c.sb["asb"][:].rearrange("p a b -> p (a b)"),
                         c.sb["asb"][:].rearrange("p a b -> p (a b)"),
                         -1.0, None, ALU.mult)
        self.mk["aneg"] = me.mark()
        for sg in range(NSEG if "E0" in MSTAGES else 0):
            for tl in range(SEGTILES):
                i = sg * SEGTILES + tl
                me.wait(self.W["sync"], self.mk.get(f"ld{i}", 0))
                me.tensor_scalar(c.sb["m_xnbf"][:], self.xt(i),
                                 c.sb["stats_m" + self.si][:, i:i + 1],
                                 c.sb["stats_r" + self.si][:, i:i + 1],
                                 ALU.subtract, ALU.mult)
                self.mk[f"e0xn{i}"] = me.mark()
                me.wait(self.W["tensor"], self.mk.get(f"e0tr{i}", 0))
                if tl == 0 and sg > 0:
                    me.wait(self.W["tensor"], self.mk.get(("xzmm", sg - 1, 15), 0))
                me.tensor_copy(c.sb["xnTseg"][:, :, tl * 128:(tl + 1) * 128],
                               c.pgu_bf[:, 0:1024].rearrange(
                                   "p (a b) -> p a b", b=128))
                self.mk[f"e0t{i}"] = me.mark()
            for dc in range(16):
                me.wait(self.W["tensor"], self.mk.get(("xzmm", sg, dc), 0))
                if dc >= 2:
                    me.wait(self.W["gpsimd"], self.mk.get(("xrw", sg, dc - 2), 0))
                elif sg > 0:
                    me.wait(self.W["gpsimd"], self.mk.get(("xrw", sg - 1, 14 + dc), 0))
                me.tensor_copy(
                    c.sb["xrstage"][:, (dc % 2) * SEGT:(dc % 2) * SEGT + SEGT],
                    c.pgu[:, (dc % 2) * 512:(dc % 2) * 512 + 512])
                self.mk[("xrev", sg, dc)] = me.mark()
        me.wait_dma_upto(self.W["gpsimd"], self.mk.get("e0_dn", 0))
        me.memset(c.sb["cbuf"][:], 0.0)
        self.mk["cbz"] = me.mark()
        self.mk[("cvz", 0)] = self.mk["cbz"]
        for dcz in range(16 if "E1" in MSTAGES else 0):
            me.wait(self.W["gpsimd"], self.mk.get(("cvl", dcz), 0))
            kk = (c.sb["kx"] if dcz < 8 else c.sb["kz"])
            dc8 = dcz % 8
            me.tensor_single_scalar(c.sb["ct1"][:], c.sb["cbuf"][:, 0:S],
                                    kk[:, dc8, 0:1], ALU.mult)
            me.scalar_tensor_tensor(c.sb["ct2"][:], c.sb["cbuf"][:, 1:S + 1],
                                    kk[:, dc8, 1:2], c.sb["ct1"][:],
                                    ALU.mult, ALU.add)
            me.scalar_tensor_tensor(c.sb["ct1"][:], c.sb["cbuf"][:, 2:S + 2],
                                    kk[:, dc8, 2:3], c.sb["ct2"][:],
                                    ALU.mult, ALU.add)
            me.scalar_tensor_tensor(c.sb["ct2"][:], c.sb["cbuf"][:, 3:S + 3],
                                    kk[:, dc8, 3:4], c.sb["ct1"][:],
                                    ALU.mult, ALU.add)
            self.mk[("cv4", dcz)] = me.mark()
            me.wait(self.W["scalar"], self.mk.get(("gel", dcz), 0))
            self.mk[("cvz", dcz + 1)] = me.mark()
        for sg in range(NSEG if "E2" in MSTAGES else 0):
            if "E2X" not in MSTAGES:
                continue
            me.wait(self.W["tensor"], self.mk.get(("xdbl", sg), 0))
            me.tensor_copy(c.sb["dtlow"][0:64, :], c.pgu[0:64, 0:512])
            me.tensor_copy(c.sb["Bsb"][0:16, :], c.pgu[0:16, 512:1024])
            me.tensor_copy(c.sb["Csb"][0:16, :], c.pgu[0:16, 1536:2048])
            self.mk[("bc_ev", sg)] = me.mark()
            for dc in range(8):
                me.wait(self.W["scalar"], self.mk.get(("spl", sg, dc), 0))
                me.tensor_scalar(c.sb["spr"][:], c.ptf[:, 0:SEGT],
                                 c.sb["bdt"][:, dc:dc + 1], 0.0,
                                 ALU.add, ALU.max)
                me.tensor_tensor(c.sb["delta"][:, dc, :], c.sb["spa"][:],
                                 c.sb["spr"][:], ALU.add)
                self.mk[("spd", sg, dc)] = me.mark()
            me.wait(self.W["gpsimd"], self.mk.get(("ubf", sg), 0))
            for dc in range(8):
                me.tensor_tensor(c.sb["du"][:, dc, :], c.sb["delta"][:, dc, :],
                                 c.sb["ubf"][:, dc, :], ALU.mult)
            self.mk[("du", sg)] = me.mark()
            cr2 = c.sb["carry"][:].rearrange("p (a b) -> p a b", b=16)

            def yadd(n1, dc1):
                k1 = n1 * 8 + dc1
                if n1 == 0:
                    me.tensor_copy(c.sb["y"][:, dc1, :],
                                   c.sb["hcbbf"][:, k1 % 2, :])
                else:
                    me.tensor_tensor(c.sb["y"][:, dc1, :], c.sb["y"][:, dc1, :],
                                     c.sb["hcbbf"][:, k1 % 2, :], ALU.add)
                self.mk[("ya", sg, n1, dc1)] = me.mark()

            nloop = 16 if "E2SCAN" in MSTAGES else 0
            for n in range(nloop):
                me.wait(self.W["gpsimd"], self.mk.get(("cbc", sg, n), 0))
                for dc in range(8):
                    k = n * 8 + dc
                    me.wait(self.W["scalar"], self.mk.get(("dA", sg, n, dc), 0))
                    me.wait(self.W["gpsimd"],
                            self.mk.get(("dBu", sg, n, dc), 0))
                    me.op("tensor_tensor_scan", c.sb["hbpp"][:, k % 2, :],
                          c.sb["dApp"][:, k % 2, :], c.sb["dBubf"][:, k % 2, :],
                          c.sb["carry"][:, dc * 16 + n:dc * 16 + n + 1],
                          ALU.mult, ALU.add)
                    self.mk[("scn", sg, n, dc)] = me.mark()
                    if dc % 2 == 1:
                        me.tensor_copy(cr2[:, dc - 1:dc + 1, n:n + 1],
                                       c.sb["hbpp"][:, :, SEGT - 1:SEGT])
                    me.tensor_tensor(c.sb["hcbbf"][:, k % 2, :],
                                     c.sb["hbpp"][:, k % 2, :],
                                     c.sb["cbc" if n % 2 == 0 else
                                           "spr"][:], ALU.mult)
                    yadd(n, dc)
            for dc in range(8):
                me.scalar_tensor_tensor(c.sb["y"][:, dc, :], c.sb["ubf"][:, dc, :],
                                        c.sb["dsb"][:, dc:dc + 1],
                                        c.sb["y"][:, dc, :], ALU.mult, ALU.add)
                me.tensor_copy(c.sb["ybf"][:, dc, :], c.sb["y"][:, dc, :])
            me.wait(self.W["gpsimd"], self.mk.get(("zbf", sg), 0))
            self.mk[("ydone", sg)] = me.mark()
            for tl in range(SEGTILES if "E2OUT" in MSTAGES else 0):
                i = sg * SEGTILES + tl
                me.wait(self.W["sync"], self.mk.get(f"ld2_{i}", 0))
                me.wait(self.W["tensor"], self.mk.get(("omm", sg, tl), 0))
                if i >= 2:
                    me.wait(self.W["sync"], self.mk.get(f"st{i-2}", 0))
                    me.wait(self.W["scalar"], self.mk.get(f"sq{i-2}", 0))
                me.op("_custom_dve", AFFINE_THEN_ADD, out=self.xnew(i),
                      in0=c.ptf[:, 512:1536], in1=self.xt(i), s0=1.0, s1=0.0)
                self.mk[f"xn{i}"] = me.mark()
                me.reduce_sum(c.sb["msc"][:, 0:1], self.xnew(i), AX.X)
                me.tensor_scalar(c.sb["stats_m" + self.so][:, i:i + 1],
                                 c.sb["msc"][:, 0:1], 1.0 / DM, None, ALU.mult)
            self.mk[("useg_done", sg)] = me.mark()
        if "E2" in MSTAGES:
            me.wait(self.W["scalar"], self.mk.get(f"sq{NT-1}", 0))
            me.tensor_scalar(c.sb["tmpA"][:, 0:NT],
                             c.sb["stats_sq" + self.so][:, 0:NT],
                             1.0 / DM, None, ALU.mult)
            me.tensor_tensor(c.sb["tmpB"][:, 0:NT],
                             c.sb["stats_m" + self.so][:, 0:NT],
                             c.sb["stats_m" + self.so][:, 0:NT], ALU.mult)
            me.tensor_tensor(c.sb["tmpA"][:, 0:NT], c.sb["tmpA"][:, 0:NT],
                             c.sb["tmpB"][:, 0:NT], ALU.subtract)
            self.mk["var"] = me.mark()

    def emit_tensor(self, E=None):
        c, me = self.ctx, self.W["tensor"]
        me.wait(self.W["gpsimd"], self.mk.get("win", 0))
        for sg in range(NSEG if "E0" in MSTAGES else 0):
            for tl in range(SEGTILES):
                i = sg * SEGTILES + tl
                me.wait(self.W["vector"], self.mk.get(f"e0xn{i}", 0))
                if i > 0:
                    me.wait(self.W["vector"], self.mk.get(f"e0t{i-1}", 0))
                for cc in range(8):
                    me.transpose(c.pgu_bf[:, cc * 128:(cc + 1) * 128],
                                 c.sb["m_xnbf"][:, cc * 128:(cc + 1) * 128],
                                 c.sb["identbf"][:])
                self.mk[f"e0tr{i}"] = me.mark()
            me.wait(self.W["vector"], self.mk.get(f"e0t{sg * SEGTILES + SEGTILES - 1}", 0))
            for dc in range(16):
                if dc >= 2:
                    me.wait(self.W["vector"], self.mk.get(("xrev", sg, dc - 2), 0))
                elif sg > 0:
                    me.wait(self.W["vector"], self.mk.get(("xrev", sg - 1, 14 + dc), 0))
                for cc in range(8):
                    me.matmul(c.pgu[:, (dc % 2) * 512:(dc % 2) * 512 + 512],
                              c.sb["Win_b"][:, cc, dc * 128:(dc + 1) * 128],
                              c.sb["xnTseg"][:, cc, :],
                              start=(cc == 0), stop=(cc == 7))
                self.mk[("xzmm", sg, dc)] = me.mark()
        me.wait(self.W["gpsimd"], self.mk.get("wxp", 0))
        me.wait(self.W["gpsimd"], self.mk.get("wdt", 0))
        me.wait(self.W["gpsimd"], self.mk.get("wout", 0))
        for sg in range(NSEG if ("E2" in MSTAGES and "E2X" in MSTAGES) else 0):
            me.wait(self.W["gpsimd"], self.mk.get(("ubf", sg), 0))
            if sg > 0:
                me.wait(self.W["vector"], self.mk.get(("useg_done", sg - 1), 0))
            for (o0, w, bank) in ((0, 64, 0), (64, 16, 512), (80, 16, 1536)):
                for cc in range(8):
                    me.matmul(c.pgu[0:w, bank:bank + SEGT],
                              c.sb["Wxp_b"][:, cc, o0:o0 + w],
                              c.sb["ubf"][:, cc, :],
                              start=(cc == 0), stop=(cc == 7))
            self.mk[("xdbl", sg)] = me.mark()
            me.wait(self.W["vector"], self.mk.get(("bc_ev", sg), 0))
            for dc in range(8):
                if dc > 0:
                    me.wait(self.W["vector"], self.mk.get(("spd", sg, dc - 1), 0))
                    me.wait(self.W["scalar"], self.mk.get(("spl", sg, dc - 1), 0))
                me.matmul(c.ptf[:, 0:SEGT], c.sb["Wdt_b"][0:64, dc, :],
                          c.sb["dtlow"][0:64, :], start=True, stop=True)
                self.mk[("dtp", sg, dc)] = me.mark()
            me.wait(self.W["vector"], self.mk.get(("ydone", sg), 0))
            me.wait(self.W["scalar"], self.mk.get(("spl", sg, 7), 0))
            me.wait(self.W["vector"], self.mk.get(("spd", sg, 7), 0))
            for tl in range(SEGTILES if "E2OUT" in MSTAGES else 0):
                if not (sg == 0 and tl == 0):
                    pi = sg * SEGTILES + tl - 1
                    me.wait(self.W["vector"], self.mk.get(f"xn{pi}", 0))
                for dcz in range(16):
                    lhs = (c.sb["ybf"] if dcz < 8 else c.sb["zbf"])[
                        :, dcz % 8, tl * 128:(tl + 1) * 128]
                    for half in range(2):
                        me.matmul(c.ptf[:, 512 + half * 512:512 + half * 512 + 512],
                                  lhs, c.sb["Wout_b"][:, dcz,
                                                      half * 512:half * 512 + 512],
                                  start=(dcz == 0), stop=(dcz == 15))
                self.mk[("omm", sg, tl)] = me.mark()




def make_phases(ctx, names):
    out = []
    cur = "0"
    for pn in names:
        nxt = "1" if cur == "0" else "0"
        if pn == "S0":
            out.append(StatsPrepass(ctx))
            continue
        elif pn == "W0":
            out.append(FFNWeightPrep(ctx))
            continue
        elif pn == "A":
            out.append(FFNPhase(ctx, cur, nxt, "ffa"))
        elif pn == "B":
            out.append(AttnPhase(ctx, cur, nxt, "atb"))
        elif pn == "C":
            out.append(FFNPhase(ctx, cur, nxt, "ffc", load_stash=True))
        elif pn == "WM":
            out.append(MoEWeightPrep(ctx))
            continue
        elif pn == "D":
            out.append(MoEPhase(ctx, cur, nxt, "moed"))
        elif pn == "F":
            out.append(MoEPhase(ctx, cur, nxt, "moef"))
        elif pn == "E":
            out.append(MambaPhase(ctx, cur, nxt, "mam"))
        else:
            raise ValueError(pn)
        cur = nxt
    return out


def build_nc(phase_names):
    nc = bass.Bass("TRN2", target_bir_lowering=False)
    es = ExitStack()
    c = Ctx(nc, es)

    c.dram_t("x_in", [S, DM], F32, "ExternalInput")
    c.dram_t("x_state", [S, DM], F32, "ExternalOutput")
    for nm, shp in [("w_gate", [DFF, DM]), ("w_up", [DFF, DM]),
                    ("w_down", [DM, DFF]),
                    ("d_ident", [128, 128]), ("d_cst", [128, 8])]:
        c.dram_t(nm, shp, F32, "ExternalInput")
    for nm in ("WqT", "WkT", "WvT", "WfcT"):
        c.dram_t(nm, [DM, DM], F32, "ExternalInput")
    c.dram_t("keysR", [128, NEXP * 1024], F32, "ExternalInput")
    c.dram_t("valuesR", [128, NEXP * 1024], F32, "ExternalInput")
    c.dram_t("selT", [DM, 64], F32, "ExternalInput")
    c.dram_t("values_st", [128, NEXP * 1024], BF16, "Internal")
    c.dram_t("keys_st", [128, NEXP * 1024], BF16, "Internal")
    c.dram_t("WinR", [128, 16384], F32, "ExternalInput")
    c.dram_t("WoutR", [128, 16384], F32, "ExternalInput")
    c.dram_t("WxpR", [128, 768], F32, "ExternalInput")
    c.dram_t("WdtR", [64, 1024], F32, "ExternalInput")
    c.dram_t("KxR", [128, 32], F32, "ExternalInput")
    c.dram_t("KzR", [128, 32], F32, "ExternalInput")
    c.dram_t("bxR", [128, 8], F32, "ExternalInput")
    c.dram_t("bzR", [128, 8], F32, "ExternalInput")
    c.dram_t("AlogR", [128, 128], F32, "ExternalInput")
    c.dram_t("DR", [128, 8], F32, "ExternalInput")
    c.dram_t("bdtR", [128, 8], F32, "ExternalInput")
    c.dram_t("xraw_st", [128, 16 * S], F32, "Internal")
    c.dram_t("u_st", [128, 8 * S], BF16, "Internal")
    c.dram_t("z_st", [128, 8 * S], BF16, "Internal")
    c.dram_t("bc_st", [16, SEGT], F32, "Internal")
    c.dram_t("cc_st", [16, SEGT], F32, "Internal")
    c.dram_t("WgqT_st", [128, 8 * DFF], BF16, "Internal")
    c.dram_t("WuqT_st", [128, 8 * DFF], BF16, "Internal")
    c.dram_t("WdqT_st", [128, 22 * DM], BF16, "Internal")

    c.sbuf("cst", [128, 8], F32)
    c.sbuf("identbf", [128, 128], BF16)
    c.sbuf("onesbf", [128, 128], BF16)
    c.sbuf("xt", [128, 2 * DM], F32)
    c.sbuf("xnew", [128, 2 * DM], F32)
    c.sbuf("msc", [128, 4], F32)
    c.sbuf("sqscr", [128, DM], F32)
    c.sbuf("tmpA", [128, NT], F32)
    c.sbuf("tmpB", [128, NT], F32)
    c.sbuf("mk2", [128, 16], F32)
    for t in ("0", "1"):
        c.sbuf("stats_m" + t, [128, NT], F32)
        c.sbuf("stats_sq" + t, [128, NT], F32)
        c.sbuf("stats_r" + t, [128, NT], F32)
    arena = c.sbuf("arena", [128, 186 * 1024], mybir.dt.uint8)
    arv = arena[:]

    def carve(name, off, shape, dtype):
        n = int(np.prod(shape[1:]))
        ap = arv[:, off:off + n * mybir.dt.size(dtype)].bitcast(dtype)
        if len(shape) == 3:
            ap = ap.rearrange("p (a b) -> p a b", b=shape[2])
        elif len(shape) == 4:
            ap = ap.rearrange("p (a b cd) -> p a b cd", b=shape[2], cd=shape[3])
        c.sb[name] = ap
        return ap

    carve("Wg", 0, [128, 8 * DFF], BF16)
    carve("Wu", 45056, [128, 8 * DFF], BF16)
    carve("Wd", 90112, [128, 22 * DM], BF16)
    carve("wtile", 135168, [128, DFF], F32)
    carve("xqbf", 146432, [128, DM], BF16)
    carve("xqT", 148480, [128, DM], BF16)
    carve("hqbf", 150528, [128, DFF], BF16)
    carve("hqT", 156160, [128, DFF], BF16)
    carve("sgate", 161792, [128, 512], F32)
    carve("hpf", 163840, [128, 512], F32)
    carve("xc", 165888, [128, DM], F32)
    carve("q_a", 169984, [128, DM], F32)
    carve("q_xn", 174080, [128, DM], F32)
    carve("q_g2", 178176, [128, DM], F32)
    carve("q_sg", 182272, [128, DM], F32)
    carve("q_bm", 186368, [128, 64], F32)
    carve("q_scl", 186624, [128, 64], F32)
    carve("q_rec", 186880, [128, 64], F32)
    carve("q_sclF", 187136, [128, 64], F32)
    carve("Wat", 0, [128, 4 * 8 * 1024], BF16)
    carve("xnbf", 65536, [128, DM], BF16)
    carve("xnT", 67584, [128, DM], BF16)
    carve("qbf", 69632, [128, DM], BF16)
    carve("kbf", 71680, [128, DM], BF16)
    carve("vbf", 73728, [128, DM], BF16)
    carve("abf", 75776, [128, DM], BF16)
    carve("aT", 77824, [128, DM], BF16)
    carve("att_m4", 79872, [128, 4096], BF16)
    carve("att_sc", 88064, [128, 256], F32)
    carve("att_E", 89088, [128, 256], F32)
    carve("att_A", 90112, [128, 256], BF16)
    carve("att_mx", 90624, [128, 16], F32)
    carve("att_sm", 90688, [128, 16], F32)
    carve("att_rc", 90752, [128, 16], F32)
    carve("aof", 91136, [128, DM], F32)
    carve("m4s", 95232, [128, 2, 16384], BF16)
    carve("attE", 160768, [128, 256], F32)
    carve("vbf2", 161792, [128, 2, DM], BF16)
    carve("fcb", 165888, [128, 2, DM], F32)
    carve("asc2", 174080, [128, 2, 256], F32)

    carve("keysb", 0, [128, 8, 8192], BF16)
    carve("m_xt", 131072, [128, 2, DM], F32)
    carve("xnTs", 139264, [128, 8, 768], BF16)
    carve("outac", 151552, [128, 6, DM], BF16)
    carve("vbuf", 163840, [128, 4, 2048], BF16)
    carve("scb", 180224, [128, 2, 512], BF16)
    carve("scT", 182272, [128, 2, 512], BF16)
    carve("gates_bf", 184320, [128, 2, 384], BF16)
    carve("selsb", 185856, [128, 2, 64], F32)
    carve("selw", 186368, [128, 64], F32)
    carve("selm", 186624, [128, 64], F32)
    carve("selTb", 186880, [128, 8, 64], BF16)
    carve("m_xnbf", 187904, [128, DM], BF16)
    carve("wmA", 0, [128, 2, 8192], F32)
    carve("wmB", 65536, [128, 2, 8192], BF16)

    carve("Win_b", 0, [128, 8, 2048], BF16)
    carve("Wout_b", 32768, [128, 16, 1024], BF16)
    carve("Wxp_b", 65536, [128, 8, 96], BF16)
    carve("Wdt_b", 67072, [128, 8, 128], BF16)
    carve("kx", 69120, [128, 8, 4], F32)
    carve("kz", 69248, [128, 8, 4], F32)
    carve("bx", 69376, [128, 8], F32)
    carve("bz", 69440, [128, 8], F32)
    carve("alog", 69504, [128, 8, 16], F32)
    carve("asb", 70016, [128, 8, 16], F32)
    carve("dsb", 70528, [128, 8], F32)
    carve("bdt", 70560, [128, 8], F32)
    carve("carry", 70592, [128, 128], F32)
    carve("bbc", 71104, [128, SEGT], F32)
    carve("cbc", 73152, [128, SEGT], F32)
    carve("dtlow", 75200, [128, SEGT], BF16)
    carve("Bsb", 76224, [128, SEGT], F32)
    carve("Csb", 78272, [128, SEGT], F32)
    carve("spa", 80320, [128, SEGT], F32)
    carve("spr", 82368, [128, SEGT], F32)
    carve("dA", 84416, [128, SEGT], F32)
    carve("dBubf", 86464, [128, 2, SEGT], BF16)
    carve("hbpp", 88512, [128, 2, SEGT], F32)
    carve("hcbbf", 182720, [128, 2, SEGT], BF16)
    carve("dApp", 184768, [128, 2, SEGT], F32)
    carve("ubf", 92608, [128, 8, SEGT], BF16)
    carve("zbf", 100800, [128, 8, SEGT], BF16)
    carve("ybf", 108992, [128, 8, SEGT], BF16)
    carve("delta", 117184, [128, 8, SEGT], F32)
    carve("du", 133568, [128, 8, SEGT], F32)
    carve("y", 149952, [128, 8, SEGT], F32)
    carve("e_xt", 166336, [128, 4, DM], F32)
    carve("xnTseg", 92608, [128, 8, SEGT], BF16)
    carve("xrstage", 100800, [128, 2 * SEGT], F32)
    carve("ct1", 80320, [128, S], F32)
    carve("ct2", 96704, [128, S], F32)
    carve("cbuf", 113088, [128, 4112], F32)
    carve("cgout", 129536, [128, S], BF16)

    pt = c.psum("PT", [128, 2048], F32)
    c.pt_bf = pt[:].bitcast(BF16)
    c.ptf = pt[:]
    pgu = c.psum("PGU", [128, 2048], F32)
    c.pgu = pgu[:]
    c.pgu_bf = pgu[:].bitcast(BF16)

    cstv = [0.0, 1.0, 0.5, -0.5, -1.0, 0.125, EPS, 1.75]
    for k, v in enumerate(cstv):
        nc.const_aps.register(F32, v, c.sb["cst"][:, k:k + 1])

    c.EW = {}
    for n in ("sync", "scalar", "vector", "tensor", "gpsimd"):
        dsems = [c.sem(f"d_{n}_{s}") for s in range(NSLOT)] \
            if n in ("sync", "gpsimd") else []
        c.EW[n] = W(n, c.sem(f"c_{n}"), dsems,
                    serial=(n in ("vector", "scalar", "gpsimd")))
    phases = make_phases(c, phase_names)
    PB = c.sem("phase_barrier")
    cst_sem = c.sem("cst_load")
    id_sem = c.sem("ident_load")

    counts = {}
    for ph in phases:
        ph.emit_sync()
        ph.emit_scalar()
        ph.emit_vector()
        ph.emit_tensor()
        ph.emit_gpsimd()
        for n in ph.W:
            counts[(ph.tag, n)] = (ph.W[n].n, ph.W[n].dn)
    for w in c.EW.values():
        w.reset()
    marksnap = [dict(ph.mk) for ph in phases]

    def run_phases(eng, name, emit_name):
        wrap = c.EW[name].bind(eng)
        for k, ph in enumerate(phases):
            eng.wait_ge(PB, 5 * k)
            getattr(ph, emit_name)()
            assert (wrap.n, wrap.dn) == counts[(ph.tag, name)], (
                ph.tag, name, wrap.n, wrap.dn, counts[(ph.tag, name)])
            wrap.wait_self_all()
            eng.sem_inc(PB, 1)
        wrap.unbind()

    with nc.Block() as block:
        @block.sync
        def _(eng):
            eng.dma_start(out=c.sb["cst"][:], in_=c.dram["d_cst"]).then_inc(cst_sem, 16)
            eng.wait_ge(cst_sem, 16)
            eng.wait_ge(id_sem, 16)
            run_phases(eng, "sync", "emit_sync")

        @block.scalar
        def _(eng):
            eng.wait_ge(cst_sem, 16)
            eng.wait_ge(id_sem, 16)
            run_phases(eng, "scalar", "emit_scalar")

        @block.vector
        def _(eng):
            eng.wait_ge(cst_sem, 16)
            eng.wait_ge(id_sem, 16)
            run_phases(eng, "vector", "emit_vector")

        @block.tensor
        def _(eng):
            eng.wait_ge(cst_sem, 16)
            eng.wait_ge(id_sem, 16)
            run_phases(eng, "tensor", "emit_tensor")

        @block.gpsimd
        def _(eng):
            eng.dma_start(out=c.sb["identbf"][:], in_=c.dram["d_ident"]).then_inc(id_sem, 16)
            eng.wait_ge(cst_sem, 16)
            run_phases(eng, "gpsimd", "emit_gpsimd")

    for ph, snap in zip(phases, marksnap):
        if ph.mk != snap:
            diffs = [(k, snap.get(k), ph.mk.get(k))
                     for k in set(snap) | set(ph.mk)
                     if snap.get(k) != ph.mk.get(k)]
            raise AssertionError(f"mark drift in {ph.tag}: {diffs[:8]}")
    mybir.codegen_inst_isa_subclasses(nc)
    return nc, es


def prep_host_ffn(inputs):
    f32 = np.float32
    x = np.ascontiguousarray(inputs["L0_attn_x"], dtype=f32)
    shared = {
        "w_gate": np.ascontiguousarray(inputs["L0_ffn_w_gate"], f32),
        "w_up": np.ascontiguousarray(inputs["L0_ffn_w_up"], f32),
        "w_down": np.ascontiguousarray(inputs["L0_ffn_w_down"], f32),
        "WqT": np.ascontiguousarray(inputs["L0_attn_Wq"].T, f32),
        "WkT": np.ascontiguousarray(inputs["L0_attn_Wk"].T, f32),
        "WvT": np.ascontiguousarray(inputs["L0_attn_Wv"].T, f32),
        "WfcT": np.ascontiguousarray(inputs["L0_attn_Wfc"].T, f32),
        "keysR": np.ascontiguousarray(
            inputs["L1_ffn_keys"].reshape(NEXP, 8, 128, 128).transpose(2, 1, 0, 3)
            .reshape(128, NEXP * 1024), f32),
        "valuesR": np.ascontiguousarray(
            inputs["L1_ffn_values"].transpose(1, 0, 2).reshape(128, NEXP * 1024), f32),
        "selT": np.ascontiguousarray(inputs["L1_ffn_expert_sel"].T, f32),
        "WinR": np.ascontiguousarray(
            inputs["L1_attn_W_in"].reshape(8, 128, 2048).transpose(1, 0, 2)
            .reshape(128, 16384), f32),
        "WoutR": np.ascontiguousarray(
            inputs["L1_attn_W_out"].reshape(16, 128, 1024).transpose(1, 0, 2)
            .reshape(128, 16384), f32),
        "WxpR": np.ascontiguousarray(
            inputs["L1_attn_W_xproj"].reshape(8, 128, 96).transpose(1, 0, 2)
            .reshape(128, 768), f32),
        "WdtR": np.ascontiguousarray(inputs["L1_attn_W_dt"], f32),
        "KxR": np.ascontiguousarray(
            inputs["L1_attn_K_x"].reshape(8, 128, 4).transpose(1, 0, 2)
            .reshape(128, 32), f32),
        "KzR": np.ascontiguousarray(
            inputs["L1_attn_K_z"].reshape(8, 128, 4).transpose(1, 0, 2)
            .reshape(128, 32), f32),
        "bxR": np.ascontiguousarray(
            inputs["L1_attn_b_x"].reshape(8, 128).T, f32),
        "bzR": np.ascontiguousarray(
            inputs["L1_attn_b_z"].reshape(8, 128).T, f32),
        "AlogR": np.ascontiguousarray(
            inputs["L1_attn_A_log"].reshape(8, 128, 16).transpose(1, 0, 2)
            .reshape(128, 128), f32),
        "DR": np.ascontiguousarray(
            inputs["L1_attn_D"].reshape(8, 128).T, f32),
        "bdtR": np.ascontiguousarray(
            inputs["L1_attn_b_dt"].reshape(8, 128).T, f32),
        "d_ident": np.eye(128, dtype=f32),
        "d_cst": np.tile(np.array([0.0, 1.0, 0.5, -0.5, -1.0, 0.125, EPS, 1.75],
                                  f32), (128, 1)),
    }
    in_maps = []
    for cix in range(x.shape[0]):
        m = dict(shared)
        m["x_in"] = np.ascontiguousarray(x[cix])
        in_maps.append(m)
    return in_maps


def run_ffn_test(inputs, ncores=8):
    in_maps = prep_host_ffn(inputs)[:ncores]
    nc, es = build_nc(["S0", "W0", "A"])
    try:
        res = run_bass_kernel_spmd(nc, in_maps, core_ids=list(range(ncores)))
    finally:
        es.close()
    return np.stack([r["x_state"] for r in res.results])


def run_attn_test(inputs, x_prev, ncores=8):
    in_maps = prep_host_ffn(inputs)[:ncores]
    for cix in range(ncores):
        in_maps[cix]["x_in"] = np.ascontiguousarray(x_prev[cix].astype(np.float32))
    nc, es = build_nc(["S0", "B"])
    try:
        res = run_bass_kernel_spmd(nc, in_maps, core_ids=list(range(ncores)))
    finally:
        es.close()
    return np.stack([r["x_state"] for r in res.results])


FULL_CHAIN = ["S0", "W0", "A", "WM", "B", "C", "D", "E", "F"]


def kernel(**inputs):
    in_maps = prep_host_ffn(inputs)
    nc, es = build_nc(FULL_CHAIN)
    try:
        res = run_bass_kernel_spmd(nc, in_maps, core_ids=list(range(8)))
    finally:
        es.close()
    return np.stack([r["x_state"] for r in res.results]).astype(np.float32)
